# version 16
# baseline (speedup 1.0000x reference)
import numpy as np
from contextlib import ExitStack

import concourse.bass as bass
import concourse.mybir as mybir
from concourse.bass_utils import run_bass_kernel_spmd

F32 = mybir.dt.float32
BF16 = mybir.dt.bfloat16
FP8 = mybir.dt.float8e4
I8 = mybir.dt.int8
I16 = mybir.dt.int16
I32 = mybir.dt.int32
AF = mybir.ActivationFunctionType
ALU = mybir.AluOpType
AX = mybir.AxisListType

B, T, C = 8, 2048, 1024
NCORES = 8
TPC = T // NCORES
ROWS = B * TPC
NT = ROWS // 128
NC_ = C // 128
EPS_LN = 1e-5
EPS_ATTN = 1e-6
H, D = 16, 64
I0 = 4096
I1 = 3072
NDIM = 16
SCALE_SSM = float(np.sqrt(1.0 / NDIM))

N_DMA_SEMS = 6


class Buf:
    __slots__ = ("name", "writers", "readers")
    def __init__(self, name):
        self.name = name
        self.writers = {}
        self.readers = {}


class Builder:
    def __init__(self, nc, es):
        self.nc = nc
        self.es = es
        self.recorded = {"tensor": [], "vector": [], "scalar": [], "gpsimd": [], "sync": []}
        self.sems = {}
        self.cnt = {}
        for e in ("tensor", "vector", "scalar", "gpsimd"):
            self._new_sem("e_" + e)
        for q in ("sync", "scalar", "gpsimd"):
            for i in range(N_DMA_SEMS):
                self._new_sem(f"d_{q}{i}")
        self._new_sem("cc")
        self.dma_rr = {"sync": 0, "scalar": 0, "gpsimd": 0}
        self.waited = {e: {} for e in self.recorded}
        self.bufs = {}

    def _new_sem(self, name):
        self.sems[name] = self.es.enter_context(self.nc.semaphore(name))
        self.cnt[name] = 0

    def buf(self, name):
        if name not in self.bufs:
            self.bufs[name] = Buf(name)
        return self.bufs[name]

    def _wait(self, eng, sem_name, val):
        if val <= 0:
            return
        if eng == "tensor" and sem_name == "e_tensor":
            return
        if self.waited[eng].get(sem_name, 0) >= val:
            return
        self.waited[eng][sem_name] = val
        self.recorded[eng].append(("wait", sem_name, val))

    def op(self, eng, meth, reads=(), writes=(), dma=False, args=(), kwargs=None):
        kwargs = kwargs or {}
        rd = [self.buf(r) for r in reads]
        wr = [self.buf(w) for w in writes]
        for b in rd:
            for s, v in b.writers.items():
                self._wait(eng, s, v)
        for b in wr:
            for s, v in b.writers.items():
                self._wait(eng, s, v)
            for s, v in b.readers.items():
                self._wait(eng, s, v)
        if dma:
            q = eng if eng in ("sync", "scalar", "gpsimd") else None
            assert q is not None, f"engine {eng} cannot DMA"
            i = self.dma_rr[q]
            self.dma_rr[q] = (i + 1) % N_DMA_SEMS
            sem_name = f"d_{q}{i}"
            inc = 16
            self._wait(eng, sem_name, self.cnt[sem_name])
        else:
            sem_name = "e_" + eng
            inc = 1
        self.cnt[sem_name] += inc
        tok = (sem_name, self.cnt[sem_name])
        self.recorded[eng].append(("op", meth, args, kwargs, sem_name, inc))
        for b in wr:
            if b.writers.get(tok[0], 0) < tok[1]:
                b.writers[tok[0]] = tok[1]
        for b in rd:
            if b.readers.get(tok[0], 0) < tok[1]:
                b.readers[tok[0]] = tok[1]
        return tok

    def barrier(self):
        snap = {s: v for s, v in self.cnt.items() if v > 0}
        for eng in self.recorded:
            for s, v in snap.items():
                if eng == "tensor" and s == "e_tensor":
                    continue
                self._wait(eng, s, v)

    def finalize(self, block):
        sems = self.sems
        def run(eng_obj, eng_name):
            for item in self.recorded[eng_name]:
                if item[0] == "wait":
                    eng_obj.wait_ge(sems[item[1]], item[2])
                else:
                    _, meth, args, kwargs, sem_name, inc = item
                    f = getattr(eng_obj, meth)
                    try:
                        r = f(*args, **kwargs)
                    except Exception:
                        import sys
                        print(f"FAILED OP: {eng_name}.{meth} kwargs keys: "
                              f"{ {kk: getattr(vv, 'shape', vv) if not isinstance(vv, (int, float, str, bool, type(None))) else vv for kk, vv in kwargs.items()} }",
                              file=sys.stderr)
                        raise
                    r.then_inc(sems[sem_name], inc)
        @block.tensor
        def _(t): run(t, "tensor")
        @block.vector
        def _(v): run(v, "vector")
        @block.scalar
        def _(s): run(s, "scalar")
        @block.gpsimd
        def _(g): run(g, "gpsimd")
        @block.sync
        def _(s): run(s, "sync")




def build_program(n_sub=6, dbg=None):
    nc = bass.Bass("TRN2", target_bir_lowering=False, dynamic_dma_scratch_size=512)
    es = ExitStack()

    x_h = nc.dram_tensor("x", [B, TPC, C], F32, kind="ExternalInput").ap()
    y_h = nc.dram_tensor("y", [B, TPC, C], F32, kind="ExternalOutput").ap()
    w0g_h = nc.dram_tensor("w0g", [C, I0], F32, kind="ExternalInput").ap()
    w0u_h = nc.dram_tensor("w0u", [C, I0], F32, kind="ExternalInput").ap()
    w0d_h = nc.dram_tensor("w0d", [I0, C], F32, kind="ExternalInput").ap()
    wqkv_h = nc.dram_tensor("wqkv", [C, 3 * C], F32, kind="ExternalInput").ap()
    wproj_h = nc.dram_tensor("wproj", [C, C], F32, kind="ExternalInput").ap()
    bproj_h = nc.dram_tensor("bproj", [1, C], F32, kind="ExternalInput").ap()
    w1g_h = nc.dram_tensor("w1g", [C, I1], F32, kind="ExternalInput").ap()
    w1u_h = nc.dram_tensor("w1u", [C, I1], F32, kind="ExternalInput").ap()
    w1d_h = nc.dram_tensor("w1d", [I1, C], F32, kind="ExternalInput").ap()
    ldt_h = nc.dram_tensor("ldt", [C, 1], F32, kind="ExternalInput").ap()
    lar_h = nc.dram_tensor("lar", [C, NDIM], F32, kind="ExternalInput").ap()
    aim_h = nc.dram_tensor("aim", [C, NDIM], F32, kind="ExternalInput").ap()
    bc_h = nc.dram_tensor("bc", [2, C, NDIM], F32, kind="ExternalInput").ap()
    cc_h = nc.dram_tensor("ccp", [2, C, NDIM], F32, kind="ExternalInput").ap()
    dd_h = nc.dram_tensor("dd", [1, C], F32, kind="ExternalInput").ap()

    s_w0g = nc.dram_tensor("s_w0g", [NC_, 128, I0], FP8).ap()
    s_w0u = nc.dram_tensor("s_w0u", [NC_, 128, I0], FP8).ap()
    s_w0d = nc.dram_tensor("s_w0d", [I0 // 128, 128, C], FP8).ap()
    s_w1g = nc.dram_tensor("s_w1g", [I1 // 128, NC_, 128, 128], BF16).ap()
    s_w1u = nc.dram_tensor("s_w1u", [I1 // 128, NC_, 128, 128], BF16).ap()
    s_w1d = nc.dram_tensor("s_w1d", [I1 // 128, 128, C], BF16).ap()
    s_qkv = nc.dram_tensor("s_qkv", [NC_, 128, 3 * C], BF16).ap()
    s_proj = nc.dram_tensor("s_proj", [NC_, 128, C], BF16).ap()
    s_up = nc.dram_tensor("s_up", [I1 // 128, 128, ROWS], BF16).ap()
    kv_in = nc.dram_tensor("kv_in", [128, 65 * 64], F32).ap()
    kv_out = nc.dram_tensor("kv_out", [128, 65 * 64], F32).ap()
    NMX = 12
    mxi = [nc.dram_tensor(f"mx_in{j}", [4, 1], F32).ap() for j in range(NMX)]
    mxo = [nc.dram_tensor(f"mx_out{j}", [4, 1], F32).ap() for j in range(NMX)]

    sbp = lambda n, s, d=F32: es.enter_context(nc.sbuf_tensor(n, s, d))
    res = sbp("res", [128, NT, C])
    cm = es.enter_context(nc.sbuf_tensor("cm", [128, NC_, ROWS], BF16))
    xlnf = sbp("xlnf", [128, C])
    ident = es.enter_context(nc.sbuf_tensor("ident", [128, 128], BF16))
    ident_f = sbp("identf", [128, 128])
    ones1 = sbp("ones1", [1, 128])
    cols = sbp("cols", [128, NT, 12])
    gcols = sbp("gcols", [128, 16])
    tiny = sbp("tiny", [128, 16])
    bnst = sbp("bnst", [128, 12])
    cst = sbp("cst", [128, 1])
    psA = es.enter_context(nc.psum_tensor("psA", [128, 512], F32))
    psB = es.enter_context(nc.psum_tensor("psB", [128, 512], F32))
    psC = es.enter_context(nc.psum_tensor("psC", [128, 512], F32))
    psD = es.enter_context(nc.psum_tensor("psD", [128, 512], F32))
    psT0 = es.enter_context(nc.psum_tensor("psT0", [128, 1024], BF16))
    psT1 = es.enter_context(nc.psum_tensor("psT1", [128, 1024], BF16))
    psE = es.enter_context(nc.psum_tensor("psE", [128, 512], F32))
    psF = es.enter_context(nc.psum_tensor("psF", [128, 512], F32))

    k = Builder(nc, es)
    OP = k.op

    def scalar_bcast(name_src, src_ap, dst_name, dst_ap):
        OP("tensor", "matmul", reads=["ones1", name_src], writes=["psF"],
           kwargs=dict(out=psF[:, 0:1], lhsT=ones1[:], rhs=src_ap, start=True, stop=True))
        OP("vector", "tensor_copy", reads=["psF"], writes=[dst_name], kwargs=dict(out=dst_ap, in_=psF[:, 0:1]))

    def col_allreduce(vals, j, out_cols):
        assert len(vals) <= 4 and len(out_cols) == len(vals)
        for i, (nm, ap) in enumerate(vals):
            OP("sync", "dma_start", reads=[nm], writes=[f"mxi{j}"], dma=True,
               kwargs=dict(out=mxi[j][i:i + 1, 0:1], in_=ap))
        OP("gpsimd", "collective_compute", reads=[f"mxi{j}"], writes=[f"mxo{j}"],
           kwargs=dict(kind="AllReduce", op=ALU.max, replica_groups=[list(range(NCORES))],
                       ins=[mxi[j][:]], outs=[mxo[j][:]]))
        for i, (nm, ap) in enumerate(out_cols):
            OP("sync", "dma_start", reads=[f"mxo{j}"], writes=[nm], dma=True,
               kwargs=dict(out=ap, in_=mxo[j][i:i + 1, 0:1].broadcast_to((128, 1))))

    def part_max(src_name, src_ap, nfree, dst_name, dst_ap, ctag):
        OP("vector", "tensor_reduce", reads=[src_name], writes=[f"tc{ctag}"],
           kwargs=dict(out=tiny[:, ctag:ctag + 1], in_=src_ap, axis=AX.X, op=ALU.max))
        OP("tensor", "transpose", reads=[f"tc{ctag}", "identf"], writes=["psF"],
           kwargs=dict(out=psF[0:1, 0:128], in_=tiny[:, ctag:ctag + 1], identity=ident_f[:]))
        OP("vector", "tensor_reduce", reads=["psF"], writes=[dst_name],
           kwargs=dict(out=dst_ap, in_=psF[0:1, 0:128], axis=AX.X, op=ALU.max))

    def lnorm(t, out_f32, out_bf_ap=None, out_bf_name=None, tag=""):
        OP("vector", "bn_stats", reads=[f"res{t}"], writes=["bnsA"],
           kwargs=dict(out=bnst[:, 0:6], in_=res[:, t, 0:512]))
        OP("vector", "bn_stats", reads=[f"res{t}"], writes=["bnsB"],
           kwargs=dict(out=bnst[:, 6:12], in_=res[:, t, 512:1024]))
        OP("vector", "bn_aggr", reads=["bnsA", "bnsB"], writes=[f"mv{tag}{t}"],
           kwargs=dict(out=cols[:, t, 0:2], in_=bnst[:]))
        OP("scalar", "activation", reads=[f"mv{tag}{t}"], writes=[f"sd{tag}{t}"],
           kwargs=dict(out=cols[:, t, 2:3], in_=cols[:, t, 1:2], func=AF.Sqrt, bias=EPS_LN))
        OP("vector", "reciprocal", reads=[f"sd{tag}{t}"], writes=[f"rstd{tag}{t}"],
           kwargs=dict(out=cols[:, t, 2:3], in_=cols[:, t, 2:3]))
        if out_f32:
            OP("vector", "tensor_scalar", reads=[f"res{t}", f"mv{tag}{t}", f"rstd{tag}{t}"],
               writes=["xlnf"],
               kwargs=dict(out=xlnf[:], in0=res[:, t, :], scalar1=cols[:, t, 0:1],
                           scalar2=cols[:, t, 2:3], op0=ALU.subtract, op1=ALU.mult))
        if out_bf_ap is not None:
            OP("vector", "tensor_scalar", reads=[f"res{t}", f"mv{tag}{t}", f"rstd{tag}{t}"],
               writes=[out_bf_name],
               kwargs=dict(out=out_bf_ap, in0=res[:, t, :], scalar1=cols[:, t, 0:1],
                           scalar2=cols[:, t, 2:3], op0=ALU.subtract, op1=ALU.mult))

    def s1_tile(t, V, bufs):
        nf32b, ntokb, nbtb = bufs
        sl = t % 2
        lnorm(t, True, tag=V)
        OP("vector", "tensor_reduce", reads=["xlnf"], writes=[f"mx{V}{t}"],
           kwargs=dict(out=cols[:, t, 3:4], in_=xlnf[:], axis=AX.X, op=ALU.max,
                       apply_absolute_value=True))
        OP("vector", "tensor_scalar", reads=[f"mx{V}{t}"], writes=[f"q{V}{t}"],
           kwargs=dict(out=cols[:, t, 4:5], in0=cols[:, t, 3:4], scalar1=1.0 / 127.0,
                       scalar2=1e-5 / 127.0, op0=ALU.mult, op1=ALU.max))
        OP("vector", "reciprocal", reads=[f"q{V}{t}"], writes=[f"rs{V}{t}"],
           kwargs=dict(out=cols[:, t, 5:6], in_=cols[:, t, 4:5]))
        OP("vector", "tensor_tensor", reads=[f"q{V}{t}", "mcol_g"], writes=[f"scg{V}{t}"],
           kwargs=dict(out=cols[:, t, 6:7], in0=cols[:, t, 4:5], in1=gcols[:, 0:1], op=ALU.mult))
        OP("vector", "tensor_tensor", reads=[f"q{V}{t}", "mcol_u"], writes=[f"scu{V}{t}"],
           kwargs=dict(out=cols[:, t, 7:8], in0=cols[:, t, 4:5], in1=gcols[:, 1:2], op=ALU.mult))
        OP("scalar", "activation", reads=["xlnf", f"rs{V}{t}"], writes=["s1ntok"],
           kwargs=dict(out=ntokb[:], in_=xlnf[:], func=AF.Copy, scale=cols[:, t, 5:6], bias=0.0))
        OP("gpsimd", "tensor_copy", reads=["s1ntok"], writes=[f"s1nbt{sl}"],
           kwargs=dict(out=nbtb[:, sl, :], in_=ntokb[:]))
        transpose_tile_to_cm(f"s1nbt{sl}", nbtb[:, sl, :], t)

    def transpose_tile_to_cm(src_name, src_ap, t):
        pst, pst_name = (psT0, "psT0") if t % 2 == 0 else (psT1, "psT1")
        for c in range(NC_):
            OP("tensor", "transpose", reads=[src_name, "ident"], writes=[pst_name],
               kwargs=dict(out=pst[:, c * 128:(c + 1) * 128], in_=src_ap[:, c * 128:(c + 1) * 128],
                           identity=ident[:]))
        OP("vector", "tensor_copy", reads=[pst_name],
           writes=[f"cm{c}t{t}" for c in range(NC_)],
           kwargs=dict(out=cm[:, :, t * 128:(t + 1) * 128], in_=pst[:].rearrange("p (a b) -> p a b", b=128)))

    with nc.sbuf_tensor("itmp", [128, 128], I32) as itmp:
        OP("gpsimd", "iota", writes=["itmp"], kwargs=dict(out=itmp[:], pattern=[[1, 128]], base=0, channel_multiplier=-1))
        OP("gpsimd", "tensor_scalar", reads=["itmp"], writes=["ident"],
           kwargs=dict(out=ident[:], in0=itmp[:], scalar1=0, scalar2=None, op0=ALU.is_equal))
        OP("gpsimd", "tensor_copy", reads=["ident"], writes=["identf"], kwargs=dict(out=ident_f[:], in_=ident[:]))
        OP("vector", "memset", writes=["ones1"], kwargs=dict(ap=ones1[:], constant=1.0))
        OP("vector", "memset", writes=["cst"], kwargs=dict(ap=cst[:], constant=EPS_LN))
        nc.const_aps.register(F32, EPS_LN, cst[:, 0:1])
        for t in range(NT):
            b, half = t // 2, t % 2
            OP("sync" if t % 2 == 0 else "scalar", "dma_start", writes=[f"res{t}"], dma=True,
               kwargs=dict(out=res[:, t, :], in_=x_h[b, half * 128:(half + 1) * 128, :]))
    k.barrier()

    pes = ExitStack()
    wld = pes.enter_context(nc.sbuf_tensor("wld", [128, 3, 2048], F32))
    wq8 = pes.enter_context(nc.sbuf_tensor("wq8", [128, 3, 2048], I8))
    wf8 = pes.enter_context(nc.sbuf_tensor("wf8", [128, 3, 2048], FP8))
    wbf = pes.enter_context(nc.sbuf_tensor("wbf", [128, 3, 2048], BF16))
    wsum = pes.enter_context(nc.sbuf_tensor("wsum", [128, 80], F32))
    wbt = pes.enter_context(nc.sbuf_tensor("wbt", [128, 3, 2048], F32))
    wsc = pes.enter_context(nc.sbuf_tensor("wsc", [128, 3, 64], F32))
    wrc = pes.enter_context(nc.sbuf_tensor("wrc", [128, 3, 64], F32))
    wi8 = pes.enter_context(nc.sbuf_tensor("wi8", [128, 3, 2048], I8))
    wqb = pes.enter_context(nc.sbuf_tensor("wqb", [128, 3, 2048], BF16))
    wob = pes.enter_context(nc.sbuf_tensor("wob", [128, 3, 2048], BF16))

    def slabs(n):
        out, off = [], 0
        while off < n:
            w = min(2048, n - off)
            out.append((off, w))
            off += w
        return out

    WSUM_OFF = {0: 0, 1: 16, 2: 32}

    def prep_w0(src_h, dst_s, n_i, mcol_name, mcol_ap, swci):
        nch = src_h.shape[0] // 128
        sl_list = slabs(n_i)
        wsoff = WSUM_OFF[swci]
        nslab = 0
        for c in range(nch):
            for (off, w) in sl_list:
                s = nslab; nslab += 1
                sl = s % 3
                OP("sync" if s % 2 == 0 else "scalar", "dma_start", writes=[f"wld{sl}"], dma=True,
                   kwargs=dict(out=wld[:, sl, 0:w], in_=src_h[c * 128:(c + 1) * 128, off:off + w]))
                OP("vector", "tensor_reduce", reads=[f"wld{sl}"], writes=[f"wsum{wsoff + s}"],
                   kwargs=dict(out=wsum[:, wsoff + s:wsoff + s + 1], in_=wld[:, sl, 0:w], axis=AX.X,
                               op=ALU.add, apply_absolute_value=True))
        tot = nslab
        c0 = 4 * swci
        OP("vector", "tensor_reduce", reads=[f"wsum{wsoff + s}" for s in range(tot)], writes=[f"tinyA{swci}"],
           kwargs=dict(out=tiny[:, c0:c0 + 1], in_=wsum[:, wsoff:wsoff + tot], axis=AX.X, op=ALU.add))
        OP("tensor", "transpose", reads=[f"tinyA{swci}", "identf"], writes=["psF"],
           kwargs=dict(out=psF[0:1, 0:128], in_=tiny[:, c0:c0 + 1], identity=ident_f[:]))
        OP("vector", "tensor_reduce", reads=["psF"], writes=[f"tinyB{swci}"],
           kwargs=dict(out=tiny[0:1, c0 + 1:c0 + 2], in_=psF[0:1, 0:128], axis=AX.X, op=ALU.add))
        OP("vector", "tensor_scalar", reads=[f"tinyB{swci}"], writes=[f"tinyC{swci}"],
           kwargs=dict(out=tiny[0:1, c0 + 2:c0 + 3], in0=tiny[0:1, c0 + 1:c0 + 2],
                       scalar1=1.0 / (src_h.shape[0] * n_i), scalar2=1e-5, op0=ALU.mult, op1=ALU.max))
        OP("vector", "reciprocal", reads=[f"tinyC{swci}"], writes=[f"tinyD{swci}"],
           kwargs=dict(out=tiny[0:1, c0 + 3:c0 + 4], in_=tiny[0:1, c0 + 2:c0 + 3]))
        scalar_bcast(f"tinyD{swci}", tiny[0:1, c0 + 3:c0 + 4], "swcol" + mcol_name, gcols[:, 8 + swci:9 + swci])
        scalar_bcast(f"tinyC{swci}", tiny[0:1, c0 + 2:c0 + 3], mcol_name, mcol_ap)
        nslab = 0
        for c in range(nch):
            for (off, w) in sl_list:
                s = nslab % 3; nslab += 1
                OP("sync" if s % 2 == 0 else "scalar", "dma_start", writes=[f"wld{s}"], dma=True,
                   kwargs=dict(out=wld[:, s, 0:w], in_=src_h[c * 128:(c + 1) * 128, off:off + w]))
                OP("scalar", "activation", reads=[f"wld{s}", "swcol" + mcol_name], writes=[f"wq8{s}"],
                   kwargs=dict(out=wq8[:, s, 0:w], in_=wld[:, s, 0:w], func=AF.Copy,
                               scale=gcols[:, 8 + swci:9 + swci], bias=0.0))
                OP("gpsimd", "tensor_scalar", reads=[f"wq8{s}"], writes=[f"wf8{s}"],
                   kwargs=dict(out=wf8[:, s, 0:w], in0=wq8[:, s, 0:w], scalar1=1.0, scalar2=-1.0,
                               op0=ALU.min, op1=ALU.max))
                OP("scalar", "dma_start", reads=[f"wf8{s}"], writes=[f"dst{dst_s.tensor.name}"], dma=True,
                   kwargs=dict(out=dst_s[c, :, off:off + w], in_=wf8[:, s, 0:w]))

    def prep_cast(src_h, dst_s, n_o):
        nch = src_h.shape[0] // 128
        i = 0
        for c in range(nch):
            for (off, w) in slabs(n_o):
                s = i % 3; i += 1
                OP("sync" if s % 2 == 0 else "scalar", "dma_start", writes=[f"wld{s}"], dma=True,
                   kwargs=dict(out=wld[:, s, 0:w], in_=src_h[c * 128:(c + 1) * 128, off:off + w]))
                OP("gpsimd", "tensor_copy", reads=[f"wld{s}"], writes=[f"wbf{s}"],
                   kwargs=dict(out=wbf[:, s, 0:w], in_=wld[:, s, 0:w]))
                OP("scalar", "dma_start", reads=[f"wbf{s}"], writes=[f"dst{dst_s.tensor.name}"], dma=True,
                   kwargs=dict(out=dst_s[c, :, off:off + w], in_=wbf[:, s, 0:w]))

    def lpbq_slab(sl, w, store_fn):
        OP("vector", "transpose", reads=[f"wld{sl}"], writes=[f"wbt{sl}"],
           kwargs=dict(out=wbt[:, sl, 0:w], in_=wld[:, sl, 0:w]))
        OP("vector", "tensor_reduce", reads=[f"wbt{sl}"], writes=[f"wsc{sl}"],
           kwargs=dict(out=wsc[:, sl, 0:w // 32], in_=wbt[:, sl, 0:w].rearrange("p (a b) -> p a b", b=32),
                       axis=AX.X, op=ALU.max, apply_absolute_value=True))
        OP("vector", "tensor_scalar", reads=[f"wsc{sl}"], writes=[f"wsc{sl}"],
           kwargs=dict(out=wsc[:, sl, 0:w // 32], in0=wsc[:, sl, 0:w // 32],
                       scalar1=1.0 / 7.0, scalar2=1e-12, op0=ALU.mult, op1=ALU.max))
        OP("vector", "reciprocal", reads=[f"wsc{sl}"], writes=[f"wrc{sl}"],
           kwargs=dict(out=wrc[:, sl, 0:w // 32], in_=wsc[:, sl, 0:w // 32]))
        OP("vector", "tensor_tensor", reads=[f"wbt{sl}", f"wrc{sl}"], writes=[f"wbt{sl}"],
           kwargs=dict(out=wbt[:, sl, 0:w].rearrange("p (a b) -> p a b", b=32),
                       in0=wbt[:, sl, 0:w].rearrange("p (a b) -> p a b", b=32),
                       in1=wrc[:, sl, 0:w // 32, None].broadcast_to((128, w // 32, 32)),
                       op=ALU.mult))
        OP("vector", "tensor_copy", reads=[f"wbt{sl}"], writes=[f"wi8{sl}"],
           kwargs=dict(out=wi8[:, sl, 0:w], in_=wbt[:, sl, 0:w]))
        OP("vector", "tensor_tensor", reads=[f"wi8{sl}", f"wsc{sl}"], writes=[f"wqb{sl}"],
           kwargs=dict(out=wqb[:, sl, 0:w].rearrange("p (a b) -> p a b", b=32),
                       in0=wi8[:, sl, 0:w].rearrange("p (a b) -> p a b", b=32),
                       in1=wsc[:, sl, 0:w // 32, None].broadcast_to((128, w // 32, 32)),
                       op=ALU.mult))
        OP("vector", "transpose", reads=[f"wqb{sl}"], writes=[f"wob{sl}"],
           kwargs=dict(out=wob[:, sl, 0:w], in_=wqb[:, sl, 0:w]))
        store_fn(sl, w)

    def prep_w1ud(src_h, dst_s):
        i = 0
        for c in range(NC_):
            for (off, w) in slabs(I1):
                sl = i % 3; i += 1
                OP("sync", "dma_start", writes=[f"wld{sl}"], dma=True,
                   kwargs=dict(out=wld[:, sl, 0:w], in_=src_h[c * 128:(c + 1) * 128, off:off + w]))
                def store(sl, w, c=c, off=off):
                    for j in range(w // 128):
                        of = (off + j * 128) // 128
                        OP("scalar", "dma_start", reads=[f"wob{sl}"], writes=[f"dst{dst_s.tensor.name}"], dma=True,
                           kwargs=dict(out=dst_s[of, c, :, :], in_=wob[:, sl, j * 128:(j + 1) * 128]))
                lpbq_slab(sl, w, store)

    def prep_w1d(src_h, dst_s):
        for c in range(I1 // 128):
            sl = c % 3
            OP("sync", "dma_start", writes=[f"wld{sl}"], dma=True,
               kwargs=dict(out=wld[:, sl, 0:1024], in_=src_h[c * 128:(c + 1) * 128, :]))
            def store(sl, w, c=c):
                OP("scalar", "dma_start", reads=[f"wob{sl}"], writes=[f"dst{dst_s.tensor.name}"], dma=True,
                   kwargs=dict(out=dst_s[c, :, :], in_=wob[:, sl, 0:w]))
            lpbq_slab(sl, 1024, store)

    nf32p = pes.enter_context(nc.sbuf_tensor("nf32p", [128, C], F32))
    ntokp = pes.enter_context(nc.sbuf_tensor("ntokp", [128, C], I8))
    nbtp = pes.enter_context(nc.sbuf_tensor("nbtp", [128, 2, C], BF16))
    if dbg != "noprep":
        prep_w0(w0g_h, s_w0g, I0, "mcol_g", gcols[:, 0:1], 0)
        prep_w0(w0u_h, s_w0u, I0, "mcol_u", gcols[:, 1:2], 1)
    else:
        OP("vector", "memset", writes=["mcol_g"], kwargs=dict(ap=gcols[:, 0:1], constant=1.0))
        OP("vector", "memset", writes=["mcol_u"], kwargs=dict(ap=gcols[:, 1:2], constant=1.0))
    for t in range(NT):
        s1_tile(t, "f0", (nf32p, ntokp, nbtp))
    if dbg != "noprep":
        prep_w0(w0d_h, s_w0d, C, "mcol_d", gcols[:, 2:3], 2)
    else:
        OP("vector", "memset", writes=["mcol_d"], kwargs=dict(ap=gcols[:, 2:3], constant=1.0))
    if n_sub >= 2:
        prep_cast(wqkv_h, s_qkv, 3 * C)
        prep_cast(wproj_h, s_proj, C)
    if n_sub >= 4:
        prep_w1ud(w1g_h, s_w1g)
        prep_w1ud(w1u_h, s_w1u)
        prep_w1d(w1d_h, s_w1d)
    OP("vector", "tensor_scalar", reads=["mcol_d"], writes=["gc2h"],
       kwargs=dict(out=gcols[:, 5:6], in0=gcols[:, 2:3], scalar1=0.5, scalar2=None, op0=ALU.mult))
    k.barrier()
    pes.close()

    def l0ffn(inv):
        V = f"f{inv}"
        fes = ExitStack()
        w0sb = fes.enter_context(nc.sbuf_tensor(f"w0sb{inv}", [128, 64, 1024], FP8))
        n2i8 = fes.enter_context(nc.sbuf_tensor(f"n2i8{inv}", [128, 4, I0], I8))
        inner = fes.enter_context(nc.sbuf_tensor(f"innr{inv}", [128, I0], F32))
        sg_t = fes.enter_context(nc.sbuf_tensor(f"sg_t{inv}", [128, 512], F32))
        ntok = fes.enter_context(nc.sbuf_tensor(f"ntk{inv}", [128, C], I8))
        nf32 = fes.enter_context(nc.sbuf_tensor(f"nf32{inv}", [128, C], F32))
        nbt = fes.enter_context(nc.sbuf_tensor(f"nbt{inv}", [128, 2, C], BF16))
        mx8 = fes.enter_context(nc.sbuf_tensor(f"mx8{inv}", [128, 16], F32))
        n2bf = fes.enter_context(nc.sbuf_tensor(f"n2bf{inv}", [128, I0 // 4], BF16))
        n2cm = fes.enter_context(nc.sbuf_tensor(f"n2cm{inv}", [128, I0 // 128, 128], BF16))

        def load_gu():
            for c in range(NC_):
                OP("sync", "dma_start", writes=[f"w0p{4 * c + j}" for j in range(4)], dma=True,
                   kwargs=dict(out=w0sb[:, 4 * c:4 * c + 4, :], in_=s_w0g[c, :, :].rearrange("p (a b) -> p a b", b=1024)))
                OP("scalar", "dma_start", writes=[f"w0p{32 + 4 * c + j}" for j in range(4)], dma=True,
                   kwargs=dict(out=w0sb[:, 32 + 4 * c:32 + 4 * c + 4, :], in_=s_w0u[c, :, :].rearrange("p (a b) -> p a b", b=1024)))

        def load_d():
            for ic in range(I0 // 128):
                q = "sync" if ic % 2 == 0 else "scalar"
                OP(q, "dma_start", writes=[f"w0p{ic}"], dma=True,
                   kwargs=dict(out=w0sb[:, ic, :], in_=s_w0d[ic, :, :]))

        if inv == 1:
            for t in range(NT):
                s1_tile(t, V, (nf32, ntok, nbt))
        for pack in range(4):
            tiles = range(pack * 4, pack * 4 + 4)
            load_gu()
            for t in tiles:
                tsl = t % 4
                for og in range(8):
                    pg, pg_n, pu, pu_n = [(psA, "psA", psB, "psB"), (psC, "psC", psD, "psD"),
                                          (psE, "psE", psF, "psF")][og % 3]
                    gplane = lambda c: 4 * c + og // 2
                    uplane = lambda c: 32 + 4 * c + og // 2
                    gsl = (og % 2) * 512
                    for c in range(NC_):
                        OP("tensor", "matmul", reads=[f"cm{c}t{t}", f"w0p{gplane(c)}"],
                           writes=[pg_n] if c in (0, NC_ - 1) else [],
                           kwargs=dict(out=pg[:], lhsT=cm[:, c, t * 128:(t + 1) * 128],
                                       rhs=w0sb[:, gplane(c), gsl:gsl + 512],
                                       start=(c == 0), stop=(c == NC_ - 1)))
                        OP("tensor", "matmul", reads=[f"cm{c}t{t}", f"w0p{uplane(c)}"],
                           writes=[pu_n] if c in (0, NC_ - 1) else [],
                           kwargs=dict(out=pu[:], lhsT=cm[:, c, t * 128:(t + 1) * 128],
                                       rhs=w0sb[:, uplane(c), gsl:gsl + 512],
                                       start=(c == 0), stop=(c == NC_ - 1)))
                    OP("scalar", "activation", reads=[pg_n, f"scg{V}{t}"], writes=["sg_t"],
                       kwargs=dict(out=sg_t[:], in_=pg[:], func=AF.Silu, scale=cols[:, t, 6:7]))
                    OP("vector", "tensor_scalar", reads=[pu_n, f"scu{V}{t}"], writes=[f"innr{og}"],
                       kwargs=dict(out=inner[:, og * 512:(og + 1) * 512], in0=pu[:],
                                   scalar1=cols[:, t, 7:8], scalar2=None, op0=ALU.mult))
                    OP("vector", "tensor_tensor", reads=[f"innr{og}", "sg_t"], writes=[f"innr{og}"],
                       kwargs=dict(out=inner[:, og * 512:(og + 1) * 512],
                                   in0=inner[:, og * 512:(og + 1) * 512], in1=sg_t[:], op=ALU.mult))
                OP("vector", "tensor_reduce", reads=[f"innr{og2}" for og2 in range(8)], writes=[f"mx2{V}{t}"],
                   kwargs=dict(out=cols[:, t, 8:9], in_=inner[:], axis=AX.X, op=ALU.max,
                               apply_absolute_value=True))
                OP("vector", "tensor_scalar", reads=[f"mx2{V}{t}"], writes=[f"q2{V}{t}"],
                   kwargs=dict(out=cols[:, t, 9:10], in0=cols[:, t, 8:9], scalar1=1.0 / 127.0,
                               scalar2=1e-5 / 127.0, op0=ALU.mult, op1=ALU.max))
                OP("vector", "reciprocal", reads=[f"q2{V}{t}"], writes=[f"rs2{V}{t}"],
                   kwargs=dict(out=cols[:, t, 10:11], in_=cols[:, t, 9:10]))
                OP("vector", "tensor_tensor", reads=[f"q2{V}{t}", "gc2h"], writes=[f"scd{V}{t}"],
                   kwargs=dict(out=cols[:, t, 11:12], in0=cols[:, t, 9:10], in1=gcols[:, 5:6], op=ALU.mult))
                OP("scalar", "activation", reads=[f"innr{og}" for og in range(8)] + [f"rs2{V}{t}"],
                   writes=[f"n2i8_{tsl}"],
                   kwargs=dict(out=n2i8[:, tsl, :], in_=inner[:], func=AF.Copy,
                               scale=cols[:, t, 10:11], bias=0.0))
            load_d()
            for t in tiles:
                tsl = t % 4
                for q4 in range(4):
                    pst, pst_name = (psT0, "psT0") if q4 % 2 == 0 else (psT1, "psT1")
                    OP("gpsimd", "tensor_copy", reads=[f"n2i8_{tsl}"], writes=["n2bf"],
                       kwargs=dict(out=n2bf[:], in_=n2i8[:, tsl, q4 * 1024:(q4 + 1) * 1024]))
                    for j in range(8):
                        OP("tensor", "transpose", reads=["n2bf", "ident"], writes=[pst_name],
                           kwargs=dict(out=pst[:, j * 128:(j + 1) * 128],
                                       in_=n2bf[:, j * 128:(j + 1) * 128], identity=ident[:]))
                    OP("vector", "tensor_copy", reads=[pst_name], writes=[f"n2cm{q4}"],
                       kwargs=dict(out=n2cm[:, q4 * 8:(q4 + 1) * 8, :],
                                   in_=pst[:].rearrange("p (a b) -> p a b", b=128)))
                for half in range(2):
                    pd, pd_n = (psE, "psE") if half == 0 else (psF, "psF")
                    for ic in range(I0 // 128):
                        OP("tensor", "matmul", reads=[f"n2cm{ic // 8}", f"w0p{ic}"],
                           writes=[pd_n] if ic in (0, I0 // 128 - 1) else [],
                           kwargs=dict(out=pd[:], lhsT=n2cm[:, ic, :],
                                       rhs=w0sb[:, ic, half * 512:(half + 1) * 512],
                                       start=(ic == 0), stop=(ic == I0 // 128 - 1)))
                    OP("vector", "scalar_tensor_tensor", reads=[pd_n, f"scd{V}{t}", f"res{t}"],
                       writes=[f"res{t}"],
                       kwargs=dict(out=res[:, t, half * 512:(half + 1) * 512], in0=pd[:],
                                   scalar=cols[:, t, 11:12], in1=res[:, t, half * 512:(half + 1) * 512],
                                   op0=ALU.mult, op1=ALU.add))
        k.barrier()
        fes.close()

    def l0attn():
        aes = ExitStack()
        wA = aes.enter_context(nc.sbuf_tensor("wA", [128, NC_, 2048], BF16))
        bias_b = aes.enter_context(nc.sbuf_tensor("bias_b", [128, C], F32))
        phe = aes.enter_context(nc.sbuf_tensor("phe", [128, 512], F32))
        OP("sync", "dma_start", writes=["bias_b"], dma=True,
           kwargs=dict(out=bias_b[:], in_=bproj_h[0:1, :].broadcast_to((128, C))))
        qfm = aes.enter_context(nc.sbuf_tensor("qfm", [128, NC_, ROWS], BF16))
        atk = aes.enter_context(nc.sbuf_tensor("atk", [128, NT, 1040], BF16))
        xbt = aes.enter_context(nc.sbuf_tensor("xbt", [128, C], BF16))
        nrm4 = aes.enter_context(nc.sbuf_tensor("nrm4", [128, 2, 4], F32))
        kvst = aes.enter_context(nc.sbuf_tensor("kvst", [128, 2, 455], F32))

        def phi_evac(ps_ap, ps_name, dst_ap, dst_name):
            OP("vector", "tensor_scalar", reads=[ps_name], writes=["phe"],
               kwargs=dict(out=phe[:], in0=ps_ap, scalar1=0.0, scalar2=None, op0=ALU.min))
            OP("scalar", "activation", reads=["phe"], writes=["phe"],
               kwargs=dict(out=phe[:], in_=phe[:], func=AF.Exp))
            OP("vector", "scalar_tensor_tensor", reads=[ps_name, "phe"], writes=[dst_name],
               kwargs=dict(out=dst_ap, in0=ps_ap, scalar=0.0, in1=phe[:], op0=ALU.max, op1=ALU.add))
        for t in range(NT):
            lnorm(t, False, out_bf_ap=xbt[:], out_bf_name="xbt", tag="a")
            transpose_tile_to_cm("xbt", xbt[:], t)
        for c in range(NC_):
            OP("sync", "dma_start", writes=[f"wA{c}"], dma=True,
               kwargs=dict(out=wA[:, c, 0:C], in_=s_qkv[c, :, 0:C]))
        for j in range(NC_):
            for tg in range(4):
                pq, pq_n = (psA, "psA") if tg % 2 == 0 else (psB, "psB")
                for c in range(NC_):
                    OP("tensor", "matmul",
                       reads=[f"wA{c}"] + [f"cm{c}t{tg * 4 + j2}" for j2 in range(4)],
                       writes=[pq_n] if c in (0, NC_ - 1) else [],
                       kwargs=dict(out=pq[:], lhsT=wA[:, c, j * 128:(j + 1) * 128],
                                   rhs=cm[:, c, tg * 512:(tg + 1) * 512],
                                   start=(c == 0), stop=(c == NC_ - 1)))
                phi_evac(pq[:], pq_n, qfm[:, j, tg * 512:(tg + 1) * 512], f"qfm{j}")
        for c in range(NC_):
            OP("sync", "dma_start", writes=[f"wA{c}"], dma=True,
               kwargs=dict(out=wA[:, c, :], in_=s_qkv[c, :, C:3 * C]))
        for hh in range(2):
            for t in range(NT):
                for kv in range(2):
                    pk, pk_n = (psC, "psC") if kv == 0 else (psD, "psD")
                    col0 = hh * 512 if kv == 0 else 1024 + hh * 512
                    for c in range(NC_):
                        OP("tensor", "matmul", reads=[f"cm{c}t{t}", f"wA{c}"],
                           writes=[pk_n] if c in (0, NC_ - 1) else [],
                           kwargs=dict(out=pk[:], lhsT=cm[:, c, t * 128:(t + 1) * 128],
                                       rhs=wA[:, c, col0:col0 + 512],
                                       start=(c == 0), stop=(c == NC_ - 1)))
                    if kv == 0:
                        phi_evac(pk[:], pk_n, atk[:, t, 0:512], f"atk{t}")
                    else:
                        OP("vector", "tensor_copy", reads=[pk_n], writes=[f"atk{t}"],
                           kwargs=dict(out=atk[:, t, 512:1032].rearrange("p (h j) -> p h j", j=65)[:, :, 0:64],
                                       in_=pk[:].rearrange("p (h j) -> p h j", j=64)))
                OP("vector", "memset", writes=[f"atk{t}"],
                   kwargs=dict(ap=atk[:, t, 512:1032].rearrange("p (h j) -> p h j", j=65)[:, :, 64:65], constant=1.0))
            for i in range(32):
                b, mloc = i // 4, i % 4
                m = hh * 4 + mloc
                g = b * 8 + m
                pk, pk_n = (psE, "psE") if (i // 7) % 2 == 0 else (psF, "psF")
                sl7 = i % 7
                for par in range(2):
                    hloc = 2 * mloc + par
                    for dt_ in range(2):
                        t = 2 * b + dt_
                        OP("tensor", "matmul", reads=[f"atk{t}"],
                           writes=[pk_n] if (par, dt_) in ((0, 0), (1, 1)) else [],
                           kwargs=dict(out=pk[par * 64:(par + 1) * 64, sl7 * 65:(sl7 + 1) * 65],
                                       lhsT=atk[:, t, hloc * 64:(hloc + 1) * 64],
                                       rhs=atk[:, t, 512 + hloc * 65:512 + (hloc + 1) * 65],
                                       start=(dt_ == 0), stop=(dt_ == 1),
                                       tile_position=(0, par * 64)))
                if sl7 == 6 or i == 31:
                    n = sl7 + 1
                    g0i = i - sl7
                    lin0 = hh * 32 + g0i
                    ksl = (i // 7) % 2
                    OP("vector", "tensor_copy", reads=[pk_n], writes=[f"kvst{ksl}"],
                       kwargs=dict(out=kvst[:, ksl, 0:n * 65], in_=pk[:, 0:n * 65]))
                    OP("sync", "dma_start", reads=[f"kvst{ksl}"], writes=["kv_in_d"], dma=True,
                       kwargs=dict(out=kv_in[:, lin0 * 65:(lin0 + n) * 65], in_=kvst[:, ksl, 0:n * 65]))
        OP("gpsimd", "collective_compute", reads=["kv_in_d"], writes=["kv_out_d"],
           kwargs=dict(kind="AllReduce", op=ALU.add, replica_groups=[list(range(NCORES))],
                       ins=[kv_in[:]], outs=[kv_out[:]]))
        kvb = aes.enter_context(nc.sbuf_tensor("kvb", [128, 64, 65], BF16))
        g0 = 0
        j = 0
        while g0 < 64:
            n = min(7, 64 - g0)
            ksl = j % 2; j += 1
            OP("sync", "dma_start", reads=["kv_out_d"], writes=[f"kvst{ksl}"], dma=True,
               kwargs=dict(out=kvst[:, ksl, 0:n * 65], in_=kv_out[:, g0 * 65:(g0 + n) * 65]))
            OP("vector", "tensor_copy", reads=[f"kvst{ksl}"], writes=["kvb"],
               kwargs=dict(out=kvb[:, g0:g0 + n, :], in_=kvst[:, ksl, 0:n * 65]))
            g0 += n
        for b in range(B):
            for half in range(2):
                t = 2 * b + half
                for par in range(2):
                    pm, pm_n = (psE, "psE") if par == 0 else (psF, "psF")
                    for m4 in range(2):
                        for mm in range(4):
                            m = m4 * 4 + mm
                            h = 2 * m + par
                            lin = (m // 4) * 32 + b * 4 + (m % 4)
                            OP("tensor", "matmul", reads=[f"qfm{h // 2}", "kvb"],
                               writes=[pm_n] if mm in (0, 3) else [],
                               kwargs=dict(out=pm[:, mm * 65:(mm + 1) * 65],
                                           lhsT=qfm[par * 64:(par + 1) * 64, h // 2, t * 128:(t + 1) * 128],
                                           rhs=kvb[par * 64:(par + 1) * 64, lin, :],
                                           start=True, stop=True,
                                           tile_position=(par * 64, 0)))
                        OP("vector", "tensor_scalar", reads=[pm_n], writes=[f"nrm{par}"],
                           kwargs=dict(out=nrm4[:, par, :],
                                       in0=pm[:, 0:260].rearrange("p (s j) -> p s j", j=65)[:, :, 64],
                                       scalar1=EPS_ATTN, scalar2=None, op0=ALU.add))
                        OP("vector", "reciprocal", reads=[f"nrm{par}"], writes=[f"nrm{par}"],
                           kwargs=dict(out=nrm4[:, par, :], in_=nrm4[:, par, :]))
                        OP("vector", "tensor_tensor", reads=[pm_n, f"nrm{par}"], writes=[f"atk{t}"],
                           kwargs=dict(out=atk[:, t, 0:1024].rearrange("p (m par2 j) -> p m par2 j", par2=2, j=64)[:, m4 * 4:(m4 + 1) * 4, par, :],
                                       in0=pm[:, 0:260].rearrange("p (s j) -> p s j", j=65)[:, :, 0:64],
                                       in1=nrm4[:, par, :, None].broadcast_to((128, 4, 64)),
                                       op=ALU.mult))
        for t in range(NT):
            transpose_tile_to_cm(f"atk{t}", atk[:, t, 0:1024], t)
        for c in range(NC_):
            OP("sync", "dma_start", writes=[f"wA{c}"], dma=True,
               kwargs=dict(out=wA[:, c, 0:C], in_=s_proj[c, :, :]))
        for t in range(NT):
            for half in range(2):
                pp, pp_n = (psA, "psA") if half == 0 else (psB, "psB")
                for c in range(NC_):
                    OP("tensor", "matmul", reads=[f"cm{c}t{t}", f"wA{c}"],
                       writes=[pp_n] if c in (0, NC_ - 1) else [],
                       kwargs=dict(out=pp[:], lhsT=cm[:, c, t * 128:(t + 1) * 128],
                                   rhs=wA[:, c, half * 512:(half + 1) * 512],
                                   start=(c == 0), stop=(c == NC_ - 1)))
                OP("vector", "tensor_tensor", reads=[pp_n, "bias_b"], writes=["phe"],
                   kwargs=dict(out=phe[:], in0=pp[:], in1=bias_b[:, half * 512:(half + 1) * 512], op=ALU.add))
                OP("vector", "tensor_tensor", reads=["phe", f"res{t}"], writes=[f"res{t}"],
                   kwargs=dict(out=res[:, t, half * 512:(half + 1) * 512],
                               in0=phe[:], in1=res[:, t, half * 512:(half + 1) * 512], op=ALU.add))
        k.barrier()
        aes.close()


    def l1ffn(inv):
        V = f"g{inv}"
        J0 = inv * 4
        les0 = ExitStack()
        A = les0.enter_context(nc.sbuf_tensor(f"A{inv}", [128, 24, 2048], BF16))
        i16b = les0.enter_context(nc.sbuf_tensor(f"i16b{inv}", [128, 2, 2048], I16))

        lesa = ExitStack()
        xbt2 = lesa.enter_context(nc.sbuf_tensor(f"xbt2{inv}", [128, C], BF16))
        for t in range(NT):
            lnorm(t, True, out_bf_ap=xbt2[:], out_bf_name="xbt2", tag=V)
            OP("vector", "tensor_reduce", reads=["xlnf"], writes=[f"xm{V}{t}"],
               kwargs=dict(out=cols[:, t, 3:4], in_=xlnf[:], axis=AX.X, op=ALU.max,
                           apply_absolute_value=True))
            transpose_tile_to_cm("xbt2", xbt2[:], t)
        OP("vector", "tensor_reduce", reads=[f"xm{V}{t}" for t in range(NT)], writes=["tc8"],
           kwargs=dict(out=tiny[:, 8:9], in_=cols[:, :, 3], axis=AX.X, op=ALU.max))
        part_max("tc8", tiny[:, 8:9], NT, "tc9", tiny[0:1, 9:10], 7)
        col_allreduce([("tc9", tiny[0:1, 9:10])], J0 + 0, [("gcol6", gcols[:, 6:7])])
        OP("vector", "tensor_scalar", reads=["gcol6"], writes=["gcol7"],
           kwargs=dict(out=gcols[:, 7:8], in0=gcols[:, 6:7], scalar1=1.0 / 32767.0,
                       scalar2=1e-12, op0=ALU.mult, op1=ALU.max))
        OP("vector", "reciprocal", reads=["gcol7"], writes=["gcol8"],
           kwargs=dict(out=gcols[:, 8:9], in_=gcols[:, 7:8]))
        k.barrier()
        lesa.close()
        for c in range(NC_):
            OP("scalar", "activation", reads=[f"cm{c}t{tt_}" for tt_ in range(NT)] + ["gcol8"],
               writes=[f"i16b{c % 2}"],
               kwargs=dict(out=i16b[:, c % 2, :], in_=cm[:, c, :], func=AF.Copy, scale=gcols[:, 8:9], bias=0.0))
            OP("vector", "tensor_scalar", reads=[f"i16b{c % 2}", "gcol7"],
               writes=[f"cm{c}t{tt_}" for tt_ in range(NT)],
               kwargs=dict(out=cm[:, c, :], in0=i16b[:, c % 2, :], scalar1=gcols[:, 7:8], scalar2=None, op0=ALU.mult))

        def dump_fm(get, n=8):
            for j in range(n):
                ap, nm = get(j)
                OP("vector", "tensor_copy", reads=[nm], writes=[f"res{2 * j}"],
                   kwargs=dict(out=res[:, 2 * j, :], in_=ap[:, 0:1024]))
                OP("vector", "tensor_copy", reads=[nm], writes=[f"res{2 * j + 1}"],
                   kwargs=dict(out=res[:, 2 * j + 1, :], in_=ap[:, 1024:2048]))
        if dbg == "l1xq":
            dump_fm(lambda j: (cm[:, j, :], f"cm{j}"))
            k.barrier(); les0.close()
            return
        les1 = ExitStack()
        w1p = les1.enter_context(nc.sbuf_tensor(f"w1p{inv}", [128, 1, NC_, 128], BF16))
        sgf = les1.enter_context(nc.sbuf_tensor(f"sgf{inv}", [128, 2, 1024], F32))
        u32 = les1.enter_context(nc.sbuf_tensor(f"u32{inv}", [128, 1024], I32))
        upb = les1.enter_context(nc.sbuf_tensor(f"upb{inv}", [128, 2, 512], BF16))

        OP("vector", "memset", writes=["tc11"], kwargs=dict(ap=tiny[:, 11:12], constant=0.0))
        def ugmat(of, which, sl_ignore):
            src = s_w1g if which == "g" else s_w1u
            sl = 0
            OP("sync", "dma_start", writes=[f"w1p{sl}"], dma=True,
               kwargs=dict(out=w1p[:, sl, :, :], in_=src[of, :, :, :].rearrange("c p q -> p c q")))
            for tg in range(4):
                pq, pq_n = (psA, "psA") if tg % 2 == 0 else (psB, "psB")
                for c in range(NC_):
                    OP("tensor", "matmul",
                       reads=[f"w1p{sl}"] + [f"cm{c}t{tg * 4 + j2}" for j2 in range(4)],
                       writes=[pq_n] if c in (0, NC_ - 1) else [],
                       kwargs=dict(out=pq[:], lhsT=w1p[:, sl, c, :],
                                   rhs=cm[:, c, tg * 512:(tg + 1) * 512],
                                   start=(c == 0), stop=(c == NC_ - 1)))
                yield of, tg, pq, pq_n
        for of in range(24):
            for _, tg, pq, pq_n in ugmat(of, "g", of % 2):
                OP("vector", "tensor_copy", reads=[pq_n], writes=[f"A{of}"],
                   kwargs=dict(out=A[:, of, tg * 512:(tg + 1) * 512], in_=pq[:]))
            OP("vector", "tensor_reduce", reads=[f"A{of}"], writes=["tc10"],
               kwargs=dict(out=tiny[:, 10:11], in_=A[:, of, :], axis=AX.X, op=ALU.max,
                           apply_absolute_value=True))
            OP("vector", "tensor_tensor", reads=["tc10", "tc11"], writes=["tc11"],
               kwargs=dict(out=tiny[:, 11:12], in0=tiny[:, 10:11], in1=tiny[:, 11:12], op=ALU.max))
        if dbg == "l1gate":
            dump_fm(lambda j: (A[:, j, :], f"A{j}"))
            k.barrier(); les1.close(); les0.close()
            return
        part_max("tc11", tiny[:, 11:12], 1, "tc12", tiny[0:1, 12:13], 7)
        col_allreduce([("tc12", tiny[0:1, 12:13])], J0 + 1, [("gcol9", gcols[:, 9:10])])
        OP("vector", "tensor_scalar", reads=["gcol9"], writes=["gcol10"],
           kwargs=dict(out=gcols[:, 10:11], in0=gcols[:, 9:10], scalar1=1.0 / 32767.0,
                       scalar2=1e-12, op0=ALU.mult, op1=ALU.max))
        OP("vector", "reciprocal", reads=["gcol10"], writes=["gcol11"],
           kwargs=dict(out=gcols[:, 11:12], in_=gcols[:, 10:11]))
        OP("vector", "memset", writes=["tc13"], kwargs=dict(ap=tiny[:, 13:14], constant=0.0))
        OP("vector", "memset", writes=["tc15"], kwargs=dict(ap=tiny[:, 15:16], constant=0.0))
        for of in range(24):
            for _, tg, pq, pq_n in ugmat(of, "u", of % 2):
                OP("vector", "tensor_copy", reads=[pq_n], writes=[f"upb{tg % 2}"],
                   kwargs=dict(out=upb[:, tg % 2, :], in_=pq[:]))
                OP("vector", "tensor_reduce", reads=[pq_n], writes=["tc10"],
                   kwargs=dict(out=tiny[:, 10:11], in_=pq[:], axis=AX.X, op=ALU.max,
                               apply_absolute_value=True))
                OP("vector", "tensor_tensor", reads=["tc10", "tc13"], writes=["tc13"],
                   kwargs=dict(out=tiny[:, 13:14], in0=tiny[:, 10:11], in1=tiny[:, 13:14], op=ALU.max))
                OP("sync", "dma_start", reads=[f"upb{tg % 2}"], writes=["s_up_d"], dma=True,
                   kwargs=dict(out=s_up[of, :, tg * 512:(tg + 1) * 512], in_=upb[:, tg % 2, :]))
            ib = of % 2
            OP("scalar", "activation", reads=[f"A{of}", "gcol11"], writes=[f"i16b{ib}"],
               kwargs=dict(out=i16b[:, ib, :], in_=A[:, of, :], func=AF.Copy, scale=gcols[:, 11:12], bias=0.0))
            for hf in range(2):
                sl_ = slice(hf * 1024, (hf + 1) * 1024)
                OP("scalar", "activation", reads=[f"i16b{ib}", "gcol10"], writes=[f"sgf{hf}"],
                   kwargs=dict(out=sgf[:, hf, :], in_=i16b[:, ib, sl_], func=AF.Sigmoid, scale=gcols[:, 10:11]))
                OP("scalar", "activation", reads=[f"sgf{hf}"], writes=["u32"],
                   kwargs=dict(out=u32[:], in_=sgf[:, hf, :], func=AF.Copy, scale=65536.0, bias=0.0))
                OP("vector", "tensor_scalar", reads=["u32"], writes=[f"sgf{hf}"],
                   kwargs=dict(out=sgf[:, hf, :], in0=u32[:], scalar1=65535.0, scalar2=1.0 / 65536.0,
                               op0=ALU.min, op1=ALU.mult))
                OP("vector", "tensor_tensor", reads=[f"i16b{ib}", f"sgf{hf}"], writes=[f"A{of}"],
                   kwargs=dict(out=A[:, of, sl_], in0=i16b[:, ib, sl_], in1=sgf[:, hf, :], op=ALU.mult))
            OP("vector", "tensor_reduce", reads=[f"A{of}"], writes=["tc10"],
               kwargs=dict(out=tiny[:, 10:11], in_=A[:, of, :], axis=AX.X, op=ALU.max,
                           apply_absolute_value=True))
            OP("vector", "tensor_tensor", reads=["tc10", "tc15"], writes=["tc15"],
               kwargs=dict(out=tiny[:, 15:16], in0=tiny[:, 10:11], in1=tiny[:, 15:16], op=ALU.max))
        if dbg == "l1g2":
            dump_fm(lambda j: (A[:, j, :], f"A{j}"))
            k.barrier(); les1.close(); les0.close()
            return
        OP("vector", "tensor_tensor", reads=["tc15", "gcol10"], writes=["tc15"],
           kwargs=dict(out=tiny[:, 15:16], in0=tiny[:, 15:16], in1=gcols[:, 10:11], op=ALU.mult))
        part_max("tc13", tiny[:, 13:14], 1, "tc14", tiny[0:1, 14:15], 7)
        part_max("tc15", tiny[:, 15:16], 1, "tc15", tiny[0:1, 15:16], 7)
        col_allreduce([("tc14", tiny[0:1, 14:15]), ("tc15", tiny[0:1, 15:16])], J0 + 2,
                      [("gcol12", gcols[:, 12:13]), ("gcol13", gcols[:, 13:14])])
        OP("vector", "tensor_scalar", reads=["gcol12"], writes=["gcol12"],
           kwargs=dict(out=gcols[:, 12:13], in0=gcols[:, 12:13], scalar1=1.0 / 32767.0,
                       scalar2=1e-12, op0=ALU.mult, op1=ALU.max))
        OP("vector", "reciprocal", reads=["gcol12"], writes=["gcol14"],
           kwargs=dict(out=gcols[:, 14:15], in_=gcols[:, 12:13]))
        OP("vector", "tensor_scalar", reads=["gcol13"], writes=["gcol13"],
           kwargs=dict(out=gcols[:, 13:14], in0=gcols[:, 13:14], scalar1=1.0 / 32767.0,
                       scalar2=1e-12, op0=ALU.mult, op1=ALU.max))
        OP("vector", "reciprocal", reads=["gcol13"], writes=["gcol15"],
           kwargs=dict(out=gcols[:, 15:16], in_=gcols[:, 13:14]))
        OP("vector", "tensor_tensor", reads=["gcol15", "gcol10"], writes=["gcol15"],
           kwargs=dict(out=gcols[:, 15:16], in0=gcols[:, 15:16], in1=gcols[:, 10:11], op=ALU.mult))
        k.barrier()
        les1.close()
        les2 = ExitStack()
        upf = les2.enter_context(nc.sbuf_tensor(f"upf{inv}", [128, 2, 2048], BF16))
        i16c = les2.enter_context(nc.sbuf_tensor(f"i16c{inv}", [128, 2, 2048], I16))
        OP("vector", "memset", writes=["tc13"], kwargs=dict(ap=tiny[:, 13:14], constant=0.0))
        for of in range(24):
            sl = of % 2
            OP("sync", "dma_start", reads=["s_up_d"], writes=[f"upf{sl}"], dma=True,
               kwargs=dict(out=upf[:, sl, :], in_=s_up[of, :, :]))
            ib = of % 2
            OP("scalar", "activation", reads=[f"upf{sl}", "gcol14"], writes=[f"i16c{ib}"],
               kwargs=dict(out=i16c[:, ib, :], in_=upf[:, sl, :], func=AF.Copy, scale=gcols[:, 14:15], bias=0.0))
            OP("scalar", "activation", reads=[f"A{of}", "gcol15"], writes=[f"i16b{ib}"],
               kwargs=dict(out=i16b[:, ib, :], in_=A[:, of, :], func=AF.Copy, scale=gcols[:, 15:16], bias=0.0))
            OP("vector", "tensor_tensor", reads=[f"i16b{ib}", f"i16c{ib}"], writes=[f"A{of}"],
               kwargs=dict(out=A[:, of, :], in0=i16b[:, ib, :], in1=i16c[:, ib, :], op=ALU.mult))
            OP("vector", "tensor_reduce", reads=[f"A{of}"], writes=["tc10"],
               kwargs=dict(out=tiny[:, 10:11], in_=A[:, of, :], axis=AX.X, op=ALU.max,
                           apply_absolute_value=True))
            OP("vector", "tensor_tensor", reads=["tc10", "tc13"], writes=["tc13"],
               kwargs=dict(out=tiny[:, 13:14], in0=tiny[:, 10:11], in1=tiny[:, 13:14], op=ALU.max))
        if dbg == "l1p":
            dump_fm(lambda j: (A[:, j, :], f"A{j}"))
            k.barrier(); les2.close(); les1.close() if False else None; les0.close()
            return
        OP("vector", "tensor_tensor", reads=["tc13", "gcol13"], writes=["tc13"],
           kwargs=dict(out=tiny[:, 13:14], in0=tiny[:, 13:14], in1=gcols[:, 13:14], op=ALU.mult))
        OP("vector", "tensor_tensor", reads=["tc13", "gcol12"], writes=["tc13"],
           kwargs=dict(out=tiny[:, 13:14], in0=tiny[:, 13:14], in1=gcols[:, 12:13], op=ALU.mult))
        part_max("tc13", tiny[:, 13:14], 1, "tc14", tiny[0:1, 14:15], 7)
        col_allreduce([("tc14", tiny[0:1, 14:15])], J0 + 3, [("gcol9", gcols[:, 9:10])])
        OP("vector", "tensor_scalar", reads=["gcol9"], writes=["gcol9"],
           kwargs=dict(out=gcols[:, 9:10], in0=gcols[:, 9:10], scalar1=1.0 / 32767.0,
                       scalar2=1e-12, op0=ALU.mult, op1=ALU.max))
        OP("vector", "reciprocal", reads=["gcol9"], writes=["gcol10"],
           kwargs=dict(out=gcols[:, 10:11], in_=gcols[:, 9:10]))
        OP("vector", "tensor_tensor", reads=["gcol10", "gcol13"], writes=["gcol10"],
           kwargs=dict(out=gcols[:, 10:11], in0=gcols[:, 10:11], in1=gcols[:, 13:14], op=ALU.mult))
        OP("vector", "tensor_tensor", reads=["gcol10", "gcol12"], writes=["gcol10"],
           kwargs=dict(out=gcols[:, 10:11], in0=gcols[:, 10:11], in1=gcols[:, 12:13], op=ALU.mult))
        OP("vector", "tensor_scalar", reads=["gcol9"], writes=["gcol11"],
           kwargs=dict(out=gcols[:, 11:12], in0=gcols[:, 9:10], scalar1=0.5, scalar2=None, op0=ALU.mult))
        if dbg == "l1oq":
            dump_fm(lambda j: (A[:, j, :], f"A{j}"))
            k.barrier(); les2.close(); les0.close()
            return
        k.barrier()
        les2.close()
        les3 = ExitStack()
        w1dg = les3.enter_context(nc.sbuf_tensor(f"w1dg{inv}", [128, 4, C], BF16))
        for ocg in range(6):
            for j in range(4):
                of = ocg * 4 + j
                OP("scalar", "activation", reads=[f"A{of}", "gcol10"], writes=[f"i16b{of % 2}"],
                   kwargs=dict(out=i16b[:, of % 2, :], in_=A[:, of, :], func=AF.Copy, scale=gcols[:, 10:11], bias=0.0))
                OP("gpsimd", "tensor_copy", reads=[f"i16b{of % 2}"], writes=[f"A{of}"],
                   kwargs=dict(out=A[:, of, :], in_=i16b[:, of % 2, :]))
                OP("sync", "dma_start", writes=[f"w1dg{j}"], dma=True,
                   kwargs=dict(out=w1dg[:, j, :], in_=s_w1d[ocg * 4 + j, :, :]))
            for t in range(NT):
                for half in range(2):
                    pd, pd_n = (psE, "psE") if half == 0 else (psF, "psF")
                    for j in range(4):
                        oc = ocg * 4 + j
                        OP("tensor", "matmul", reads=[f"A{oc}", f"w1dg{j}"],
                           writes=[pd_n] if j in (0, 3) else [],
                           kwargs=dict(out=pd[:], lhsT=A[:, oc, t * 128:(t + 1) * 128],
                                       rhs=w1dg[:, j, half * 512:(half + 1) * 512],
                                       start=(j == 0), stop=(j == 3)))
                    OP("vector", "scalar_tensor_tensor", reads=[pd_n, "gcol11", f"res{t}"],
                       writes=[f"res{t}"],
                       kwargs=dict(out=res[:, t, half * 512:(half + 1) * 512], in0=pd[:],
                                   scalar=gcols[:, 11:12], in1=res[:, t, half * 512:(half + 1) * 512],
                                   op0=ALU.mult, op1=ALU.add))
        k.barrier()
        les3.close()
        les0.close()

    def l1attn():
        ses = ExitStack()
        ut = ses.enter_context(nc.sbuf_tensor("ut", [128, NT, C], BF16))
        selmat = ses.enter_context(nc.sbuf_tensor("selmat", [8, 1024], F32))
        isel = ses.enter_context(nc.sbuf_tensor("isel", [8, 1024], I32))
        OP("gpsimd", "iota", writes=["isel"], kwargs=dict(out=isel[:], pattern=[[1, 1024]], base=0, channel_multiplier=-128))
        OP("gpsimd", "tensor_scalar", reads=["isel"], writes=["self0"],
           kwargs=dict(out=selmat[:], in0=isel[:], scalar1=0, scalar2=None, op0=ALU.is_ge))
        OP("gpsimd", "tensor_scalar", reads=["isel"], writes=["self1"],
           kwargs=dict(out=isel[:], in0=isel[:], scalar1=128, scalar2=None, op0=ALU.is_lt))
        OP("gpsimd", "tensor_tensor", reads=["self0", "self1"], writes=["selmat"],
           kwargs=dict(out=selmat[:], in0=selmat[:], in1=isel[:], op=ALU.mult))
        Kb = ses.enter_context(nc.sbuf_tensor("Kb", [128, 8, C], F32))
        Db = ses.enter_context(nc.sbuf_tensor("Db", [128, C], F32))
        accV = ses.enter_context(nc.sbuf_tensor("accV", [128, C], F32))
        tmpV = ses.enter_context(nc.sbuf_tensor("tmpV", [128, C], F32))
        accG = ses.enter_context(nc.sbuf_tensor("accG", [128, C], F32))
        tmpG = ses.enter_context(nc.sbuf_tensor("tmpG", [128, C], F32))
        Krow = ses.enter_context(nc.sbuf_tensor("Krow", [8, 1024], F32))
        sN = 16 * NDIM // 16
        ss = ses.enter_context(nc.sbuf_tensor("ssm", [128, 16, NC_, NDIM], F32))
        SS = lambda i: ss[:, i, :, :]

        def ld_param(dst_i, src, ncols, name):
            OP("sync", "dma_start", writes=[name], dma=True,
               kwargs=dict(out=ss[:, dst_i, :, 0:ncols],
                           in_=src.rearrange("(c p) n -> p c n", p=128),
                           allow_slow_non_contiguous=True))

        ld_param(0, ldt_h, 1, "ssm0")
        ld_param(1, lar_h, NDIM, "ssm1")
        ld_param(2, aim_h, NDIM, "ssm2")
        OP("sync", "dma_start", writes=["ssm3"], dma=True,
           kwargs=dict(out=ss[:, 3, :, :], in_=bc_h[0].rearrange("(c p) n -> p c n", p=128)))
        OP("sync", "dma_start", writes=["ssm4"], dma=True,
           kwargs=dict(out=ss[:, 4, :, :], in_=bc_h[1].rearrange("(c p) n -> p c n", p=128)))
        OP("sync", "dma_start", writes=["ssm5"], dma=True,
           kwargs=dict(out=ss[:, 5, :, :], in_=cc_h[0].rearrange("(c p) n -> p c n", p=128)))
        OP("sync", "dma_start", writes=["ssm6"], dma=True,
           kwargs=dict(out=ss[:, 6, :, :], in_=cc_h[1].rearrange("(c p) n -> p c n", p=128)))
        OP("sync", "dma_start", writes=["Db"], dma=True,
           kwargs=dict(out=Db[:], in_=dd_h[0:1, :].broadcast_to((128, C))))
        def tt(o, a, b, op, ro=None, rn=None):
            OP("vector", "tensor_tensor", reads=[f"ssm{a}", f"ssm{b}"] + (ro or []), writes=[f"ssm{o}"] + (rn or []),
               kwargs=dict(out=SS(o), in0=SS(a), in1=SS(b), op=op))
        def tsc(o, a, s1, s2, op0, op1=None):
            OP("vector", "tensor_scalar", reads=[f"ssm{a}"], writes=[f"ssm{o}"],
               kwargs=dict(out=SS(o), in0=SS(a), scalar1=s1, scalar2=s2, op0=op0,
                           **({"op1": op1} if op1 else {})))
        OP("scalar", "activation", reads=["ssm0"], writes=["ssm7"],
           kwargs=dict(out=ss[:, 7, :, 0:1], in_=ss[:, 0, :, 0:1], func=AF.Exp))
        OP("scalar", "activation", reads=["ssm1"], writes=["ssm8"],
           kwargs=dict(out=SS(8), in_=SS(1), func=AF.Exp))
        OP("vector", "tensor_tensor", reads=["ssm8", "ssm7"], writes=["ssm8"],
           kwargs=dict(out=SS(8), in0=SS(8), in1=ss[:, 7, :, 0:1].broadcast_to((128, NC_, NDIM)), op=ALU.mult))
        OP("vector", "tensor_tensor", reads=["ssm2", "ssm7"], writes=["ssm9"],
           kwargs=dict(out=SS(9), in0=SS(2), in1=ss[:, 7, :, 0:1].broadcast_to((128, NC_, NDIM)), op=ALU.mult))
        tsc(10, 8, 0.5, 1.0, ALU.mult, ALU.add)
        tsc(11, 9, -0.5, None, ALU.mult)
        OP("scalar", "activation", reads=["ssm10"], writes=["ssm12"],
           kwargs=dict(out=SS(12), in_=SS(10), func=AF.Square))
        OP("scalar", "activation", reads=["ssm11"], writes=["ssm13"],
           kwargs=dict(out=SS(13), in_=SS(11), func=AF.Square))
        tt(12, 12, 13, ALU.add)
        OP("vector", "reciprocal", reads=["ssm12"], writes=["ssm14"],
           kwargs=dict(out=SS(14), in_=SS(12)))
        tsc(12, 8, -0.5, 1.0, ALU.mult, ALU.add)
        tsc(13, 9, 0.5, None, ALU.mult)
        tt(15, 12, 10, ALU.mult)
        tt(8, 13, 11, ALU.mult, ro=["ssm8"])
        tt(8, 15, 8, ALU.add)
        tt(8, 8, 14, ALU.mult)
        tt(15, 13, 10, ALU.mult)
        tt(9, 12, 11, ALU.mult, ro=["ssm9"])
        tt(9, 15, 9, ALU.subtract)
        tt(9, 9, 14, ALU.mult)
        tt(15, 5, 3, ALU.mult)
        tt(12, 6, 4, ALU.mult)
        tt(12, 15, 12, ALU.subtract)
        tt(15, 5, 4, ALU.mult)
        tt(13, 6, 3, ALU.mult)
        tt(13, 15, 13, ALU.add)
        OP("vector", "tensor_tensor", reads=["ssm12", "ssm7"], writes=["ssm12"],
           kwargs=dict(out=SS(12), in0=SS(12), in1=ss[:, 7, :, 0:1].broadcast_to((128, NC_, NDIM)), op=ALU.mult))
        OP("vector", "tensor_tensor", reads=["ssm13", "ssm7"], writes=["ssm13"],
           kwargs=dict(out=SS(13), in0=SS(13), in1=ss[:, 7, :, 0:1].broadcast_to((128, NC_, NDIM)), op=ALU.mult))
        tt(15, 12, 10, ALU.mult)
        tt(5, 13, 11, ALU.mult, ro=["ssm5"])
        tt(5, 15, 5, ALU.add)
        tt(5, 5, 14, ALU.mult)
        tt(15, 13, 10, ALU.mult)
        tt(6, 12, 11, ALU.mult, ro=["ssm6"])
        tt(6, 15, 6, ALU.subtract)
        tt(6, 6, 14, ALU.mult)
        tsc(5, 5, 2.0 * SCALE_SSM, None, ALU.mult)
        tsc(6, 6, 2.0 * SCALE_SSM, None, ALU.mult)
        OP("vector", "memset", writes=["ssm10"], kwargs=dict(ap=SS(10), constant=1.0))
        OP("vector", "memset", writes=["ssm11"], kwargs=dict(ap=SS(11), constant=0.0))
        Kc = ses.enter_context(nc.sbuf_tensor("Kc", [128, NC_, 8], F32))
        for l in range(8):
            tt(15, 5, 10, ALU.mult)
            tt(12, 6, 11, ALU.mult)
            tt(12, 15, 12, ALU.subtract)
            OP("vector", "tensor_reduce", reads=["ssm12"], writes=[f"Kc{l}"],
               kwargs=dict(out=Kc[:, :, l], in_=SS(12), axis=AX.X, op=ALU.add))
            if l < 7:
                tt(15, 10, 8, ALU.mult)
                tt(12, 11, 9, ALU.mult)
                tt(13, 10, 9, ALU.mult)
                tt(10, 11, 8, ALU.mult, ro=["ssm10"])
                tt(11, 13, 10, ALU.add)
                tt(10, 15, 12, ALU.subtract)
        for c in range(NC_):
            OP("tensor", "transpose", reads=[f"Kc{l}" for l in range(8)] + ["identf"], writes=["psF"],
               kwargs=dict(out=psF[0:8, 0:128], in_=Kc[:, c, :], identity=ident_f[:]))
            OP("vector", "tensor_copy", reads=["psF"], writes=["Krow"],
               kwargs=dict(out=Krow[:, c * 128:(c + 1) * 128], in_=psF[0:8, 0:128]))
        for tau in range(8):
            for half in range(2):
                pq, pq_n = (psA, "psA") if half == 0 else (psB, "psB")
                OP("tensor", "matmul", reads=["selmat", "Krow"], writes=[pq_n],
                   kwargs=dict(out=pq[:], lhsT=selmat[:, tau * 128:(tau + 1) * 128],
                               rhs=Krow[:, half * 512:(half + 1) * 512], start=True, stop=True))
                OP("vector", "tensor_copy", reads=[pq_n], writes=[f"Kb{tau}"],
                   kwargs=dict(out=Kb[:, tau, half * 512:(half + 1) * 512], in_=pq[:]))
        OP("vector", "tensor_tensor", reads=["Kb0", "Db"], writes=["Kb0"],
           kwargs=dict(out=Kb[:, 0, :], in0=Kb[:, 0, :], in1=Db[:], op=ALU.add))
        for t in range(NT):
            lnorm(t, False, out_bf_ap=ut[:, t, :], out_bf_name=f"ut{t}", tag="s")
        for t in range(NT):
            b, half = t // 2, t % 2
            eng = "vector" if t % 2 == 0 else "gpsimd"
            acc, accn = (accV, "accV") if eng == "vector" else (accG, "accG")
            tmp, tmpn = (tmpV, "tmpV") if eng == "vector" else (tmpG, "tmpG")
            OP(eng, "tensor_tensor", reads=[f"ut{t}", "Kb0"], writes=[accn],
               kwargs=dict(out=acc[:], in0=ut[:, t, :], in1=Kb[:, 0, :], op=ALU.mult))
            for tau in range(1, b + 1):
                ts_ = 2 * (b - tau) + half
                OP(eng, "tensor_tensor", reads=[f"ut{ts_}", f"Kb{tau}"], writes=[tmpn],
                   kwargs=dict(out=tmp[:], in0=ut[:, ts_, :], in1=Kb[:, tau, :], op=ALU.mult))
                OP(eng, "tensor_tensor", reads=[tmpn, accn], writes=[accn],
                   kwargs=dict(out=acc[:], in0=tmp[:], in1=acc[:], op=ALU.add))
            OP(eng, "tensor_tensor", reads=[accn, f"res{t}"], writes=[f"res{t}"],
               kwargs=dict(out=res[:, t, :], in0=acc[:], in1=res[:, t, :], op=ALU.add))
        k.barrier()
        ses.close()

    if n_sub >= 1:
        l0ffn(0)
    if n_sub >= 2:
        l0attn()
    if n_sub >= 3:
        l0ffn(1)
    if n_sub >= 4:
        l1ffn(0)
    if n_sub >= 5:
        l1attn()
    if n_sub >= 6:
        l1ffn(1)

    for t in range(NT):
        b, half = t // 2, t % 2
        OP("sync" if t % 2 == 0 else "scalar", "dma_start", reads=[f"res{t}"], writes=["y_d"], dma=True,
           kwargs=dict(out=y_h[b, half * 128:(half + 1) * 128, :], in_=res[:, t, :]))
    for i in range(N_DMA_SEMS):
        k._wait("sync", f"d_sync{i}", k.cnt[f"d_sync{i}"])
        k._wait("sync", f"d_scalar{i}", k.cnt[f"d_scalar{i}"])

    block = es.enter_context(nc.Block())
    k.finalize(block)
    es.close()
    return nc


def host_inputs(inputs, core):
    x = np.ascontiguousarray(inputs["L0_attn_x"][:, core * TPC:(core + 1) * TPC, :], dtype=np.float32)
    T32 = lambda a: np.ascontiguousarray(np.asarray(a, dtype=np.float32).T)
    C32 = lambda a: np.ascontiguousarray(np.asarray(a, dtype=np.float32))
    return {
        "x": x,
        "w0g": T32(inputs["L0_ffn_w_gate"]),
        "w0u": T32(inputs["L0_ffn_w_up"]),
        "w0d": T32(inputs["L0_ffn_w_down"]),
        "wqkv": T32(inputs["L0_attn_Wqkv"]),
        "wproj": T32(inputs["L0_attn_Wproj"]),
        "bproj": C32(inputs["L0_attn_bproj"]).reshape(1, C),
        "w1g": T32(inputs["L1_ffn_w_gate"]),
        "w1u": T32(inputs["L1_ffn_w_up"]),
        "w1d": T32(inputs["L1_ffn_w_down"]),
        "ldt": C32(inputs["L1_attn_log_dt"]).reshape(C, 1),
        "lar": C32(inputs["L1_attn_log_A_real"]),
        "aim": C32(inputs["L1_attn_A_imag"]),
        "bc": np.ascontiguousarray(np.asarray(inputs["L1_attn_B"], dtype=np.float32).transpose(2, 0, 1)),
        "ccp": np.ascontiguousarray(np.asarray(inputs["L1_attn_C"], dtype=np.float32).transpose(2, 0, 1)),
        "dd": C32(inputs["L1_attn_D"]).reshape(1, C),
    }


def run(inputs, n_sub=6):
    nc = build_program(n_sub)
    in_maps = [host_inputs(inputs, i) for i in range(NCORES)]
    res_ = run_bass_kernel_spmd(nc, in_maps, core_ids=list(range(NCORES)))
    return np.concatenate([r["y"] for r in res_.results], axis=1)


def kernel(**inputs):
    nc = build_program(6)
    in_maps = [host_inputs(inputs, i) for i in range(NCORES)]
    res_ = run_bass_kernel_spmd(nc, in_maps, core_ids=list(range(NCORES)))
    return np.concatenate([r["y"] for r in res_.results], axis=1)


def kernel(**inputs):
    nc = build_program(6)
    in_maps = [host_inputs(inputs, i) for i in range(NCORES)]
    res_ = run_bass_kernel_spmd(nc, in_maps, core_ids=list(range(NCORES)))
    return np.concatenate([r["y"] for r in res_.results], axis=1)


def kernel(**inputs):
    nc = build_program(6)
    in_maps = [host_inputs(inputs, i) for i in range(NCORES)]
    res_ = run_bass_kernel_spmd(nc, in_maps, core_ids=list(range(NCORES)))
    return np.concatenate([r["y"] for r in res_.results], axis=1)


# version 17
# speedup vs baseline: 1.0071x; 1.0071x over previous
import numpy as np
from contextlib import ExitStack

import concourse.bass as bass
import concourse.mybir as mybir
from concourse.bass_utils import run_bass_kernel_spmd

F32 = mybir.dt.float32
BF16 = mybir.dt.bfloat16
FP8 = mybir.dt.float8e4
I8 = mybir.dt.int8
I16 = mybir.dt.int16
I32 = mybir.dt.int32
AF = mybir.ActivationFunctionType
ALU = mybir.AluOpType
AX = mybir.AxisListType

B, T, C = 8, 2048, 1024
NCORES = 8
TPC = T // NCORES
ROWS = B * TPC
NT = ROWS // 128
NC_ = C // 128
EPS_LN = 1e-5
EPS_ATTN = 1e-6
H, D = 16, 64
I0 = 4096
I1 = 3072
NDIM = 16
SCALE_SSM = float(np.sqrt(1.0 / NDIM))

N_DMA_SEMS = 6


class Buf:
    __slots__ = ("name", "writers", "readers")
    def __init__(self, name):
        self.name = name
        self.writers = {}
        self.readers = {}


class Builder:
    def __init__(self, nc, es):
        self.nc = nc
        self.es = es
        self.recorded = {"tensor": [], "vector": [], "scalar": [], "gpsimd": [], "sync": []}
        self.sems = {}
        self.cnt = {}
        for e in ("tensor", "vector", "scalar", "gpsimd"):
            self._new_sem("e_" + e)
        for q in ("sync", "scalar", "gpsimd"):
            for i in range(N_DMA_SEMS):
                self._new_sem(f"d_{q}{i}")
        self._new_sem("cc")
        self.dma_rr = {"sync": 0, "scalar": 0, "gpsimd": 0}
        self.waited = {e: {} for e in self.recorded}
        self.bufs = {}

    def _new_sem(self, name):
        self.sems[name] = self.es.enter_context(self.nc.semaphore(name))
        self.cnt[name] = 0

    def buf(self, name):
        if name not in self.bufs:
            self.bufs[name] = Buf(name)
        return self.bufs[name]

    def _wait(self, eng, sem_name, val):
        if val <= 0:
            return
        if eng == "tensor" and sem_name == "e_tensor":
            return
        if self.waited[eng].get(sem_name, 0) >= val:
            return
        self.waited[eng][sem_name] = val
        self.recorded[eng].append(("wait", sem_name, val))

    def op(self, eng, meth, reads=(), writes=(), dma=False, args=(), kwargs=None):
        kwargs = kwargs or {}
        rd = [self.buf(r) for r in reads]
        wr = [self.buf(w) for w in writes]
        for b in rd:
            for s, v in b.writers.items():
                self._wait(eng, s, v)
        for b in wr:
            for s, v in b.writers.items():
                self._wait(eng, s, v)
            for s, v in b.readers.items():
                self._wait(eng, s, v)
        if dma:
            q = eng if eng in ("sync", "scalar", "gpsimd") else None
            assert q is not None, f"engine {eng} cannot DMA"
            i = self.dma_rr[q]
            self.dma_rr[q] = (i + 1) % N_DMA_SEMS
            sem_name = f"d_{q}{i}"
            inc = 16
            self._wait(eng, sem_name, self.cnt[sem_name])
        else:
            sem_name = "e_" + eng
            inc = 1
        self.cnt[sem_name] += inc
        tok = (sem_name, self.cnt[sem_name])
        self.recorded[eng].append(("op", meth, args, kwargs, sem_name, inc))
        for b in wr:
            if b.writers.get(tok[0], 0) < tok[1]:
                b.writers[tok[0]] = tok[1]
        for b in rd:
            if b.readers.get(tok[0], 0) < tok[1]:
                b.readers[tok[0]] = tok[1]
        return tok

    def barrier(self):
        snap = {s: v for s, v in self.cnt.items() if v > 0}
        for eng in self.recorded:
            for s, v in snap.items():
                if eng == "tensor" and s == "e_tensor":
                    continue
                self._wait(eng, s, v)

    def finalize(self, block):
        sems = self.sems
        def run(eng_obj, eng_name):
            for item in self.recorded[eng_name]:
                if item[0] == "wait":
                    eng_obj.wait_ge(sems[item[1]], item[2])
                else:
                    _, meth, args, kwargs, sem_name, inc = item
                    f = getattr(eng_obj, meth)
                    try:
                        r = f(*args, **kwargs)
                    except Exception:
                        import sys
                        print(f"FAILED OP: {eng_name}.{meth} kwargs keys: "
                              f"{ {kk: getattr(vv, 'shape', vv) if not isinstance(vv, (int, float, str, bool, type(None))) else vv for kk, vv in kwargs.items()} }",
                              file=sys.stderr)
                        raise
                    r.then_inc(sems[sem_name], inc)
        @block.tensor
        def _(t): run(t, "tensor")
        @block.vector
        def _(v): run(v, "vector")
        @block.scalar
        def _(s): run(s, "scalar")
        @block.gpsimd
        def _(g): run(g, "gpsimd")
        @block.sync
        def _(s): run(s, "sync")




def build_program(n_sub=6, dbg=None):
    nc = bass.Bass("TRN2", target_bir_lowering=False, dynamic_dma_scratch_size=512)
    es = ExitStack()

    x_h = nc.dram_tensor("x", [B, TPC, C], F32, kind="ExternalInput").ap()
    y_h = nc.dram_tensor("y", [B, TPC, C], F32, kind="ExternalOutput").ap()
    w0g_h = nc.dram_tensor("w0g", [C, I0], F32, kind="ExternalInput").ap()
    w0u_h = nc.dram_tensor("w0u", [C, I0], F32, kind="ExternalInput").ap()
    w0d_h = nc.dram_tensor("w0d", [I0, C], F32, kind="ExternalInput").ap()
    wqkv_h = nc.dram_tensor("wqkv", [C, 3 * C], F32, kind="ExternalInput").ap()
    wproj_h = nc.dram_tensor("wproj", [C, C], F32, kind="ExternalInput").ap()
    bproj_h = nc.dram_tensor("bproj", [1, C], F32, kind="ExternalInput").ap()
    w1g_h = nc.dram_tensor("w1g", [C, I1], F32, kind="ExternalInput").ap()
    w1u_h = nc.dram_tensor("w1u", [C, I1], F32, kind="ExternalInput").ap()
    w1d_h = nc.dram_tensor("w1d", [I1, C], F32, kind="ExternalInput").ap()
    ldt_h = nc.dram_tensor("ldt", [C, 1], F32, kind="ExternalInput").ap()
    lar_h = nc.dram_tensor("lar", [C, NDIM], F32, kind="ExternalInput").ap()
    aim_h = nc.dram_tensor("aim", [C, NDIM], F32, kind="ExternalInput").ap()
    bc_h = nc.dram_tensor("bc", [2, C, NDIM], F32, kind="ExternalInput").ap()
    cc_h = nc.dram_tensor("ccp", [2, C, NDIM], F32, kind="ExternalInput").ap()
    dd_h = nc.dram_tensor("dd", [1, C], F32, kind="ExternalInput").ap()

    s_w0g = nc.dram_tensor("s_w0g", [NC_, 128, I0], FP8).ap()
    s_w0u = nc.dram_tensor("s_w0u", [NC_, 128, I0], FP8).ap()
    s_w0d = nc.dram_tensor("s_w0d", [I0 // 128, 128, C], FP8).ap()
    s_w1g = nc.dram_tensor("s_w1g", [I1 // 128, NC_, 128, 128], BF16).ap()
    s_w1u = nc.dram_tensor("s_w1u", [I1 // 128, NC_, 128, 128], BF16).ap()
    s_w1d = nc.dram_tensor("s_w1d", [I1 // 128, 128, C], BF16).ap()
    s_qkv = nc.dram_tensor("s_qkv", [NC_, 128, 3 * C], BF16).ap()
    s_proj = nc.dram_tensor("s_proj", [NC_, 128, C], BF16).ap()
    s_up = nc.dram_tensor("s_up", [I1 // 128, 128, ROWS], BF16).ap()
    kv_in = nc.dram_tensor("kv_in", [128, 65 * 64], F32).ap()
    kv_out = nc.dram_tensor("kv_out", [128, 65 * 64], F32).ap()
    NMX = 12
    mxi = [nc.dram_tensor(f"mx_in{j}", [4, 1], F32).ap() for j in range(NMX)]
    mxo = [nc.dram_tensor(f"mx_out{j}", [4, 1], F32).ap() for j in range(NMX)]

    sbp = lambda n, s, d=F32: es.enter_context(nc.sbuf_tensor(n, s, d))
    res = sbp("res", [128, NT, C])
    cm = es.enter_context(nc.sbuf_tensor("cm", [128, NC_, ROWS], BF16))
    xlnf = sbp("xlnf", [128, C])
    ident = es.enter_context(nc.sbuf_tensor("ident", [128, 128], BF16))
    ident_f = sbp("identf", [128, 128])
    ones1 = sbp("ones1", [1, 128])
    cols = sbp("cols", [128, NT, 12])
    gcols = sbp("gcols", [128, 16])
    tiny = sbp("tiny", [128, 16])
    bnst = sbp("bnst", [128, 12])
    cst = sbp("cst", [128, 1])
    psA = es.enter_context(nc.psum_tensor("psA", [128, 512], F32))
    psB = es.enter_context(nc.psum_tensor("psB", [128, 512], F32))
    psC = es.enter_context(nc.psum_tensor("psC", [128, 512], F32))
    psD = es.enter_context(nc.psum_tensor("psD", [128, 512], F32))
    psT0 = es.enter_context(nc.psum_tensor("psT0", [128, 1024], BF16))
    psT1 = es.enter_context(nc.psum_tensor("psT1", [128, 1024], BF16))
    psE = es.enter_context(nc.psum_tensor("psE", [128, 512], F32))
    psF = es.enter_context(nc.psum_tensor("psF", [128, 512], F32))

    k = Builder(nc, es)
    OP = k.op

    def scalar_bcast(name_src, src_ap, dst_name, dst_ap):
        OP("tensor", "matmul", reads=["ones1", name_src], writes=["psF"],
           kwargs=dict(out=psF[:, 0:1], lhsT=ones1[:], rhs=src_ap, start=True, stop=True))
        OP("vector", "tensor_copy", reads=["psF"], writes=[dst_name], kwargs=dict(out=dst_ap, in_=psF[:, 0:1]))

    def col_allreduce(vals, j, out_cols):
        assert len(vals) <= 4 and len(out_cols) == len(vals)
        for i, (nm, ap) in enumerate(vals):
            OP("sync", "dma_start", reads=[nm], writes=[f"mxi{j}"], dma=True,
               kwargs=dict(out=mxi[j][i:i + 1, 0:1], in_=ap))
        OP("gpsimd", "collective_compute", reads=[f"mxi{j}"], writes=[f"mxo{j}"],
           kwargs=dict(kind="AllReduce", op=ALU.max, replica_groups=[list(range(NCORES))],
                       ins=[mxi[j][:]], outs=[mxo[j][:]]))
        for i, (nm, ap) in enumerate(out_cols):
            OP("sync", "dma_start", reads=[f"mxo{j}"], writes=[nm], dma=True,
               kwargs=dict(out=ap, in_=mxo[j][i:i + 1, 0:1].broadcast_to((128, 1))))

    def part_max(src_name, src_ap, nfree, dst_name, dst_ap, ctag):
        OP("vector", "tensor_reduce", reads=[src_name], writes=[f"tc{ctag}"],
           kwargs=dict(out=tiny[:, ctag:ctag + 1], in_=src_ap, axis=AX.X, op=ALU.max))
        OP("tensor", "transpose", reads=[f"tc{ctag}", "identf"], writes=["psF"],
           kwargs=dict(out=psF[0:1, 0:128], in_=tiny[:, ctag:ctag + 1], identity=ident_f[:]))
        OP("vector", "tensor_reduce", reads=["psF"], writes=[dst_name],
           kwargs=dict(out=dst_ap, in_=psF[0:1, 0:128], axis=AX.X, op=ALU.max))

    def lnorm(t, out_f32, out_bf_ap=None, out_bf_name=None, tag=""):
        OP("vector", "bn_stats", reads=[f"res{t}"], writes=["bnsA"],
           kwargs=dict(out=bnst[:, 0:6], in_=res[:, t, 0:512]))
        OP("vector", "bn_stats", reads=[f"res{t}"], writes=["bnsB"],
           kwargs=dict(out=bnst[:, 6:12], in_=res[:, t, 512:1024]))
        OP("vector", "bn_aggr", reads=["bnsA", "bnsB"], writes=[f"mv{tag}{t}"],
           kwargs=dict(out=cols[:, t, 0:2], in_=bnst[:]))
        OP("scalar", "activation", reads=[f"mv{tag}{t}"], writes=[f"sd{tag}{t}"],
           kwargs=dict(out=cols[:, t, 2:3], in_=cols[:, t, 1:2], func=AF.Sqrt, bias=EPS_LN))
        OP("vector", "reciprocal", reads=[f"sd{tag}{t}"], writes=[f"rstd{tag}{t}"],
           kwargs=dict(out=cols[:, t, 2:3], in_=cols[:, t, 2:3]))
        if out_f32:
            OP("vector", "tensor_scalar", reads=[f"res{t}", f"mv{tag}{t}", f"rstd{tag}{t}"],
               writes=["xlnf"],
               kwargs=dict(out=xlnf[:], in0=res[:, t, :], scalar1=cols[:, t, 0:1],
                           scalar2=cols[:, t, 2:3], op0=ALU.subtract, op1=ALU.mult))
        if out_bf_ap is not None:
            OP("vector", "tensor_scalar", reads=[f"res{t}", f"mv{tag}{t}", f"rstd{tag}{t}"],
               writes=[out_bf_name],
               kwargs=dict(out=out_bf_ap, in0=res[:, t, :], scalar1=cols[:, t, 0:1],
                           scalar2=cols[:, t, 2:3], op0=ALU.subtract, op1=ALU.mult))

    def s1_tile(t, V, bufs):
        nf32b, ntokb, nbtb = bufs
        sl = t % 2
        lnorm(t, True, tag=V)
        OP("vector", "tensor_reduce", reads=["xlnf"], writes=[f"mx{V}{t}"],
           kwargs=dict(out=cols[:, t, 3:4], in_=xlnf[:], axis=AX.X, op=ALU.max,
                       apply_absolute_value=True))
        OP("vector", "tensor_scalar", reads=[f"mx{V}{t}"], writes=[f"q{V}{t}"],
           kwargs=dict(out=cols[:, t, 4:5], in0=cols[:, t, 3:4], scalar1=1.0 / 127.0,
                       scalar2=1e-5 / 127.0, op0=ALU.mult, op1=ALU.max))
        OP("vector", "reciprocal", reads=[f"q{V}{t}"], writes=[f"rs{V}{t}"],
           kwargs=dict(out=cols[:, t, 5:6], in_=cols[:, t, 4:5]))
        OP("vector", "tensor_tensor", reads=[f"q{V}{t}", "mcol_g"], writes=[f"scg{V}{t}"],
           kwargs=dict(out=cols[:, t, 6:7], in0=cols[:, t, 4:5], in1=gcols[:, 0:1], op=ALU.mult))
        OP("vector", "tensor_tensor", reads=[f"q{V}{t}", "mcol_u"], writes=[f"scu{V}{t}"],
           kwargs=dict(out=cols[:, t, 7:8], in0=cols[:, t, 4:5], in1=gcols[:, 1:2], op=ALU.mult))
        OP("scalar", "activation", reads=["xlnf", f"rs{V}{t}"], writes=["s1ntok"],
           kwargs=dict(out=ntokb[:], in_=xlnf[:], func=AF.Copy, scale=cols[:, t, 5:6], bias=0.0))
        OP("gpsimd", "tensor_copy", reads=["s1ntok"], writes=[f"s1nbt{sl}"],
           kwargs=dict(out=nbtb[:, sl, :], in_=ntokb[:]))
        transpose_tile_to_cm(f"s1nbt{sl}", nbtb[:, sl, :], t)

    def transpose_tile_to_cm(src_name, src_ap, t):
        pst, pst_name = (psT0, "psT0") if t % 2 == 0 else (psT1, "psT1")
        for c in range(NC_):
            OP("tensor", "transpose", reads=[src_name, "ident"], writes=[pst_name],
               kwargs=dict(out=pst[:, c * 128:(c + 1) * 128], in_=src_ap[:, c * 128:(c + 1) * 128],
                           identity=ident[:]))
        OP("vector", "tensor_copy", reads=[pst_name],
           writes=[f"cm{c}t{t}" for c in range(NC_)],
           kwargs=dict(out=cm[:, :, t * 128:(t + 1) * 128], in_=pst[:].rearrange("p (a b) -> p a b", b=128)))

    with nc.sbuf_tensor("itmp", [128, 128], I32) as itmp:
        OP("gpsimd", "iota", writes=["itmp"], kwargs=dict(out=itmp[:], pattern=[[1, 128]], base=0, channel_multiplier=-1))
        OP("gpsimd", "tensor_scalar", reads=["itmp"], writes=["ident"],
           kwargs=dict(out=ident[:], in0=itmp[:], scalar1=0, scalar2=None, op0=ALU.is_equal))
        OP("gpsimd", "tensor_copy", reads=["ident"], writes=["identf"], kwargs=dict(out=ident_f[:], in_=ident[:]))
        OP("vector", "memset", writes=["ones1"], kwargs=dict(ap=ones1[:], constant=1.0))
        OP("vector", "memset", writes=["cst"], kwargs=dict(ap=cst[:], constant=EPS_LN))
        nc.const_aps.register(F32, EPS_LN, cst[:, 0:1])
        for t in range(NT):
            b, half = t // 2, t % 2
            OP("sync" if t % 2 == 0 else "scalar", "dma_start", writes=[f"res{t}"], dma=True,
               kwargs=dict(out=res[:, t, :], in_=x_h[b, half * 128:(half + 1) * 128, :]))
    k.barrier()

    pes = ExitStack()
    wld = pes.enter_context(nc.sbuf_tensor("wld", [128, 3, 2048], F32))
    wq8 = pes.enter_context(nc.sbuf_tensor("wq8", [128, 3, 2048], I8))
    wf8 = pes.enter_context(nc.sbuf_tensor("wf8", [128, 3, 2048], FP8))
    wbf = pes.enter_context(nc.sbuf_tensor("wbf", [128, 3, 2048], BF16))
    wsum = pes.enter_context(nc.sbuf_tensor("wsum", [128, 80], F32))
    wbt = pes.enter_context(nc.sbuf_tensor("wbt", [128, 3, 2048], F32))
    wsc = pes.enter_context(nc.sbuf_tensor("wsc", [128, 3, 64], F32))
    wrc = pes.enter_context(nc.sbuf_tensor("wrc", [128, 3, 64], F32))
    wi8 = pes.enter_context(nc.sbuf_tensor("wi8", [128, 3, 2048], I8))
    wqb = pes.enter_context(nc.sbuf_tensor("wqb", [128, 3, 2048], BF16))
    wob = pes.enter_context(nc.sbuf_tensor("wob", [128, 3, 2048], BF16))

    def slabs(n):
        out, off = [], 0
        while off < n:
            w = min(2048, n - off)
            out.append((off, w))
            off += w
        return out

    WSUM_OFF = {0: 0, 1: 16, 2: 32}

    def prep_w0(src_h, dst_s, n_i, mcol_name, mcol_ap, swci):
        nch = src_h.shape[0] // 128
        sl_list = slabs(n_i)
        wsoff = WSUM_OFF[swci]
        nslab = 0
        for c in range(nch):
            for (off, w) in sl_list:
                s = nslab; nslab += 1
                sl = s % 3
                OP("sync" if s % 2 == 0 else "scalar", "dma_start", writes=[f"wld{sl}"], dma=True,
                   kwargs=dict(out=wld[:, sl, 0:w], in_=src_h[c * 128:(c + 1) * 128, off:off + w]))
                OP("vector", "tensor_reduce", reads=[f"wld{sl}"], writes=[f"wsum{wsoff + s}"],
                   kwargs=dict(out=wsum[:, wsoff + s:wsoff + s + 1], in_=wld[:, sl, 0:w], axis=AX.X,
                               op=ALU.add, apply_absolute_value=True))
        tot = nslab
        c0 = 4 * swci
        OP("vector", "tensor_reduce", reads=[f"wsum{wsoff + s}" for s in range(tot)], writes=[f"tinyA{swci}"],
           kwargs=dict(out=tiny[:, c0:c0 + 1], in_=wsum[:, wsoff:wsoff + tot], axis=AX.X, op=ALU.add))
        OP("tensor", "transpose", reads=[f"tinyA{swci}", "identf"], writes=["psF"],
           kwargs=dict(out=psF[0:1, 0:128], in_=tiny[:, c0:c0 + 1], identity=ident_f[:]))
        OP("vector", "tensor_reduce", reads=["psF"], writes=[f"tinyB{swci}"],
           kwargs=dict(out=tiny[0:1, c0 + 1:c0 + 2], in_=psF[0:1, 0:128], axis=AX.X, op=ALU.add))
        OP("vector", "tensor_scalar", reads=[f"tinyB{swci}"], writes=[f"tinyC{swci}"],
           kwargs=dict(out=tiny[0:1, c0 + 2:c0 + 3], in0=tiny[0:1, c0 + 1:c0 + 2],
                       scalar1=1.0 / (src_h.shape[0] * n_i), scalar2=1e-5, op0=ALU.mult, op1=ALU.max))
        OP("vector", "reciprocal", reads=[f"tinyC{swci}"], writes=[f"tinyD{swci}"],
           kwargs=dict(out=tiny[0:1, c0 + 3:c0 + 4], in_=tiny[0:1, c0 + 2:c0 + 3]))
        scalar_bcast(f"tinyD{swci}", tiny[0:1, c0 + 3:c0 + 4], "swcol" + mcol_name, gcols[:, 8 + swci:9 + swci])
        scalar_bcast(f"tinyC{swci}", tiny[0:1, c0 + 2:c0 + 3], mcol_name, mcol_ap)
        nslab = 0
        for c in range(nch):
            for (off, w) in sl_list:
                s = nslab % 3; nslab += 1
                OP("sync" if s % 2 == 0 else "scalar", "dma_start", writes=[f"wld{s}"], dma=True,
                   kwargs=dict(out=wld[:, s, 0:w], in_=src_h[c * 128:(c + 1) * 128, off:off + w]))
                OP("scalar", "activation", reads=[f"wld{s}", "swcol" + mcol_name], writes=[f"wq8{s}"],
                   kwargs=dict(out=wq8[:, s, 0:w], in_=wld[:, s, 0:w], func=AF.Copy,
                               scale=gcols[:, 8 + swci:9 + swci], bias=0.0))
                OP("gpsimd", "tensor_scalar", reads=[f"wq8{s}"], writes=[f"wf8{s}"],
                   kwargs=dict(out=wf8[:, s, 0:w], in0=wq8[:, s, 0:w], scalar1=1.0, scalar2=-1.0,
                               op0=ALU.min, op1=ALU.max))
                OP("scalar", "dma_start", reads=[f"wf8{s}"], writes=[f"dst{dst_s.tensor.name}"], dma=True,
                   kwargs=dict(out=dst_s[c, :, off:off + w], in_=wf8[:, s, 0:w]))

    def prep_cast(src_h, dst_s, n_o):
        nch = src_h.shape[0] // 128
        i = 0
        for c in range(nch):
            for (off, w) in slabs(n_o):
                s = i % 3; i += 1
                OP("sync" if s % 2 == 0 else "scalar", "dma_start", writes=[f"wld{s}"], dma=True,
                   kwargs=dict(out=wld[:, s, 0:w], in_=src_h[c * 128:(c + 1) * 128, off:off + w]))
                OP("gpsimd", "tensor_copy", reads=[f"wld{s}"], writes=[f"wbf{s}"],
                   kwargs=dict(out=wbf[:, s, 0:w], in_=wld[:, s, 0:w]))
                OP("scalar", "dma_start", reads=[f"wbf{s}"], writes=[f"dst{dst_s.tensor.name}"], dma=True,
                   kwargs=dict(out=dst_s[c, :, off:off + w], in_=wbf[:, s, 0:w]))

    def lpbq_slab(sl, w, store_fn):
        OP("vector", "transpose", reads=[f"wld{sl}"], writes=[f"wbt{sl}"],
           kwargs=dict(out=wbt[:, sl, 0:w], in_=wld[:, sl, 0:w]))
        OP("vector", "tensor_reduce", reads=[f"wbt{sl}"], writes=[f"wsc{sl}"],
           kwargs=dict(out=wsc[:, sl, 0:w // 32], in_=wbt[:, sl, 0:w].rearrange("p (a b) -> p a b", b=32),
                       axis=AX.X, op=ALU.max, apply_absolute_value=True))
        OP("vector", "tensor_scalar", reads=[f"wsc{sl}"], writes=[f"wsc{sl}"],
           kwargs=dict(out=wsc[:, sl, 0:w // 32], in0=wsc[:, sl, 0:w // 32],
                       scalar1=1.0 / 7.0, scalar2=1e-12, op0=ALU.mult, op1=ALU.max))
        OP("vector", "reciprocal", reads=[f"wsc{sl}"], writes=[f"wrc{sl}"],
           kwargs=dict(out=wrc[:, sl, 0:w // 32], in_=wsc[:, sl, 0:w // 32]))
        OP("vector", "tensor_tensor", reads=[f"wbt{sl}", f"wrc{sl}"], writes=[f"wbt{sl}"],
           kwargs=dict(out=wbt[:, sl, 0:w].rearrange("p (a b) -> p a b", b=32),
                       in0=wbt[:, sl, 0:w].rearrange("p (a b) -> p a b", b=32),
                       in1=wrc[:, sl, 0:w // 32, None].broadcast_to((128, w // 32, 32)),
                       op=ALU.mult))
        OP("vector", "tensor_copy", reads=[f"wbt{sl}"], writes=[f"wi8{sl}"],
           kwargs=dict(out=wi8[:, sl, 0:w], in_=wbt[:, sl, 0:w]))
        OP("vector", "tensor_tensor", reads=[f"wi8{sl}", f"wsc{sl}"], writes=[f"wqb{sl}"],
           kwargs=dict(out=wqb[:, sl, 0:w].rearrange("p (a b) -> p a b", b=32),
                       in0=wi8[:, sl, 0:w].rearrange("p (a b) -> p a b", b=32),
                       in1=wsc[:, sl, 0:w // 32, None].broadcast_to((128, w // 32, 32)),
                       op=ALU.mult))
        OP("vector", "transpose", reads=[f"wqb{sl}"], writes=[f"wob{sl}"],
           kwargs=dict(out=wob[:, sl, 0:w], in_=wqb[:, sl, 0:w]))
        store_fn(sl, w)

    def prep_w1ud(src_h, dst_s):
        i = 0
        for c in range(NC_):
            for (off, w) in slabs(I1):
                sl = i % 3; i += 1
                OP("sync", "dma_start", writes=[f"wld{sl}"], dma=True,
                   kwargs=dict(out=wld[:, sl, 0:w], in_=src_h[c * 128:(c + 1) * 128, off:off + w]))
                def store(sl, w, c=c, off=off):
                    for j in range(w // 128):
                        of = (off + j * 128) // 128
                        OP("scalar", "dma_start", reads=[f"wob{sl}"], writes=[f"dst{dst_s.tensor.name}"], dma=True,
                           kwargs=dict(out=dst_s[of, c, :, :], in_=wob[:, sl, j * 128:(j + 1) * 128]))
                lpbq_slab(sl, w, store)

    def prep_w1d(src_h, dst_s):
        for c in range(I1 // 128):
            sl = c % 3
            OP("sync", "dma_start", writes=[f"wld{sl}"], dma=True,
               kwargs=dict(out=wld[:, sl, 0:1024], in_=src_h[c * 128:(c + 1) * 128, :]))
            def store(sl, w, c=c):
                OP("scalar", "dma_start", reads=[f"wob{sl}"], writes=[f"dst{dst_s.tensor.name}"], dma=True,
                   kwargs=dict(out=dst_s[c, :, :], in_=wob[:, sl, 0:w]))
            lpbq_slab(sl, 1024, store)

    nf32p = pes.enter_context(nc.sbuf_tensor("nf32p", [128, C], F32))
    ntokp = pes.enter_context(nc.sbuf_tensor("ntokp", [128, C], I8))
    nbtp = pes.enter_context(nc.sbuf_tensor("nbtp", [128, 2, C], BF16))
    if dbg != "noprep":
        prep_w0(w0g_h, s_w0g, I0, "mcol_g", gcols[:, 0:1], 0)
        prep_w0(w0u_h, s_w0u, I0, "mcol_u", gcols[:, 1:2], 1)
    else:
        OP("vector", "memset", writes=["mcol_g"], kwargs=dict(ap=gcols[:, 0:1], constant=1.0))
        OP("vector", "memset", writes=["mcol_u"], kwargs=dict(ap=gcols[:, 1:2], constant=1.0))
    for t in range(NT):
        s1_tile(t, "f0", (nf32p, ntokp, nbtp))
    if dbg != "noprep":
        prep_w0(w0d_h, s_w0d, C, "mcol_d", gcols[:, 2:3], 2)
    else:
        OP("vector", "memset", writes=["mcol_d"], kwargs=dict(ap=gcols[:, 2:3], constant=1.0))
    if n_sub >= 2:
        prep_cast(wqkv_h, s_qkv, 3 * C)
        prep_cast(wproj_h, s_proj, C)
    if n_sub >= 4:
        prep_w1ud(w1g_h, s_w1g)
        prep_w1ud(w1u_h, s_w1u)
        prep_w1d(w1d_h, s_w1d)
    OP("vector", "tensor_scalar", reads=["mcol_d"], writes=["gc2h"],
       kwargs=dict(out=gcols[:, 5:6], in0=gcols[:, 2:3], scalar1=0.5, scalar2=None, op0=ALU.mult))
    k.barrier()
    pes.close()

    def l0ffn(inv):
        V = f"f{inv}"
        fes = ExitStack()
        w0sb = fes.enter_context(nc.sbuf_tensor(f"w0sb{inv}", [128, 64, 1024], FP8))
        n2i8 = fes.enter_context(nc.sbuf_tensor(f"n2i8{inv}", [128, 4, I0], I8))
        inner = fes.enter_context(nc.sbuf_tensor(f"innr{inv}", [128, I0], F32))
        sg_t = fes.enter_context(nc.sbuf_tensor(f"sg_t{inv}", [128, 512], F32))
        ntok = fes.enter_context(nc.sbuf_tensor(f"ntk{inv}", [128, C], I8))
        nf32 = fes.enter_context(nc.sbuf_tensor(f"nf32{inv}", [128, C], F32))
        nbt = fes.enter_context(nc.sbuf_tensor(f"nbt{inv}", [128, 2, C], BF16))
        mx8 = fes.enter_context(nc.sbuf_tensor(f"mx8{inv}", [128, 16], F32))
        n2bf = fes.enter_context(nc.sbuf_tensor(f"n2bf{inv}", [128, I0 // 4], BF16))
        n2cm = fes.enter_context(nc.sbuf_tensor(f"n2cm{inv}", [128, I0 // 128, 128], BF16))

        def load_gu():
            for c in range(NC_):
                OP("sync", "dma_start", writes=[f"w0p{4 * c + j}" for j in range(4)], dma=True,
                   kwargs=dict(out=w0sb[:, 4 * c:4 * c + 4, :], in_=s_w0g[c, :, :].rearrange("p (a b) -> p a b", b=1024)))
                OP("scalar", "dma_start", writes=[f"w0p{32 + 4 * c + j}" for j in range(4)], dma=True,
                   kwargs=dict(out=w0sb[:, 32 + 4 * c:32 + 4 * c + 4, :], in_=s_w0u[c, :, :].rearrange("p (a b) -> p a b", b=1024)))

        def load_d():
            for ic in range(I0 // 128):
                q = "sync" if ic % 2 == 0 else "scalar"
                OP(q, "dma_start", writes=[f"w0p{ic}"], dma=True,
                   kwargs=dict(out=w0sb[:, ic, :], in_=s_w0d[ic, :, :]))

        if inv == 1:
            for t in range(NT):
                s1_tile(t, V, (nf32, ntok, nbt))
        for pack in range(4):
            tiles = range(pack * 4, pack * 4 + 4)
            load_gu()
            for t in tiles:
                tsl = t % 4
                for og in range(8):
                    pg, pg_n, pu, pu_n = [(psA, "psA", psB, "psB"), (psC, "psC", psD, "psD"),
                                          (psE, "psE", psF, "psF")][og % 3]
                    gplane = lambda c: 4 * c + og // 2
                    uplane = lambda c: 32 + 4 * c + og // 2
                    gsl = (og % 2) * 512
                    for c in range(NC_):
                        OP("tensor", "matmul", reads=[f"cm{c}t{t}", f"w0p{gplane(c)}"],
                           writes=[pg_n] if c in (0, NC_ - 1) else [],
                           kwargs=dict(out=pg[:], lhsT=cm[:, c, t * 128:(t + 1) * 128],
                                       rhs=w0sb[:, gplane(c), gsl:gsl + 512],
                                       start=(c == 0), stop=(c == NC_ - 1)))
                        OP("tensor", "matmul", reads=[f"cm{c}t{t}", f"w0p{uplane(c)}"],
                           writes=[pu_n] if c in (0, NC_ - 1) else [],
                           kwargs=dict(out=pu[:], lhsT=cm[:, c, t * 128:(t + 1) * 128],
                                       rhs=w0sb[:, uplane(c), gsl:gsl + 512],
                                       start=(c == 0), stop=(c == NC_ - 1)))
                    OP("scalar", "activation", reads=[pg_n, f"scg{V}{t}"], writes=["sg_t"],
                       kwargs=dict(out=sg_t[:], in_=pg[:], func=AF.Silu, scale=cols[:, t, 6:7]))
                    OP("vector", "tensor_scalar", reads=[pu_n, f"scu{V}{t}"], writes=[f"innr{og}"],
                       kwargs=dict(out=inner[:, og * 512:(og + 1) * 512], in0=pu[:],
                                   scalar1=cols[:, t, 7:8], scalar2=None, op0=ALU.mult))
                    OP("vector", "tensor_tensor", reads=[f"innr{og}", "sg_t"], writes=[f"innr{og}"],
                       kwargs=dict(out=inner[:, og * 512:(og + 1) * 512],
                                   in0=inner[:, og * 512:(og + 1) * 512], in1=sg_t[:], op=ALU.mult))
                OP("vector", "tensor_reduce", reads=[f"innr{og2}" for og2 in range(8)], writes=[f"mx2{V}{t}"],
                   kwargs=dict(out=cols[:, t, 8:9], in_=inner[:], axis=AX.X, op=ALU.max,
                               apply_absolute_value=True))
                OP("vector", "tensor_scalar", reads=[f"mx2{V}{t}"], writes=[f"q2{V}{t}"],
                   kwargs=dict(out=cols[:, t, 9:10], in0=cols[:, t, 8:9], scalar1=1.0 / 127.0,
                               scalar2=1e-5 / 127.0, op0=ALU.mult, op1=ALU.max))
                OP("vector", "reciprocal", reads=[f"q2{V}{t}"], writes=[f"rs2{V}{t}"],
                   kwargs=dict(out=cols[:, t, 10:11], in_=cols[:, t, 9:10]))
                OP("vector", "tensor_tensor", reads=[f"q2{V}{t}", "gc2h"], writes=[f"scd{V}{t}"],
                   kwargs=dict(out=cols[:, t, 11:12], in0=cols[:, t, 9:10], in1=gcols[:, 5:6], op=ALU.mult))
                OP("scalar", "activation", reads=[f"innr{og}" for og in range(8)] + [f"rs2{V}{t}"],
                   writes=[f"n2i8_{tsl}"],
                   kwargs=dict(out=n2i8[:, tsl, :], in_=inner[:], func=AF.Copy,
                               scale=cols[:, t, 10:11], bias=0.0))
            load_d()
            for t in tiles:
                tsl = t % 4
                for q4 in range(4):
                    pst, pst_name = (psT0, "psT0") if q4 % 2 == 0 else (psT1, "psT1")
                    OP("gpsimd", "tensor_copy", reads=[f"n2i8_{tsl}"], writes=["n2bf"],
                       kwargs=dict(out=n2bf[:], in_=n2i8[:, tsl, q4 * 1024:(q4 + 1) * 1024]))
                    for j in range(8):
                        OP("tensor", "transpose", reads=["n2bf", "ident"], writes=[pst_name],
                           kwargs=dict(out=pst[:, j * 128:(j + 1) * 128],
                                       in_=n2bf[:, j * 128:(j + 1) * 128], identity=ident[:]))
                    OP("vector", "tensor_copy", reads=[pst_name], writes=[f"n2cm{q4}"],
                       kwargs=dict(out=n2cm[:, q4 * 8:(q4 + 1) * 8, :],
                                   in_=pst[:].rearrange("p (a b) -> p a b", b=128)))
                for half in range(2):
                    pd, pd_n = (psE, "psE") if half == 0 else (psF, "psF")
                    for ic in range(I0 // 128):
                        OP("tensor", "matmul", reads=[f"n2cm{ic // 8}", f"w0p{ic}"],
                           writes=[pd_n] if ic in (0, I0 // 128 - 1) else [],
                           kwargs=dict(out=pd[:], lhsT=n2cm[:, ic, :],
                                       rhs=w0sb[:, ic, half * 512:(half + 1) * 512],
                                       start=(ic == 0), stop=(ic == I0 // 128 - 1)))
                    OP("vector", "scalar_tensor_tensor", reads=[pd_n, f"scd{V}{t}", f"res{t}"],
                       writes=[f"res{t}"],
                       kwargs=dict(out=res[:, t, half * 512:(half + 1) * 512], in0=pd[:],
                                   scalar=cols[:, t, 11:12], in1=res[:, t, half * 512:(half + 1) * 512],
                                   op0=ALU.mult, op1=ALU.add))
        k.barrier()
        fes.close()

    def l0attn():
        aes = ExitStack()
        wA = aes.enter_context(nc.sbuf_tensor("wA", [128, NC_, 2048], BF16))
        bias_b = aes.enter_context(nc.sbuf_tensor("bias_b", [128, C], F32))
        phe = aes.enter_context(nc.sbuf_tensor("phe", [128, 512], F32))
        OP("sync", "dma_start", writes=["bias_b"], dma=True,
           kwargs=dict(out=bias_b[:], in_=bproj_h[0:1, :].broadcast_to((128, C))))
        qfm = aes.enter_context(nc.sbuf_tensor("qfm", [128, NC_, ROWS], BF16))
        atk = aes.enter_context(nc.sbuf_tensor("atk", [128, NT, 1040], BF16))
        xbt = aes.enter_context(nc.sbuf_tensor("xbt", [128, C], BF16))
        nrm4 = aes.enter_context(nc.sbuf_tensor("nrm4", [128, 2, 4], F32))
        kvst = aes.enter_context(nc.sbuf_tensor("kvst", [128, 2, 455], F32))

        def phi_evac(ps_ap, ps_name, dst_ap, dst_name):
            OP("vector", "tensor_scalar", reads=[ps_name], writes=["phe"],
               kwargs=dict(out=phe[:], in0=ps_ap, scalar1=0.0, scalar2=None, op0=ALU.min))
            OP("scalar", "activation", reads=["phe"], writes=["phe"],
               kwargs=dict(out=phe[:], in_=phe[:], func=AF.Exp))
            OP("vector", "scalar_tensor_tensor", reads=[ps_name, "phe"], writes=[dst_name],
               kwargs=dict(out=dst_ap, in0=ps_ap, scalar=0.0, in1=phe[:], op0=ALU.max, op1=ALU.add))
        for t in range(NT):
            lnorm(t, False, out_bf_ap=xbt[:], out_bf_name="xbt", tag="a")
            transpose_tile_to_cm("xbt", xbt[:], t)
        for c in range(NC_):
            OP("sync", "dma_start", writes=[f"wA{c}"], dma=True,
               kwargs=dict(out=wA[:, c, 0:C], in_=s_qkv[c, :, 0:C]))
        for j in range(NC_):
            for tg in range(4):
                pq, pq_n = (psA, "psA") if tg % 2 == 0 else (psB, "psB")
                for c in range(NC_):
                    OP("tensor", "matmul",
                       reads=[f"wA{c}"] + [f"cm{c}t{tg * 4 + j2}" for j2 in range(4)],
                       writes=[pq_n] if c in (0, NC_ - 1) else [],
                       kwargs=dict(out=pq[:], lhsT=wA[:, c, j * 128:(j + 1) * 128],
                                   rhs=cm[:, c, tg * 512:(tg + 1) * 512],
                                   start=(c == 0), stop=(c == NC_ - 1)))
                phi_evac(pq[:], pq_n, qfm[:, j, tg * 512:(tg + 1) * 512], f"qfm{j}")
        for c in range(NC_):
            OP("sync", "dma_start", writes=[f"wA{c}"], dma=True,
               kwargs=dict(out=wA[:, c, :], in_=s_qkv[c, :, C:3 * C]))
        for hh in range(2):
            for t in range(NT):
                for kv in range(2):
                    pk, pk_n = (psC, "psC") if kv == 0 else (psD, "psD")
                    col0 = hh * 512 if kv == 0 else 1024 + hh * 512
                    for c in range(NC_):
                        OP("tensor", "matmul", reads=[f"cm{c}t{t}", f"wA{c}"],
                           writes=[pk_n] if c in (0, NC_ - 1) else [],
                           kwargs=dict(out=pk[:], lhsT=cm[:, c, t * 128:(t + 1) * 128],
                                       rhs=wA[:, c, col0:col0 + 512],
                                       start=(c == 0), stop=(c == NC_ - 1)))
                    if kv == 0:
                        phi_evac(pk[:], pk_n, atk[:, t, 0:512], f"atk{t}")
                    else:
                        OP("vector", "tensor_copy", reads=[pk_n], writes=[f"atk{t}"],
                           kwargs=dict(out=atk[:, t, 512:1032].rearrange("p (h j) -> p h j", j=65)[:, :, 0:64],
                                       in_=pk[:].rearrange("p (h j) -> p h j", j=64)))
                OP("vector", "memset", writes=[f"atk{t}"],
                   kwargs=dict(ap=atk[:, t, 512:1032].rearrange("p (h j) -> p h j", j=65)[:, :, 64:65], constant=1.0))
            for i in range(32):
                b, mloc = i // 4, i % 4
                m = hh * 4 + mloc
                g = b * 8 + m
                pk, pk_n = (psE, "psE") if (i // 7) % 2 == 0 else (psF, "psF")
                sl7 = i % 7
                for par in range(2):
                    hloc = 2 * mloc + par
                    for dt_ in range(2):
                        t = 2 * b + dt_
                        OP("tensor", "matmul", reads=[f"atk{t}"],
                           writes=[pk_n] if (par, dt_) in ((0, 0), (1, 1)) else [],
                           kwargs=dict(out=pk[par * 64:(par + 1) * 64, sl7 * 65:(sl7 + 1) * 65],
                                       lhsT=atk[:, t, hloc * 64:(hloc + 1) * 64],
                                       rhs=atk[:, t, 512 + hloc * 65:512 + (hloc + 1) * 65],
                                       start=(dt_ == 0), stop=(dt_ == 1),
                                       tile_position=(0, par * 64)))
                if sl7 == 6 or i == 31:
                    n = sl7 + 1
                    g0i = i - sl7
                    lin0 = hh * 32 + g0i
                    ksl = (i // 7) % 2
                    OP("vector", "tensor_copy", reads=[pk_n], writes=[f"kvst{ksl}"],
                       kwargs=dict(out=kvst[:, ksl, 0:n * 65], in_=pk[:, 0:n * 65]))
                    OP("sync", "dma_start", reads=[f"kvst{ksl}"], writes=["kv_in_d"], dma=True,
                       kwargs=dict(out=kv_in[:, lin0 * 65:(lin0 + n) * 65], in_=kvst[:, ksl, 0:n * 65]))
        OP("gpsimd", "collective_compute", reads=["kv_in_d"], writes=["kv_out_d"],
           kwargs=dict(kind="AllReduce", op=ALU.add, replica_groups=[list(range(NCORES))],
                       ins=[kv_in[:]], outs=[kv_out[:]]))
        kvb = aes.enter_context(nc.sbuf_tensor("kvb", [128, 64, 65], BF16))
        g0 = 0
        j = 0
        while g0 < 64:
            n = min(7, 64 - g0)
            ksl = j % 2; j += 1
            OP("sync", "dma_start", reads=["kv_out_d"], writes=[f"kvst{ksl}"], dma=True,
               kwargs=dict(out=kvst[:, ksl, 0:n * 65], in_=kv_out[:, g0 * 65:(g0 + n) * 65]))
            OP("vector", "tensor_copy", reads=[f"kvst{ksl}"], writes=["kvb"],
               kwargs=dict(out=kvb[:, g0:g0 + n, :], in_=kvst[:, ksl, 0:n * 65]))
            g0 += n
        for b in range(B):
            for half in range(2):
                t = 2 * b + half
                for par in range(2):
                    pm, pm_n = (psE, "psE") if par == 0 else (psF, "psF")
                    for m4 in range(2):
                        for mm in range(4):
                            m = m4 * 4 + mm
                            h = 2 * m + par
                            lin = (m // 4) * 32 + b * 4 + (m % 4)
                            OP("tensor", "matmul", reads=[f"qfm{h // 2}", "kvb"],
                               writes=[pm_n] if mm in (0, 3) else [],
                               kwargs=dict(out=pm[:, mm * 65:(mm + 1) * 65],
                                           lhsT=qfm[par * 64:(par + 1) * 64, h // 2, t * 128:(t + 1) * 128],
                                           rhs=kvb[par * 64:(par + 1) * 64, lin, :],
                                           start=True, stop=True,
                                           tile_position=(par * 64, 0)))
                        OP("vector", "tensor_scalar", reads=[pm_n], writes=[f"nrm{par}"],
                           kwargs=dict(out=nrm4[:, par, :],
                                       in0=pm[:, 0:260].rearrange("p (s j) -> p s j", j=65)[:, :, 64],
                                       scalar1=EPS_ATTN, scalar2=None, op0=ALU.add))
                        OP("vector", "reciprocal", reads=[f"nrm{par}"], writes=[f"nrm{par}"],
                           kwargs=dict(out=nrm4[:, par, :], in_=nrm4[:, par, :]))
                        OP("vector", "tensor_tensor", reads=[pm_n, f"nrm{par}"], writes=[f"atk{t}"],
                           kwargs=dict(out=atk[:, t, 0:1024].rearrange("p (m par2 j) -> p m par2 j", par2=2, j=64)[:, m4 * 4:(m4 + 1) * 4, par, :],
                                       in0=pm[:, 0:260].rearrange("p (s j) -> p s j", j=65)[:, :, 0:64],
                                       in1=nrm4[:, par, :, None].broadcast_to((128, 4, 64)),
                                       op=ALU.mult))
        for t in range(NT):
            transpose_tile_to_cm(f"atk{t}", atk[:, t, 0:1024], t)
        for c in range(NC_):
            OP("sync", "dma_start", writes=[f"wA{c}"], dma=True,
               kwargs=dict(out=wA[:, c, 0:C], in_=s_proj[c, :, :]))
        for t in range(NT):
            for half in range(2):
                pp, pp_n = (psA, "psA") if half == 0 else (psB, "psB")
                for c in range(NC_):
                    OP("tensor", "matmul", reads=[f"cm{c}t{t}", f"wA{c}"],
                       writes=[pp_n] if c in (0, NC_ - 1) else [],
                       kwargs=dict(out=pp[:], lhsT=cm[:, c, t * 128:(t + 1) * 128],
                                   rhs=wA[:, c, half * 512:(half + 1) * 512],
                                   start=(c == 0), stop=(c == NC_ - 1)))
                OP("vector", "tensor_tensor", reads=[pp_n, "bias_b"], writes=["phe"],
                   kwargs=dict(out=phe[:], in0=pp[:], in1=bias_b[:, half * 512:(half + 1) * 512], op=ALU.add))
                OP("vector", "tensor_tensor", reads=["phe", f"res{t}"], writes=[f"res{t}"],
                   kwargs=dict(out=res[:, t, half * 512:(half + 1) * 512],
                               in0=phe[:], in1=res[:, t, half * 512:(half + 1) * 512], op=ALU.add))
        k.barrier()
        aes.close()


    def l1ffn(inv, emit_out=False):
        V = f"g{inv}"
        J0 = inv * 4
        les0 = ExitStack()
        A = les0.enter_context(nc.sbuf_tensor(f"A{inv}", [128, 24, 2048], BF16))
        i16b = les0.enter_context(nc.sbuf_tensor(f"i16b{inv}", [128, 2, 2048], I16))

        lesa = ExitStack()
        xbt2 = lesa.enter_context(nc.sbuf_tensor(f"xbt2{inv}", [128, C], BF16))
        for t in range(NT):
            lnorm(t, True, out_bf_ap=xbt2[:], out_bf_name="xbt2", tag=V)
            OP("vector", "tensor_reduce", reads=["xlnf"], writes=[f"xm{V}{t}"],
               kwargs=dict(out=cols[:, t, 3:4], in_=xlnf[:], axis=AX.X, op=ALU.max,
                           apply_absolute_value=True))
            transpose_tile_to_cm("xbt2", xbt2[:], t)
        OP("vector", "tensor_reduce", reads=[f"xm{V}{t}" for t in range(NT)], writes=["tc8"],
           kwargs=dict(out=tiny[:, 8:9], in_=cols[:, :, 3], axis=AX.X, op=ALU.max))
        part_max("tc8", tiny[:, 8:9], NT, "tc9", tiny[0:1, 9:10], 7)
        col_allreduce([("tc9", tiny[0:1, 9:10])], J0 + 0, [("gcol6", gcols[:, 6:7])])
        OP("vector", "tensor_scalar", reads=["gcol6"], writes=["gcol7"],
           kwargs=dict(out=gcols[:, 7:8], in0=gcols[:, 6:7], scalar1=1.0 / 32767.0,
                       scalar2=1e-12, op0=ALU.mult, op1=ALU.max))
        OP("vector", "reciprocal", reads=["gcol7"], writes=["gcol8"],
           kwargs=dict(out=gcols[:, 8:9], in_=gcols[:, 7:8]))
        k.barrier()
        lesa.close()
        for c in range(NC_):
            OP("scalar", "activation", reads=[f"cm{c}t{tt_}" for tt_ in range(NT)] + ["gcol8"],
               writes=[f"i16b{c % 2}"],
               kwargs=dict(out=i16b[:, c % 2, :], in_=cm[:, c, :], func=AF.Copy, scale=gcols[:, 8:9], bias=0.0))
            OP("vector", "tensor_scalar", reads=[f"i16b{c % 2}", "gcol7"],
               writes=[f"cm{c}t{tt_}" for tt_ in range(NT)],
               kwargs=dict(out=cm[:, c, :], in0=i16b[:, c % 2, :], scalar1=gcols[:, 7:8], scalar2=None, op0=ALU.mult))

        def dump_fm(get, n=8):
            for j in range(n):
                ap, nm = get(j)
                OP("vector", "tensor_copy", reads=[nm], writes=[f"res{2 * j}"],
                   kwargs=dict(out=res[:, 2 * j, :], in_=ap[:, 0:1024]))
                OP("vector", "tensor_copy", reads=[nm], writes=[f"res{2 * j + 1}"],
                   kwargs=dict(out=res[:, 2 * j + 1, :], in_=ap[:, 1024:2048]))
        if dbg == "l1xq":
            dump_fm(lambda j: (cm[:, j, :], f"cm{j}"))
            k.barrier(); les0.close()
            return
        les1 = ExitStack()
        w1p = les1.enter_context(nc.sbuf_tensor(f"w1p{inv}", [128, 1, NC_, 128], BF16))
        sgf = les1.enter_context(nc.sbuf_tensor(f"sgf{inv}", [128, 2, 1024], F32))
        u32 = les1.enter_context(nc.sbuf_tensor(f"u32{inv}", [128, 1024], I32))
        upb = les1.enter_context(nc.sbuf_tensor(f"upb{inv}", [128, 2, 512], BF16))

        OP("vector", "memset", writes=["tc11"], kwargs=dict(ap=tiny[:, 11:12], constant=0.0))
        def ugmat(of, which, sl_ignore):
            src = s_w1g if which == "g" else s_w1u
            sl = 0
            OP("sync", "dma_start", writes=[f"w1p{sl}"], dma=True,
               kwargs=dict(out=w1p[:, sl, :, :], in_=src[of, :, :, :].rearrange("c p q -> p c q")))
            for tg in range(4):
                pq, pq_n = (psA, "psA") if tg % 2 == 0 else (psB, "psB")
                for c in range(NC_):
                    OP("tensor", "matmul",
                       reads=[f"w1p{sl}"] + [f"cm{c}t{tg * 4 + j2}" for j2 in range(4)],
                       writes=[pq_n] if c in (0, NC_ - 1) else [],
                       kwargs=dict(out=pq[:], lhsT=w1p[:, sl, c, :],
                                   rhs=cm[:, c, tg * 512:(tg + 1) * 512],
                                   start=(c == 0), stop=(c == NC_ - 1)))
                yield of, tg, pq, pq_n
        for of in range(24):
            for _, tg, pq, pq_n in ugmat(of, "g", of % 2):
                OP("vector", "tensor_copy", reads=[pq_n], writes=[f"A{of}"],
                   kwargs=dict(out=A[:, of, tg * 512:(tg + 1) * 512], in_=pq[:]))
            OP("vector", "tensor_reduce", reads=[f"A{of}"], writes=["tc10"],
               kwargs=dict(out=tiny[:, 10:11], in_=A[:, of, :], axis=AX.X, op=ALU.max,
                           apply_absolute_value=True))
            OP("vector", "tensor_tensor", reads=["tc10", "tc11"], writes=["tc11"],
               kwargs=dict(out=tiny[:, 11:12], in0=tiny[:, 10:11], in1=tiny[:, 11:12], op=ALU.max))
        if dbg == "l1gate":
            dump_fm(lambda j: (A[:, j, :], f"A{j}"))
            k.barrier(); les1.close(); les0.close()
            return
        part_max("tc11", tiny[:, 11:12], 1, "tc12", tiny[0:1, 12:13], 7)
        col_allreduce([("tc12", tiny[0:1, 12:13])], J0 + 1, [("gcol9", gcols[:, 9:10])])
        OP("vector", "tensor_scalar", reads=["gcol9"], writes=["gcol10"],
           kwargs=dict(out=gcols[:, 10:11], in0=gcols[:, 9:10], scalar1=1.0 / 32767.0,
                       scalar2=1e-12, op0=ALU.mult, op1=ALU.max))
        OP("vector", "reciprocal", reads=["gcol10"], writes=["gcol11"],
           kwargs=dict(out=gcols[:, 11:12], in_=gcols[:, 10:11]))
        OP("vector", "memset", writes=["tc13"], kwargs=dict(ap=tiny[:, 13:14], constant=0.0))
        OP("vector", "memset", writes=["tc15"], kwargs=dict(ap=tiny[:, 15:16], constant=0.0))
        for of in range(24):
            for _, tg, pq, pq_n in ugmat(of, "u", of % 2):
                OP("vector", "tensor_copy", reads=[pq_n], writes=[f"upb{tg % 2}"],
                   kwargs=dict(out=upb[:, tg % 2, :], in_=pq[:]))
                OP("vector", "tensor_reduce", reads=[pq_n], writes=["tc10"],
                   kwargs=dict(out=tiny[:, 10:11], in_=pq[:], axis=AX.X, op=ALU.max,
                               apply_absolute_value=True))
                OP("vector", "tensor_tensor", reads=["tc10", "tc13"], writes=["tc13"],
                   kwargs=dict(out=tiny[:, 13:14], in0=tiny[:, 10:11], in1=tiny[:, 13:14], op=ALU.max))
                OP("sync", "dma_start", reads=[f"upb{tg % 2}"], writes=["s_up_d"], dma=True,
                   kwargs=dict(out=s_up[of, :, tg * 512:(tg + 1) * 512], in_=upb[:, tg % 2, :]))
            ib = of % 2
            OP("scalar", "activation", reads=[f"A{of}", "gcol11"], writes=[f"i16b{ib}"],
               kwargs=dict(out=i16b[:, ib, :], in_=A[:, of, :], func=AF.Copy, scale=gcols[:, 11:12], bias=0.0))
            for hf in range(2):
                sl_ = slice(hf * 1024, (hf + 1) * 1024)
                OP("scalar", "activation", reads=[f"i16b{ib}", "gcol10"], writes=[f"sgf{hf}"],
                   kwargs=dict(out=sgf[:, hf, :], in_=i16b[:, ib, sl_], func=AF.Sigmoid, scale=gcols[:, 10:11]))
                OP("scalar", "activation", reads=[f"sgf{hf}"], writes=["u32"],
                   kwargs=dict(out=u32[:], in_=sgf[:, hf, :], func=AF.Copy, scale=65536.0, bias=0.0))
                OP("vector", "tensor_scalar", reads=["u32"], writes=[f"sgf{hf}"],
                   kwargs=dict(out=sgf[:, hf, :], in0=u32[:], scalar1=65535.0, scalar2=1.0 / 65536.0,
                               op0=ALU.min, op1=ALU.mult))
                OP("vector", "tensor_tensor", reads=[f"i16b{ib}", f"sgf{hf}"], writes=[f"A{of}"],
                   kwargs=dict(out=A[:, of, sl_], in0=i16b[:, ib, sl_], in1=sgf[:, hf, :], op=ALU.mult))
            OP("vector", "tensor_reduce", reads=[f"A{of}"], writes=["tc10"],
               kwargs=dict(out=tiny[:, 10:11], in_=A[:, of, :], axis=AX.X, op=ALU.max,
                           apply_absolute_value=True))
            OP("vector", "tensor_tensor", reads=["tc10", "tc15"], writes=["tc15"],
               kwargs=dict(out=tiny[:, 15:16], in0=tiny[:, 10:11], in1=tiny[:, 15:16], op=ALU.max))
        if dbg == "l1g2":
            dump_fm(lambda j: (A[:, j, :], f"A{j}"))
            k.barrier(); les1.close(); les0.close()
            return
        OP("vector", "tensor_tensor", reads=["tc15", "gcol10"], writes=["tc15"],
           kwargs=dict(out=tiny[:, 15:16], in0=tiny[:, 15:16], in1=gcols[:, 10:11], op=ALU.mult))
        part_max("tc13", tiny[:, 13:14], 1, "tc14", tiny[0:1, 14:15], 7)
        part_max("tc15", tiny[:, 15:16], 1, "tc15", tiny[0:1, 15:16], 7)
        col_allreduce([("tc14", tiny[0:1, 14:15]), ("tc15", tiny[0:1, 15:16])], J0 + 2,
                      [("gcol12", gcols[:, 12:13]), ("gcol13", gcols[:, 13:14])])
        OP("vector", "tensor_scalar", reads=["gcol12"], writes=["gcol12"],
           kwargs=dict(out=gcols[:, 12:13], in0=gcols[:, 12:13], scalar1=1.0 / 32767.0,
                       scalar2=1e-12, op0=ALU.mult, op1=ALU.max))
        OP("vector", "reciprocal", reads=["gcol12"], writes=["gcol14"],
           kwargs=dict(out=gcols[:, 14:15], in_=gcols[:, 12:13]))
        OP("vector", "tensor_scalar", reads=["gcol13"], writes=["gcol13"],
           kwargs=dict(out=gcols[:, 13:14], in0=gcols[:, 13:14], scalar1=1.0 / 32767.0,
                       scalar2=1e-12, op0=ALU.mult, op1=ALU.max))
        OP("vector", "reciprocal", reads=["gcol13"], writes=["gcol15"],
           kwargs=dict(out=gcols[:, 15:16], in_=gcols[:, 13:14]))
        OP("vector", "tensor_tensor", reads=["gcol15", "gcol10"], writes=["gcol15"],
           kwargs=dict(out=gcols[:, 15:16], in0=gcols[:, 15:16], in1=gcols[:, 10:11], op=ALU.mult))
        k.barrier()
        les1.close()
        les2 = ExitStack()
        upf = les2.enter_context(nc.sbuf_tensor(f"upf{inv}", [128, 2, 2048], BF16))
        i16c = les2.enter_context(nc.sbuf_tensor(f"i16c{inv}", [128, 2, 2048], I16))
        OP("vector", "memset", writes=["tc13"], kwargs=dict(ap=tiny[:, 13:14], constant=0.0))
        for of in range(24):
            sl = of % 2
            OP("sync", "dma_start", reads=["s_up_d"], writes=[f"upf{sl}"], dma=True,
               kwargs=dict(out=upf[:, sl, :], in_=s_up[of, :, :]))
            ib = of % 2
            OP("scalar", "activation", reads=[f"upf{sl}", "gcol14"], writes=[f"i16c{ib}"],
               kwargs=dict(out=i16c[:, ib, :], in_=upf[:, sl, :], func=AF.Copy, scale=gcols[:, 14:15], bias=0.0))
            OP("scalar", "activation", reads=[f"A{of}", "gcol15"], writes=[f"i16b{ib}"],
               kwargs=dict(out=i16b[:, ib, :], in_=A[:, of, :], func=AF.Copy, scale=gcols[:, 15:16], bias=0.0))
            OP("vector", "tensor_tensor", reads=[f"i16b{ib}", f"i16c{ib}"], writes=[f"A{of}"],
               kwargs=dict(out=A[:, of, :], in0=i16b[:, ib, :], in1=i16c[:, ib, :], op=ALU.mult))
            OP("vector", "tensor_reduce", reads=[f"A{of}"], writes=["tc10"],
               kwargs=dict(out=tiny[:, 10:11], in_=A[:, of, :], axis=AX.X, op=ALU.max,
                           apply_absolute_value=True))
            OP("vector", "tensor_tensor", reads=["tc10", "tc13"], writes=["tc13"],
               kwargs=dict(out=tiny[:, 13:14], in0=tiny[:, 10:11], in1=tiny[:, 13:14], op=ALU.max))
        if dbg == "l1p":
            dump_fm(lambda j: (A[:, j, :], f"A{j}"))
            k.barrier(); les2.close(); les1.close() if False else None; les0.close()
            return
        OP("vector", "tensor_tensor", reads=["tc13", "gcol13"], writes=["tc13"],
           kwargs=dict(out=tiny[:, 13:14], in0=tiny[:, 13:14], in1=gcols[:, 13:14], op=ALU.mult))
        OP("vector", "tensor_tensor", reads=["tc13", "gcol12"], writes=["tc13"],
           kwargs=dict(out=tiny[:, 13:14], in0=tiny[:, 13:14], in1=gcols[:, 12:13], op=ALU.mult))
        part_max("tc13", tiny[:, 13:14], 1, "tc14", tiny[0:1, 14:15], 7)
        col_allreduce([("tc14", tiny[0:1, 14:15])], J0 + 3, [("gcol9", gcols[:, 9:10])])
        OP("vector", "tensor_scalar", reads=["gcol9"], writes=["gcol9"],
           kwargs=dict(out=gcols[:, 9:10], in0=gcols[:, 9:10], scalar1=1.0 / 32767.0,
                       scalar2=1e-12, op0=ALU.mult, op1=ALU.max))
        OP("vector", "reciprocal", reads=["gcol9"], writes=["gcol10"],
           kwargs=dict(out=gcols[:, 10:11], in_=gcols[:, 9:10]))
        OP("vector", "tensor_tensor", reads=["gcol10", "gcol13"], writes=["gcol10"],
           kwargs=dict(out=gcols[:, 10:11], in0=gcols[:, 10:11], in1=gcols[:, 13:14], op=ALU.mult))
        OP("vector", "tensor_tensor", reads=["gcol10", "gcol12"], writes=["gcol10"],
           kwargs=dict(out=gcols[:, 10:11], in0=gcols[:, 10:11], in1=gcols[:, 12:13], op=ALU.mult))
        OP("vector", "tensor_scalar", reads=["gcol9"], writes=["gcol11"],
           kwargs=dict(out=gcols[:, 11:12], in0=gcols[:, 9:10], scalar1=0.5, scalar2=None, op0=ALU.mult))
        if dbg == "l1oq":
            dump_fm(lambda j: (A[:, j, :], f"A{j}"))
            k.barrier(); les2.close(); les0.close()
            return
        k.barrier()
        les2.close()
        les3 = ExitStack()
        w1dg = les3.enter_context(nc.sbuf_tensor(f"w1dg{inv}", [128, 4, C], BF16))
        for ocg in range(6):
            for j in range(4):
                of = ocg * 4 + j
                OP("scalar", "activation", reads=[f"A{of}", "gcol10"], writes=[f"i16b{of % 2}"],
                   kwargs=dict(out=i16b[:, of % 2, :], in_=A[:, of, :], func=AF.Copy, scale=gcols[:, 10:11], bias=0.0))
                OP("gpsimd", "tensor_copy", reads=[f"i16b{of % 2}"], writes=[f"A{of}"],
                   kwargs=dict(out=A[:, of, :], in_=i16b[:, of % 2, :]))
                OP("sync", "dma_start", writes=[f"w1dg{j}"], dma=True,
                   kwargs=dict(out=w1dg[:, j, :], in_=s_w1d[ocg * 4 + j, :, :]))
            for t in range(NT):
                for half in range(2):
                    pd, pd_n = (psE, "psE") if half == 0 else (psF, "psF")
                    for j in range(4):
                        oc = ocg * 4 + j
                        OP("tensor", "matmul", reads=[f"A{oc}", f"w1dg{j}"],
                           writes=[pd_n] if j in (0, 3) else [],
                           kwargs=dict(out=pd[:], lhsT=A[:, oc, t * 128:(t + 1) * 128],
                                       rhs=w1dg[:, j, half * 512:(half + 1) * 512],
                                       start=(j == 0), stop=(j == 3)))
                    OP("vector", "scalar_tensor_tensor", reads=[pd_n, "gcol11", f"res{t}"],
                       writes=[f"res{t}"],
                       kwargs=dict(out=res[:, t, half * 512:(half + 1) * 512], in0=pd[:],
                                   scalar=gcols[:, 11:12], in1=res[:, t, half * 512:(half + 1) * 512],
                                   op0=ALU.mult, op1=ALU.add))
                if emit_out and ocg == 5:
                    b_, half_ = t // 2, t % 2
                    OP("sync", "dma_start", reads=[f"res{t}"], writes=["y_d"], dma=True,
                       kwargs=dict(out=y_h[b_, half_ * 128:(half_ + 1) * 128, :], in_=res[:, t, :]))
        k.barrier()
        les3.close()
        les0.close()

    def l1attn():
        ses = ExitStack()
        ut = ses.enter_context(nc.sbuf_tensor("ut", [128, NT, C], BF16))
        selmat = ses.enter_context(nc.sbuf_tensor("selmat", [8, 1024], F32))
        isel = ses.enter_context(nc.sbuf_tensor("isel", [8, 1024], I32))
        OP("gpsimd", "iota", writes=["isel"], kwargs=dict(out=isel[:], pattern=[[1, 1024]], base=0, channel_multiplier=-128))
        OP("gpsimd", "tensor_scalar", reads=["isel"], writes=["self0"],
           kwargs=dict(out=selmat[:], in0=isel[:], scalar1=0, scalar2=None, op0=ALU.is_ge))
        OP("gpsimd", "tensor_scalar", reads=["isel"], writes=["self1"],
           kwargs=dict(out=isel[:], in0=isel[:], scalar1=128, scalar2=None, op0=ALU.is_lt))
        OP("gpsimd", "tensor_tensor", reads=["self0", "self1"], writes=["selmat"],
           kwargs=dict(out=selmat[:], in0=selmat[:], in1=isel[:], op=ALU.mult))
        Kb = ses.enter_context(nc.sbuf_tensor("Kb", [128, 8, C], F32))
        Db = ses.enter_context(nc.sbuf_tensor("Db", [128, C], F32))
        accV = ses.enter_context(nc.sbuf_tensor("accV", [128, C], F32))
        tmpV = ses.enter_context(nc.sbuf_tensor("tmpV", [128, C], F32))
        accG = ses.enter_context(nc.sbuf_tensor("accG", [128, C], F32))
        tmpG = ses.enter_context(nc.sbuf_tensor("tmpG", [128, C], F32))
        Krow = ses.enter_context(nc.sbuf_tensor("Krow", [8, 1024], F32))
        sN = 16 * NDIM // 16
        ss = ses.enter_context(nc.sbuf_tensor("ssm", [128, 16, NC_, NDIM], F32))
        SS = lambda i: ss[:, i, :, :]

        def ld_param(dst_i, src, ncols, name):
            OP("sync", "dma_start", writes=[name], dma=True,
               kwargs=dict(out=ss[:, dst_i, :, 0:ncols],
                           in_=src.rearrange("(c p) n -> p c n", p=128),
                           allow_slow_non_contiguous=True))

        ld_param(0, ldt_h, 1, "ssm0")
        ld_param(1, lar_h, NDIM, "ssm1")
        ld_param(2, aim_h, NDIM, "ssm2")
        OP("sync", "dma_start", writes=["ssm3"], dma=True,
           kwargs=dict(out=ss[:, 3, :, :], in_=bc_h[0].rearrange("(c p) n -> p c n", p=128)))
        OP("sync", "dma_start", writes=["ssm4"], dma=True,
           kwargs=dict(out=ss[:, 4, :, :], in_=bc_h[1].rearrange("(c p) n -> p c n", p=128)))
        OP("sync", "dma_start", writes=["ssm5"], dma=True,
           kwargs=dict(out=ss[:, 5, :, :], in_=cc_h[0].rearrange("(c p) n -> p c n", p=128)))
        OP("sync", "dma_start", writes=["ssm6"], dma=True,
           kwargs=dict(out=ss[:, 6, :, :], in_=cc_h[1].rearrange("(c p) n -> p c n", p=128)))
        OP("sync", "dma_start", writes=["Db"], dma=True,
           kwargs=dict(out=Db[:], in_=dd_h[0:1, :].broadcast_to((128, C))))
        def tt(o, a, b, op, ro=None, rn=None):
            OP("vector", "tensor_tensor", reads=[f"ssm{a}", f"ssm{b}"] + (ro or []), writes=[f"ssm{o}"] + (rn or []),
               kwargs=dict(out=SS(o), in0=SS(a), in1=SS(b), op=op))
        def tsc(o, a, s1, s2, op0, op1=None):
            OP("vector", "tensor_scalar", reads=[f"ssm{a}"], writes=[f"ssm{o}"],
               kwargs=dict(out=SS(o), in0=SS(a), scalar1=s1, scalar2=s2, op0=op0,
                           **({"op1": op1} if op1 else {})))
        OP("scalar", "activation", reads=["ssm0"], writes=["ssm7"],
           kwargs=dict(out=ss[:, 7, :, 0:1], in_=ss[:, 0, :, 0:1], func=AF.Exp))
        OP("scalar", "activation", reads=["ssm1"], writes=["ssm8"],
           kwargs=dict(out=SS(8), in_=SS(1), func=AF.Exp))
        OP("vector", "tensor_tensor", reads=["ssm8", "ssm7"], writes=["ssm8"],
           kwargs=dict(out=SS(8), in0=SS(8), in1=ss[:, 7, :, 0:1].broadcast_to((128, NC_, NDIM)), op=ALU.mult))
        OP("vector", "tensor_tensor", reads=["ssm2", "ssm7"], writes=["ssm9"],
           kwargs=dict(out=SS(9), in0=SS(2), in1=ss[:, 7, :, 0:1].broadcast_to((128, NC_, NDIM)), op=ALU.mult))
        tsc(10, 8, 0.5, 1.0, ALU.mult, ALU.add)
        tsc(11, 9, -0.5, None, ALU.mult)
        OP("scalar", "activation", reads=["ssm10"], writes=["ssm12"],
           kwargs=dict(out=SS(12), in_=SS(10), func=AF.Square))
        OP("scalar", "activation", reads=["ssm11"], writes=["ssm13"],
           kwargs=dict(out=SS(13), in_=SS(11), func=AF.Square))
        tt(12, 12, 13, ALU.add)
        OP("vector", "reciprocal", reads=["ssm12"], writes=["ssm14"],
           kwargs=dict(out=SS(14), in_=SS(12)))
        tsc(12, 8, -0.5, 1.0, ALU.mult, ALU.add)
        tsc(13, 9, 0.5, None, ALU.mult)
        tt(15, 12, 10, ALU.mult)
        tt(8, 13, 11, ALU.mult, ro=["ssm8"])
        tt(8, 15, 8, ALU.add)
        tt(8, 8, 14, ALU.mult)
        tt(15, 13, 10, ALU.mult)
        tt(9, 12, 11, ALU.mult, ro=["ssm9"])
        tt(9, 15, 9, ALU.subtract)
        tt(9, 9, 14, ALU.mult)
        tt(15, 5, 3, ALU.mult)
        tt(12, 6, 4, ALU.mult)
        tt(12, 15, 12, ALU.subtract)
        tt(15, 5, 4, ALU.mult)
        tt(13, 6, 3, ALU.mult)
        tt(13, 15, 13, ALU.add)
        OP("vector", "tensor_tensor", reads=["ssm12", "ssm7"], writes=["ssm12"],
           kwargs=dict(out=SS(12), in0=SS(12), in1=ss[:, 7, :, 0:1].broadcast_to((128, NC_, NDIM)), op=ALU.mult))
        OP("vector", "tensor_tensor", reads=["ssm13", "ssm7"], writes=["ssm13"],
           kwargs=dict(out=SS(13), in0=SS(13), in1=ss[:, 7, :, 0:1].broadcast_to((128, NC_, NDIM)), op=ALU.mult))
        tt(15, 12, 10, ALU.mult)
        tt(5, 13, 11, ALU.mult, ro=["ssm5"])
        tt(5, 15, 5, ALU.add)
        tt(5, 5, 14, ALU.mult)
        tt(15, 13, 10, ALU.mult)
        tt(6, 12, 11, ALU.mult, ro=["ssm6"])
        tt(6, 15, 6, ALU.subtract)
        tt(6, 6, 14, ALU.mult)
        tsc(5, 5, 2.0 * SCALE_SSM, None, ALU.mult)
        tsc(6, 6, 2.0 * SCALE_SSM, None, ALU.mult)
        OP("vector", "memset", writes=["ssm10"], kwargs=dict(ap=SS(10), constant=1.0))
        OP("vector", "memset", writes=["ssm11"], kwargs=dict(ap=SS(11), constant=0.0))
        Kc = ses.enter_context(nc.sbuf_tensor("Kc", [128, NC_, 8], F32))
        for l in range(8):
            tt(15, 5, 10, ALU.mult)
            tt(12, 6, 11, ALU.mult)
            tt(12, 15, 12, ALU.subtract)
            OP("vector", "tensor_reduce", reads=["ssm12"], writes=[f"Kc{l}"],
               kwargs=dict(out=Kc[:, :, l], in_=SS(12), axis=AX.X, op=ALU.add))
            if l < 7:
                tt(15, 10, 8, ALU.mult)
                tt(12, 11, 9, ALU.mult)
                tt(13, 10, 9, ALU.mult)
                tt(10, 11, 8, ALU.mult, ro=["ssm10"])
                tt(11, 13, 10, ALU.add)
                tt(10, 15, 12, ALU.subtract)
        for c in range(NC_):
            OP("tensor", "transpose", reads=[f"Kc{l}" for l in range(8)] + ["identf"], writes=["psF"],
               kwargs=dict(out=psF[0:8, 0:128], in_=Kc[:, c, :], identity=ident_f[:]))
            OP("vector", "tensor_copy", reads=["psF"], writes=["Krow"],
               kwargs=dict(out=Krow[:, c * 128:(c + 1) * 128], in_=psF[0:8, 0:128]))
        for tau in range(8):
            for half in range(2):
                pq, pq_n = (psA, "psA") if half == 0 else (psB, "psB")
                OP("tensor", "matmul", reads=["selmat", "Krow"], writes=[pq_n],
                   kwargs=dict(out=pq[:], lhsT=selmat[:, tau * 128:(tau + 1) * 128],
                               rhs=Krow[:, half * 512:(half + 1) * 512], start=True, stop=True))
                OP("vector", "tensor_copy", reads=[pq_n], writes=[f"Kb{tau}"],
                   kwargs=dict(out=Kb[:, tau, half * 512:(half + 1) * 512], in_=pq[:]))
        OP("vector", "tensor_tensor", reads=["Kb0", "Db"], writes=["Kb0"],
           kwargs=dict(out=Kb[:, 0, :], in0=Kb[:, 0, :], in1=Db[:], op=ALU.add))
        for t in range(NT):
            lnorm(t, False, out_bf_ap=ut[:, t, :], out_bf_name=f"ut{t}", tag="s")
        for t in range(NT):
            b, half = t // 2, t % 2
            eng = "vector" if t % 2 == 0 else "gpsimd"
            acc, accn = (accV, "accV") if eng == "vector" else (accG, "accG")
            tmp, tmpn = (tmpV, "tmpV") if eng == "vector" else (tmpG, "tmpG")
            OP(eng, "tensor_tensor", reads=[f"ut{t}", "Kb0"], writes=[accn],
               kwargs=dict(out=acc[:], in0=ut[:, t, :], in1=Kb[:, 0, :], op=ALU.mult))
            for tau in range(1, b + 1):
                ts_ = 2 * (b - tau) + half
                OP(eng, "tensor_tensor", reads=[f"ut{ts_}", f"Kb{tau}"], writes=[tmpn],
                   kwargs=dict(out=tmp[:], in0=ut[:, ts_, :], in1=Kb[:, tau, :], op=ALU.mult))
                OP(eng, "tensor_tensor", reads=[tmpn, accn], writes=[accn],
                   kwargs=dict(out=acc[:], in0=tmp[:], in1=acc[:], op=ALU.add))
            OP(eng, "tensor_tensor", reads=[accn, f"res{t}"], writes=[f"res{t}"],
               kwargs=dict(out=res[:, t, :], in0=acc[:], in1=res[:, t, :], op=ALU.add))
        k.barrier()
        ses.close()

    if n_sub >= 1:
        l0ffn(0)
    if n_sub >= 2:
        l0attn()
    if n_sub >= 3:
        l0ffn(1)
    if n_sub >= 4:
        l1ffn(0)
    if n_sub >= 5:
        l1attn()
    if n_sub >= 6:
        l1ffn(1, emit_out=True)

    if n_sub < 6:
        for t in range(NT):
            b, half = t // 2, t % 2
            OP("sync", "dma_start", reads=[f"res{t}"], writes=["y_d"], dma=True,
               kwargs=dict(out=y_h[b, half * 128:(half + 1) * 128, :], in_=res[:, t, :]))
    for i in range(N_DMA_SEMS):
        k._wait("sync", f"d_sync{i}", k.cnt[f"d_sync{i}"])

    block = es.enter_context(nc.Block())
    k.finalize(block)
    es.close()
    return nc


def host_inputs(inputs, core):
    x = np.ascontiguousarray(inputs["L0_attn_x"][:, core * TPC:(core + 1) * TPC, :], dtype=np.float32)
    T32 = lambda a: np.ascontiguousarray(np.asarray(a, dtype=np.float32).T)
    C32 = lambda a: np.ascontiguousarray(np.asarray(a, dtype=np.float32))
    return {
        "x": x,
        "w0g": T32(inputs["L0_ffn_w_gate"]),
        "w0u": T32(inputs["L0_ffn_w_up"]),
        "w0d": T32(inputs["L0_ffn_w_down"]),
        "wqkv": T32(inputs["L0_attn_Wqkv"]),
        "wproj": T32(inputs["L0_attn_Wproj"]),
        "bproj": C32(inputs["L0_attn_bproj"]).reshape(1, C),
        "w1g": T32(inputs["L1_ffn_w_gate"]),
        "w1u": T32(inputs["L1_ffn_w_up"]),
        "w1d": T32(inputs["L1_ffn_w_down"]),
        "ldt": C32(inputs["L1_attn_log_dt"]).reshape(C, 1),
        "lar": C32(inputs["L1_attn_log_A_real"]),
        "aim": C32(inputs["L1_attn_A_imag"]),
        "bc": np.ascontiguousarray(np.asarray(inputs["L1_attn_B"], dtype=np.float32).transpose(2, 0, 1)),
        "ccp": np.ascontiguousarray(np.asarray(inputs["L1_attn_C"], dtype=np.float32).transpose(2, 0, 1)),
        "dd": C32(inputs["L1_attn_D"]).reshape(1, C),
    }


def run(inputs, n_sub=6):
    nc = build_program(n_sub)
    in_maps = [host_inputs(inputs, i) for i in range(NCORES)]
    res_ = run_bass_kernel_spmd(nc, in_maps, core_ids=list(range(NCORES)))
    return np.concatenate([r["y"] for r in res_.results], axis=1)


def kernel(**inputs):
    nc = build_program(6)
    in_maps = [host_inputs(inputs, i) for i in range(NCORES)]
    res_ = run_bass_kernel_spmd(nc, in_maps, core_ids=list(range(NCORES)))
    return np.concatenate([r["y"] for r in res_.results], axis=1)


def kernel(**inputs):
    nc = build_program(6)
    in_maps = [host_inputs(inputs, i) for i in range(NCORES)]
    res_ = run_bass_kernel_spmd(nc, in_maps, core_ids=list(range(NCORES)))
    return np.concatenate([r["y"] for r in res_.results], axis=1)


def kernel(**inputs):
    nc = build_program(6)
    in_maps = [host_inputs(inputs, i) for i in range(NCORES)]
    res_ = run_bass_kernel_spmd(nc, in_maps, core_ids=list(range(NCORES)))
    return np.concatenate([r["y"] for r in res_.results], axis=1)


def kernel(**inputs):
    nc = build_program(6)
    in_maps = [host_inputs(inputs, i) for i in range(NCORES)]
    res_ = run_bass_kernel_spmd(nc, in_maps, core_ids=list(range(NCORES)))
    return np.concatenate([r["y"] for r in res_.results], axis=1)


# version 18
# speedup vs baseline: 1.0154x; 1.0082x over previous
import numpy as np
from contextlib import ExitStack

import concourse.bass as bass
import concourse.mybir as mybir
from concourse.bass_utils import run_bass_kernel_spmd

F32 = mybir.dt.float32
BF16 = mybir.dt.bfloat16
FP8 = mybir.dt.float8e4
I8 = mybir.dt.int8
I16 = mybir.dt.int16
I32 = mybir.dt.int32
AF = mybir.ActivationFunctionType
ALU = mybir.AluOpType
AX = mybir.AxisListType

B, T, C = 8, 2048, 1024
NCORES = 8
TPC = T // NCORES
ROWS = B * TPC
NT = ROWS // 128
NC_ = C // 128
EPS_LN = 1e-5
EPS_ATTN = 1e-6
H, D = 16, 64
I0 = 4096
I1 = 3072
NDIM = 16
SCALE_SSM = float(np.sqrt(1.0 / NDIM))

N_DMA_SEMS = 6


class Buf:
    __slots__ = ("name", "writers", "readers")
    def __init__(self, name):
        self.name = name
        self.writers = {}
        self.readers = {}


class Builder:
    def __init__(self, nc, es):
        self.nc = nc
        self.es = es
        self.recorded = {"tensor": [], "vector": [], "scalar": [], "gpsimd": [], "sync": []}
        self.sems = {}
        self.cnt = {}
        for e in ("tensor", "vector", "scalar", "gpsimd"):
            self._new_sem("e_" + e)
        for q in ("sync", "scalar", "gpsimd"):
            for i in range(N_DMA_SEMS):
                self._new_sem(f"d_{q}{i}")
        self._new_sem("cc")
        self.dma_rr = {"sync": 0, "scalar": 0, "gpsimd": 0}
        self.waited = {e: {} for e in self.recorded}
        self.bufs = {}

    def _new_sem(self, name):
        self.sems[name] = self.es.enter_context(self.nc.semaphore(name))
        self.cnt[name] = 0

    def buf(self, name):
        if name not in self.bufs:
            self.bufs[name] = Buf(name)
        return self.bufs[name]

    def _wait(self, eng, sem_name, val):
        if val <= 0:
            return
        if eng == "tensor" and sem_name == "e_tensor":
            return
        if self.waited[eng].get(sem_name, 0) >= val:
            return
        self.waited[eng][sem_name] = val
        self.recorded[eng].append(("wait", sem_name, val))

    def op(self, eng, meth, reads=(), writes=(), dma=False, args=(), kwargs=None):
        kwargs = kwargs or {}
        rd = [self.buf(r) for r in reads]
        wr = [self.buf(w) for w in writes]
        for b in rd:
            for s, v in b.writers.items():
                self._wait(eng, s, v)
        for b in wr:
            for s, v in b.writers.items():
                self._wait(eng, s, v)
            for s, v in b.readers.items():
                self._wait(eng, s, v)
        if dma:
            q = eng if eng in ("sync", "scalar", "gpsimd") else None
            assert q is not None, f"engine {eng} cannot DMA"
            i = self.dma_rr[q]
            self.dma_rr[q] = (i + 1) % N_DMA_SEMS
            sem_name = f"d_{q}{i}"
            inc = 16
            self._wait(eng, sem_name, self.cnt[sem_name])
        else:
            sem_name = "e_" + eng
            inc = 1
        self.cnt[sem_name] += inc
        tok = (sem_name, self.cnt[sem_name])
        self.recorded[eng].append(("op", meth, args, kwargs, sem_name, inc))
        for b in wr:
            if b.writers.get(tok[0], 0) < tok[1]:
                b.writers[tok[0]] = tok[1]
        for b in rd:
            if b.readers.get(tok[0], 0) < tok[1]:
                b.readers[tok[0]] = tok[1]
        return tok

    def barrier(self):
        snap = {s: v for s, v in self.cnt.items() if v > 0}
        for eng in self.recorded:
            for s, v in snap.items():
                if eng == "tensor" and s == "e_tensor":
                    continue
                self._wait(eng, s, v)

    def finalize(self, block):
        sems = self.sems
        def run(eng_obj, eng_name):
            for item in self.recorded[eng_name]:
                if item[0] == "wait":
                    eng_obj.wait_ge(sems[item[1]], item[2])
                else:
                    _, meth, args, kwargs, sem_name, inc = item
                    f = getattr(eng_obj, meth)
                    try:
                        r = f(*args, **kwargs)
                    except Exception:
                        import sys
                        print(f"FAILED OP: {eng_name}.{meth} kwargs keys: "
                              f"{ {kk: getattr(vv, 'shape', vv) if not isinstance(vv, (int, float, str, bool, type(None))) else vv for kk, vv in kwargs.items()} }",
                              file=sys.stderr)
                        raise
                    r.then_inc(sems[sem_name], inc)
        @block.tensor
        def _(t): run(t, "tensor")
        @block.vector
        def _(v): run(v, "vector")
        @block.scalar
        def _(s): run(s, "scalar")
        @block.gpsimd
        def _(g): run(g, "gpsimd")
        @block.sync
        def _(s): run(s, "sync")




def build_program(n_sub=6, dbg=None):
    nc = bass.Bass("TRN2", target_bir_lowering=False, dynamic_dma_scratch_size=512)
    es = ExitStack()

    x_h = nc.dram_tensor("x", [B, TPC, C], F32, kind="ExternalInput").ap()
    y_h = nc.dram_tensor("y", [B, TPC, C], F32, kind="ExternalOutput").ap()
    w0g_h = nc.dram_tensor("w0g", [C, I0], F32, kind="ExternalInput").ap()
    w0u_h = nc.dram_tensor("w0u", [C, I0], F32, kind="ExternalInput").ap()
    w0d_h = nc.dram_tensor("w0d", [I0, C], F32, kind="ExternalInput").ap()
    wqkv_h = nc.dram_tensor("wqkv", [C, 3 * C], F32, kind="ExternalInput").ap()
    wproj_h = nc.dram_tensor("wproj", [C, C], F32, kind="ExternalInput").ap()
    bproj_h = nc.dram_tensor("bproj", [1, C], F32, kind="ExternalInput").ap()
    w1g_h = nc.dram_tensor("w1g", [C, I1], F32, kind="ExternalInput").ap()
    w1u_h = nc.dram_tensor("w1u", [C, I1], F32, kind="ExternalInput").ap()
    w1d_h = nc.dram_tensor("w1d", [I1, C], F32, kind="ExternalInput").ap()
    ldt_h = nc.dram_tensor("ldt", [C, 1], F32, kind="ExternalInput").ap()
    lar_h = nc.dram_tensor("lar", [C, NDIM], F32, kind="ExternalInput").ap()
    aim_h = nc.dram_tensor("aim", [C, NDIM], F32, kind="ExternalInput").ap()
    bc_h = nc.dram_tensor("bc", [2, C, NDIM], F32, kind="ExternalInput").ap()
    cc_h = nc.dram_tensor("ccp", [2, C, NDIM], F32, kind="ExternalInput").ap()
    dd_h = nc.dram_tensor("dd", [1, C], F32, kind="ExternalInput").ap()

    s_w0g = nc.dram_tensor("s_w0g", [NC_, 128, I0], FP8).ap()
    s_w0u = nc.dram_tensor("s_w0u", [NC_, 128, I0], FP8).ap()
    s_w0d = nc.dram_tensor("s_w0d", [I0 // 128, 128, C], FP8).ap()
    s_w1g = nc.dram_tensor("s_w1g", [I1 // 128, NC_, 128, 128], BF16).ap()
    s_w1u = nc.dram_tensor("s_w1u", [I1 // 128, NC_, 128, 128], BF16).ap()
    s_w1d = nc.dram_tensor("s_w1d", [I1 // 128, 128, C], BF16).ap()
    s_qkv = nc.dram_tensor("s_qkv", [NC_, 128, 3 * C], BF16).ap()
    s_proj = nc.dram_tensor("s_proj", [NC_, 128, C], BF16).ap()
    s_up = nc.dram_tensor("s_up", [I1 // 128, 128, ROWS], BF16).ap()
    kv_in = nc.dram_tensor("kv_in", [128, 65 * 64], F32).ap()
    kv_out = nc.dram_tensor("kv_out", [128, 65 * 64], F32).ap()
    NMX = 12
    mxi = [nc.dram_tensor(f"mx_in{j}", [4, 1], F32).ap() for j in range(NMX)]
    mxo = [nc.dram_tensor(f"mx_out{j}", [4, 1], F32).ap() for j in range(NMX)]

    sbp = lambda n, s, d=F32: es.enter_context(nc.sbuf_tensor(n, s, d))
    res = sbp("res", [128, NT, C])
    cm = es.enter_context(nc.sbuf_tensor("cm", [128, NC_, ROWS], BF16))
    xlnf = sbp("xlnf", [128, C])
    ident = es.enter_context(nc.sbuf_tensor("ident", [128, 128], BF16))
    ident_f = sbp("identf", [128, 128])
    ones1 = sbp("ones1", [1, 128])
    cols = sbp("cols", [128, NT, 12])
    gcols = sbp("gcols", [128, 16])
    tiny = sbp("tiny", [128, 16])
    bnst = sbp("bnst", [128, 12])
    cst = sbp("cst", [128, 1])
    psA = es.enter_context(nc.psum_tensor("psA", [128, 512], F32))
    psB = es.enter_context(nc.psum_tensor("psB", [128, 512], F32))
    psC = es.enter_context(nc.psum_tensor("psC", [128, 512], F32))
    psD = es.enter_context(nc.psum_tensor("psD", [128, 512], F32))
    psT0 = es.enter_context(nc.psum_tensor("psT0", [128, 1024], BF16))
    psT1 = es.enter_context(nc.psum_tensor("psT1", [128, 1024], BF16))
    psE = es.enter_context(nc.psum_tensor("psE", [128, 512], F32))
    psF = es.enter_context(nc.psum_tensor("psF", [128, 512], F32))

    k = Builder(nc, es)
    OP = k.op

    def scalar_bcast(name_src, src_ap, dst_name, dst_ap):
        OP("tensor", "matmul", reads=["ones1", name_src], writes=["psF"],
           kwargs=dict(out=psF[:, 0:1], lhsT=ones1[:], rhs=src_ap, start=True, stop=True))
        OP("vector", "tensor_copy", reads=["psF"], writes=[dst_name], kwargs=dict(out=dst_ap, in_=psF[:, 0:1]))

    def col_allreduce(vals, j, out_cols):
        assert len(vals) <= 4 and len(out_cols) == len(vals)
        for i, (nm, ap) in enumerate(vals):
            OP("sync", "dma_start", reads=[nm], writes=[f"mxi{j}"], dma=True,
               kwargs=dict(out=mxi[j][i:i + 1, 0:1], in_=ap))
        OP("gpsimd", "collective_compute", reads=[f"mxi{j}"], writes=[f"mxo{j}"],
           kwargs=dict(kind="AllReduce", op=ALU.max, replica_groups=[list(range(NCORES))],
                       ins=[mxi[j][:]], outs=[mxo[j][:]]))
        for i, (nm, ap) in enumerate(out_cols):
            OP("sync", "dma_start", reads=[f"mxo{j}"], writes=[nm], dma=True,
               kwargs=dict(out=ap, in_=mxo[j][i:i + 1, 0:1].broadcast_to((128, 1))))

    def part_max(src_name, src_ap, nfree, dst_name, dst_ap, ctag):
        OP("vector", "tensor_reduce", reads=[src_name], writes=[f"tc{ctag}"],
           kwargs=dict(out=tiny[:, ctag:ctag + 1], in_=src_ap, axis=AX.X, op=ALU.max))
        OP("tensor", "transpose", reads=[f"tc{ctag}", "identf"], writes=["psF"],
           kwargs=dict(out=psF[0:1, 0:128], in_=tiny[:, ctag:ctag + 1], identity=ident_f[:]))
        OP("vector", "tensor_reduce", reads=["psF"], writes=[dst_name],
           kwargs=dict(out=dst_ap, in_=psF[0:1, 0:128], axis=AX.X, op=ALU.max))

    def lnorm(t, out_f32, out_bf_ap=None, out_bf_name=None, tag=""):
        OP("vector", "bn_stats", reads=[f"res{t}"], writes=["bnsA"],
           kwargs=dict(out=bnst[:, 0:6], in_=res[:, t, 0:512]))
        OP("vector", "bn_stats", reads=[f"res{t}"], writes=["bnsB"],
           kwargs=dict(out=bnst[:, 6:12], in_=res[:, t, 512:1024]))
        OP("vector", "bn_aggr", reads=["bnsA", "bnsB"], writes=[f"mv{tag}{t}"],
           kwargs=dict(out=cols[:, t, 0:2], in_=bnst[:]))
        OP("scalar", "activation", reads=[f"mv{tag}{t}"], writes=[f"sd{tag}{t}"],
           kwargs=dict(out=cols[:, t, 2:3], in_=cols[:, t, 1:2], func=AF.Sqrt, bias=EPS_LN))
        OP("vector", "reciprocal", reads=[f"sd{tag}{t}"], writes=[f"rstd{tag}{t}"],
           kwargs=dict(out=cols[:, t, 2:3], in_=cols[:, t, 2:3]))
        if out_f32:
            OP("vector", "tensor_scalar", reads=[f"res{t}", f"mv{tag}{t}", f"rstd{tag}{t}"],
               writes=["xlnf"],
               kwargs=dict(out=xlnf[:], in0=res[:, t, :], scalar1=cols[:, t, 0:1],
                           scalar2=cols[:, t, 2:3], op0=ALU.subtract, op1=ALU.mult))
        if out_bf_ap is not None:
            OP("vector", "tensor_scalar", reads=[f"res{t}", f"mv{tag}{t}", f"rstd{tag}{t}"],
               writes=[out_bf_name],
               kwargs=dict(out=out_bf_ap, in0=res[:, t, :], scalar1=cols[:, t, 0:1],
                           scalar2=cols[:, t, 2:3], op0=ALU.subtract, op1=ALU.mult))

    def s1_tile(t, V, bufs):
        nf32b, ntokb, nbtb = bufs
        sl = t % 2
        lnorm(t, True, tag=V)
        OP("vector", "tensor_reduce", reads=["xlnf"], writes=[f"mx{V}{t}"],
           kwargs=dict(out=cols[:, t, 3:4], in_=xlnf[:], axis=AX.X, op=ALU.max,
                       apply_absolute_value=True))
        OP("vector", "tensor_scalar", reads=[f"mx{V}{t}"], writes=[f"q{V}{t}"],
           kwargs=dict(out=cols[:, t, 4:5], in0=cols[:, t, 3:4], scalar1=1.0 / 127.0,
                       scalar2=1e-5 / 127.0, op0=ALU.mult, op1=ALU.max))
        OP("vector", "reciprocal", reads=[f"q{V}{t}"], writes=[f"rs{V}{t}"],
           kwargs=dict(out=cols[:, t, 5:6], in_=cols[:, t, 4:5]))
        OP("vector", "tensor_tensor", reads=[f"q{V}{t}", "mcol_g"], writes=[f"scg{V}{t}"],
           kwargs=dict(out=cols[:, t, 6:7], in0=cols[:, t, 4:5], in1=gcols[:, 0:1], op=ALU.mult))
        OP("vector", "tensor_tensor", reads=[f"q{V}{t}", "mcol_u"], writes=[f"scu{V}{t}"],
           kwargs=dict(out=cols[:, t, 7:8], in0=cols[:, t, 4:5], in1=gcols[:, 1:2], op=ALU.mult))
        OP("scalar", "activation", reads=["xlnf", f"rs{V}{t}"], writes=["s1ntok"],
           kwargs=dict(out=ntokb[:], in_=xlnf[:], func=AF.Copy, scale=cols[:, t, 5:6], bias=0.0))
        OP("gpsimd", "tensor_copy", reads=["s1ntok"], writes=[f"s1nbt{sl}"],
           kwargs=dict(out=nbtb[:, sl, :], in_=ntokb[:]))
        transpose_tile_to_cm(f"s1nbt{sl}", nbtb[:, sl, :], t)

    def transpose_tile_to_cm(src_name, src_ap, t):
        pst, pst_name = (psT0, "psT0") if t % 2 == 0 else (psT1, "psT1")
        for c in range(NC_):
            OP("tensor", "transpose", reads=[src_name, "ident"], writes=[pst_name],
               kwargs=dict(out=pst[:, c * 128:(c + 1) * 128], in_=src_ap[:, c * 128:(c + 1) * 128],
                           identity=ident[:]))
        OP("vector", "tensor_copy", reads=[pst_name],
           writes=[f"cm{c}t{t}" for c in range(NC_)],
           kwargs=dict(out=cm[:, :, t * 128:(t + 1) * 128], in_=pst[:].rearrange("p (a b) -> p a b", b=128)))

    with nc.sbuf_tensor("itmp", [128, 128], I32) as itmp:
        OP("gpsimd", "iota", writes=["itmp"], kwargs=dict(out=itmp[:], pattern=[[1, 128]], base=0, channel_multiplier=-1))
        OP("gpsimd", "tensor_scalar", reads=["itmp"], writes=["ident"],
           kwargs=dict(out=ident[:], in0=itmp[:], scalar1=0, scalar2=None, op0=ALU.is_equal))
        OP("gpsimd", "tensor_copy", reads=["ident"], writes=["identf"], kwargs=dict(out=ident_f[:], in_=ident[:]))
        OP("vector", "memset", writes=["ones1"], kwargs=dict(ap=ones1[:], constant=1.0))
        OP("vector", "memset", writes=["cst"], kwargs=dict(ap=cst[:], constant=EPS_LN))
        nc.const_aps.register(F32, EPS_LN, cst[:, 0:1])
        for t in range(NT):
            b, half = t // 2, t % 2
            OP("sync" if t % 2 == 0 else "scalar", "dma_start", writes=[f"res{t}"], dma=True,
               kwargs=dict(out=res[:, t, :], in_=x_h[b, half * 128:(half + 1) * 128, :]))
    k.barrier()

    pes = ExitStack()
    wld = pes.enter_context(nc.sbuf_tensor("wld", [128, 3, 2048], F32))
    wq8 = pes.enter_context(nc.sbuf_tensor("wq8", [128, 3, 2048], I8))
    wf8 = pes.enter_context(nc.sbuf_tensor("wf8", [128, 3, 2048], FP8))
    wbf = pes.enter_context(nc.sbuf_tensor("wbf", [128, 3, 2048], BF16))
    wsum = pes.enter_context(nc.sbuf_tensor("wsum", [128, 80], F32))
    wbt = pes.enter_context(nc.sbuf_tensor("wbt", [128, 3, 2048], F32))
    wsc = pes.enter_context(nc.sbuf_tensor("wsc", [128, 3, 64], F32))
    wrc = pes.enter_context(nc.sbuf_tensor("wrc", [128, 3, 64], F32))
    wi8 = pes.enter_context(nc.sbuf_tensor("wi8", [128, 3, 2048], I8))
    wqb = pes.enter_context(nc.sbuf_tensor("wqb", [128, 3, 2048], BF16))
    wob = pes.enter_context(nc.sbuf_tensor("wob", [128, 3, 2048], BF16))

    def slabs(n):
        out, off = [], 0
        while off < n:
            w = min(2048, n - off)
            out.append((off, w))
            off += w
        return out

    WSUM_OFF = {0: 0, 1: 16, 2: 32}

    def prep_w0(src_h, dst_s, n_i, mcol_name, mcol_ap, swci):
        nch = src_h.shape[0] // 128
        sl_list = slabs(n_i)
        wsoff = WSUM_OFF[swci]
        nslab = 0
        for c in range(nch):
            for (off, w) in sl_list:
                s = nslab; nslab += 1
                sl = s % 3
                OP("sync" if s % 2 == 0 else "scalar", "dma_start", writes=[f"wld{sl}"], dma=True,
                   kwargs=dict(out=wld[:, sl, 0:w], in_=src_h[c * 128:(c + 1) * 128, off:off + w]))
                OP("vector", "tensor_reduce", reads=[f"wld{sl}"], writes=[f"wsum{wsoff + s}"],
                   kwargs=dict(out=wsum[:, wsoff + s:wsoff + s + 1], in_=wld[:, sl, 0:w], axis=AX.X,
                               op=ALU.add, apply_absolute_value=True))
        tot = nslab
        c0 = 4 * swci
        OP("vector", "tensor_reduce", reads=[f"wsum{wsoff + s}" for s in range(tot)], writes=[f"tinyA{swci}"],
           kwargs=dict(out=tiny[:, c0:c0 + 1], in_=wsum[:, wsoff:wsoff + tot], axis=AX.X, op=ALU.add))
        OP("tensor", "transpose", reads=[f"tinyA{swci}", "identf"], writes=["psF"],
           kwargs=dict(out=psF[0:1, 0:128], in_=tiny[:, c0:c0 + 1], identity=ident_f[:]))
        OP("vector", "tensor_reduce", reads=["psF"], writes=[f"tinyB{swci}"],
           kwargs=dict(out=tiny[0:1, c0 + 1:c0 + 2], in_=psF[0:1, 0:128], axis=AX.X, op=ALU.add))
        OP("vector", "tensor_scalar", reads=[f"tinyB{swci}"], writes=[f"tinyC{swci}"],
           kwargs=dict(out=tiny[0:1, c0 + 2:c0 + 3], in0=tiny[0:1, c0 + 1:c0 + 2],
                       scalar1=1.0 / (src_h.shape[0] * n_i), scalar2=1e-5, op0=ALU.mult, op1=ALU.max))
        OP("vector", "reciprocal", reads=[f"tinyC{swci}"], writes=[f"tinyD{swci}"],
           kwargs=dict(out=tiny[0:1, c0 + 3:c0 + 4], in_=tiny[0:1, c0 + 2:c0 + 3]))
        scalar_bcast(f"tinyD{swci}", tiny[0:1, c0 + 3:c0 + 4], "swcol" + mcol_name, gcols[:, 8 + swci:9 + swci])
        scalar_bcast(f"tinyC{swci}", tiny[0:1, c0 + 2:c0 + 3], mcol_name, mcol_ap)
        nslab = 0
        for c in range(nch):
            for (off, w) in sl_list:
                s = nslab % 3; nslab += 1
                OP("sync" if s % 2 == 0 else "scalar", "dma_start", writes=[f"wld{s}"], dma=True,
                   kwargs=dict(out=wld[:, s, 0:w], in_=src_h[c * 128:(c + 1) * 128, off:off + w]))
                OP("scalar", "activation", reads=[f"wld{s}", "swcol" + mcol_name], writes=[f"wq8{s}"],
                   kwargs=dict(out=wq8[:, s, 0:w], in_=wld[:, s, 0:w], func=AF.Copy,
                               scale=gcols[:, 8 + swci:9 + swci], bias=0.0))
                OP("gpsimd", "tensor_scalar", reads=[f"wq8{s}"], writes=[f"wf8{s}"],
                   kwargs=dict(out=wf8[:, s, 0:w], in0=wq8[:, s, 0:w], scalar1=1.0, scalar2=-1.0,
                               op0=ALU.min, op1=ALU.max))
                OP("scalar", "dma_start", reads=[f"wf8{s}"], writes=[f"dst{dst_s.tensor.name}"], dma=True,
                   kwargs=dict(out=dst_s[c, :, off:off + w], in_=wf8[:, s, 0:w]))

    def prep_cast(src_h, dst_s, n_o):
        nch = src_h.shape[0] // 128
        i = 0
        for c in range(nch):
            for (off, w) in slabs(n_o):
                s = i % 3; i += 1
                OP("sync" if s % 2 == 0 else "scalar", "dma_start", writes=[f"wld{s}"], dma=True,
                   kwargs=dict(out=wld[:, s, 0:w], in_=src_h[c * 128:(c + 1) * 128, off:off + w]))
                OP("gpsimd", "tensor_copy", reads=[f"wld{s}"], writes=[f"wbf{s}"],
                   kwargs=dict(out=wbf[:, s, 0:w], in_=wld[:, s, 0:w]))
                OP("scalar", "dma_start", reads=[f"wbf{s}"], writes=[f"dst{dst_s.tensor.name}"], dma=True,
                   kwargs=dict(out=dst_s[c, :, off:off + w], in_=wbf[:, s, 0:w]))

    def lpbq_slab(sl, w, store_fn):
        OP("vector", "transpose", reads=[f"wld{sl}"], writes=[f"wbt{sl}"],
           kwargs=dict(out=wbt[:, sl, 0:w], in_=wld[:, sl, 0:w]))
        OP("vector", "tensor_reduce", reads=[f"wbt{sl}"], writes=[f"wsc{sl}"],
           kwargs=dict(out=wsc[:, sl, 0:w // 32], in_=wbt[:, sl, 0:w].rearrange("p (a b) -> p a b", b=32),
                       axis=AX.X, op=ALU.max, apply_absolute_value=True))
        OP("vector", "tensor_scalar", reads=[f"wsc{sl}"], writes=[f"wsc{sl}"],
           kwargs=dict(out=wsc[:, sl, 0:w // 32], in0=wsc[:, sl, 0:w // 32],
                       scalar1=1.0 / 7.0, scalar2=1e-12, op0=ALU.mult, op1=ALU.max))
        OP("vector", "reciprocal", reads=[f"wsc{sl}"], writes=[f"wrc{sl}"],
           kwargs=dict(out=wrc[:, sl, 0:w // 32], in_=wsc[:, sl, 0:w // 32]))
        OP("vector", "tensor_tensor", reads=[f"wbt{sl}", f"wrc{sl}"], writes=[f"wbt{sl}"],
           kwargs=dict(out=wbt[:, sl, 0:w].rearrange("p (a b) -> p a b", b=32),
                       in0=wbt[:, sl, 0:w].rearrange("p (a b) -> p a b", b=32),
                       in1=wrc[:, sl, 0:w // 32, None].broadcast_to((128, w // 32, 32)),
                       op=ALU.mult))
        OP("vector", "tensor_copy", reads=[f"wbt{sl}"], writes=[f"wi8{sl}"],
           kwargs=dict(out=wi8[:, sl, 0:w], in_=wbt[:, sl, 0:w]))
        OP("vector", "tensor_tensor", reads=[f"wi8{sl}", f"wsc{sl}"], writes=[f"wqb{sl}"],
           kwargs=dict(out=wqb[:, sl, 0:w].rearrange("p (a b) -> p a b", b=32),
                       in0=wi8[:, sl, 0:w].rearrange("p (a b) -> p a b", b=32),
                       in1=wsc[:, sl, 0:w // 32, None].broadcast_to((128, w // 32, 32)),
                       op=ALU.mult))
        OP("vector", "transpose", reads=[f"wqb{sl}"], writes=[f"wob{sl}"],
           kwargs=dict(out=wob[:, sl, 0:w], in_=wqb[:, sl, 0:w]))
        store_fn(sl, w)

    def prep_w1ud(src_h, dst_s):
        i = 0
        for c in range(NC_):
            for (off, w) in slabs(I1):
                sl = i % 3; i += 1
                OP("sync", "dma_start", writes=[f"wld{sl}"], dma=True,
                   kwargs=dict(out=wld[:, sl, 0:w], in_=src_h[c * 128:(c + 1) * 128, off:off + w]))
                def store(sl, w, c=c, off=off):
                    for j in range(w // 128):
                        of = (off + j * 128) // 128
                        OP("scalar", "dma_start", reads=[f"wob{sl}"], writes=[f"dst{dst_s.tensor.name}"], dma=True,
                           kwargs=dict(out=dst_s[of, c, :, :], in_=wob[:, sl, j * 128:(j + 1) * 128]))
                lpbq_slab(sl, w, store)

    def prep_w1d(src_h, dst_s):
        for c in range(I1 // 128):
            sl = c % 3
            OP("sync", "dma_start", writes=[f"wld{sl}"], dma=True,
               kwargs=dict(out=wld[:, sl, 0:1024], in_=src_h[c * 128:(c + 1) * 128, :]))
            def store(sl, w, c=c):
                OP("scalar", "dma_start", reads=[f"wob{sl}"], writes=[f"dst{dst_s.tensor.name}"], dma=True,
                   kwargs=dict(out=dst_s[c, :, :], in_=wob[:, sl, 0:w]))
            lpbq_slab(sl, 1024, store)

    nf32p = pes.enter_context(nc.sbuf_tensor("nf32p", [128, C], F32))
    ntokp = pes.enter_context(nc.sbuf_tensor("ntokp", [128, C], I8))
    nbtp = pes.enter_context(nc.sbuf_tensor("nbtp", [128, 2, C], BF16))
    if dbg != "noprep":
        prep_w0(w0g_h, s_w0g, I0, "mcol_g", gcols[:, 0:1], 0)
        prep_w0(w0u_h, s_w0u, I0, "mcol_u", gcols[:, 1:2], 1)
    else:
        OP("vector", "memset", writes=["mcol_g"], kwargs=dict(ap=gcols[:, 0:1], constant=1.0))
        OP("vector", "memset", writes=["mcol_u"], kwargs=dict(ap=gcols[:, 1:2], constant=1.0))
    for t in range(NT):
        s1_tile(t, "f0", (nf32p, ntokp, nbtp))
    if dbg != "noprep":
        prep_w0(w0d_h, s_w0d, C, "mcol_d", gcols[:, 2:3], 2)
    else:
        OP("vector", "memset", writes=["mcol_d"], kwargs=dict(ap=gcols[:, 2:3], constant=1.0))
    if n_sub >= 2:
        prep_cast(wqkv_h, s_qkv, 3 * C)
        prep_cast(wproj_h, s_proj, C)
    if n_sub >= 4:
        prep_w1ud(w1g_h, s_w1g)
        prep_w1ud(w1u_h, s_w1u)
        prep_w1d(w1d_h, s_w1d)
    OP("vector", "tensor_scalar", reads=["mcol_d"], writes=["gc2h"],
       kwargs=dict(out=gcols[:, 5:6], in0=gcols[:, 2:3], scalar1=0.5, scalar2=None, op0=ALU.mult))
    k.barrier()
    pes.close()

    def l0ffn(inv):
        V = f"f{inv}"
        fes = ExitStack()
        w0sb = fes.enter_context(nc.sbuf_tensor(f"w0sb{inv}", [128, 64, 1024], FP8))
        n2i8 = fes.enter_context(nc.sbuf_tensor(f"n2i8{inv}", [128, 4, I0], I8))
        inner = fes.enter_context(nc.sbuf_tensor(f"innr{inv}", [128, I0], F32))
        sg_t = fes.enter_context(nc.sbuf_tensor(f"sg_t{inv}", [128, 512], F32))
        ntok = fes.enter_context(nc.sbuf_tensor(f"ntk{inv}", [128, C], I8))
        nf32 = fes.enter_context(nc.sbuf_tensor(f"nf32{inv}", [128, C], F32))
        nbt = fes.enter_context(nc.sbuf_tensor(f"nbt{inv}", [128, 2, C], BF16))
        mx8 = fes.enter_context(nc.sbuf_tensor(f"mx8{inv}", [128, 16], F32))
        n2bf = fes.enter_context(nc.sbuf_tensor(f"n2bf{inv}", [128, I0 // 4], BF16))
        n2cm = fes.enter_context(nc.sbuf_tensor(f"n2cm{inv}", [128, I0 // 128, 128], BF16))

        def load_gu():
            for c in range(NC_):
                OP("sync", "dma_start", writes=[f"w0p{4 * c + j}" for j in range(4)], dma=True,
                   kwargs=dict(out=w0sb[:, 4 * c:4 * c + 4, :], in_=s_w0g[c, :, :].rearrange("p (a b) -> p a b", b=1024)))
                OP("scalar", "dma_start", writes=[f"w0p{32 + 4 * c + j}" for j in range(4)], dma=True,
                   kwargs=dict(out=w0sb[:, 32 + 4 * c:32 + 4 * c + 4, :], in_=s_w0u[c, :, :].rearrange("p (a b) -> p a b", b=1024)))

        def load_d():
            for ic in range(I0 // 128):
                q = "sync" if ic % 2 == 0 else "scalar"
                OP(q, "dma_start", writes=[f"w0p{ic}"], dma=True,
                   kwargs=dict(out=w0sb[:, ic, :], in_=s_w0d[ic, :, :]))

        if inv == 1:
            for t in range(NT):
                s1_tile(t, V, (nf32, ntok, nbt))
        for pack in range(4):
            tiles = range(pack * 4, pack * 4 + 4)
            load_gu()
            for t in tiles:
                tsl = t % 4
                for og in range(8):
                    pg, pg_n, pu, pu_n = [(psA, "psA", psB, "psB"), (psC, "psC", psD, "psD"),
                                          (psE, "psE", psF, "psF")][og % 3]
                    gplane = lambda c: 4 * c + og // 2
                    uplane = lambda c: 32 + 4 * c + og // 2
                    gsl = (og % 2) * 512
                    for c in range(NC_):
                        OP("tensor", "matmul", reads=[f"cm{c}t{t}", f"w0p{gplane(c)}"],
                           writes=[pg_n] if c in (0, NC_ - 1) else [],
                           kwargs=dict(out=pg[:], lhsT=cm[:, c, t * 128:(t + 1) * 128],
                                       rhs=w0sb[:, gplane(c), gsl:gsl + 512],
                                       start=(c == 0), stop=(c == NC_ - 1)))
                        OP("tensor", "matmul", reads=[f"cm{c}t{t}", f"w0p{uplane(c)}"],
                           writes=[pu_n] if c in (0, NC_ - 1) else [],
                           kwargs=dict(out=pu[:], lhsT=cm[:, c, t * 128:(t + 1) * 128],
                                       rhs=w0sb[:, uplane(c), gsl:gsl + 512],
                                       start=(c == 0), stop=(c == NC_ - 1)))
                    OP("scalar", "activation", reads=[pg_n, f"scg{V}{t}"], writes=["sg_t"],
                       kwargs=dict(out=sg_t[:], in_=pg[:], func=AF.Silu, scale=cols[:, t, 6:7]))
                    OP("vector", "tensor_scalar", reads=[pu_n, f"scu{V}{t}"], writes=[f"innr{og}"],
                       kwargs=dict(out=inner[:, og * 512:(og + 1) * 512], in0=pu[:],
                                   scalar1=cols[:, t, 7:8], scalar2=None, op0=ALU.mult))
                    OP("vector", "tensor_tensor", reads=[f"innr{og}", "sg_t"], writes=[f"innr{og}"],
                       kwargs=dict(out=inner[:, og * 512:(og + 1) * 512],
                                   in0=inner[:, og * 512:(og + 1) * 512], in1=sg_t[:], op=ALU.mult))
                OP("vector", "tensor_reduce", reads=[f"innr{og2}" for og2 in range(8)], writes=[f"mx2{V}{t}"],
                   kwargs=dict(out=cols[:, t, 8:9], in_=inner[:], axis=AX.X, op=ALU.max,
                               apply_absolute_value=True))
                OP("vector", "tensor_scalar", reads=[f"mx2{V}{t}"], writes=[f"q2{V}{t}"],
                   kwargs=dict(out=cols[:, t, 9:10], in0=cols[:, t, 8:9], scalar1=1.0 / 127.0,
                               scalar2=1e-5 / 127.0, op0=ALU.mult, op1=ALU.max))
                OP("vector", "reciprocal", reads=[f"q2{V}{t}"], writes=[f"rs2{V}{t}"],
                   kwargs=dict(out=cols[:, t, 10:11], in_=cols[:, t, 9:10]))
                OP("vector", "tensor_tensor", reads=[f"q2{V}{t}", "gc2h"], writes=[f"scd{V}{t}"],
                   kwargs=dict(out=cols[:, t, 11:12], in0=cols[:, t, 9:10], in1=gcols[:, 5:6], op=ALU.mult))
                OP("scalar", "activation", reads=[f"innr{og}" for og in range(8)] + [f"rs2{V}{t}"],
                   writes=[f"n2i8_{tsl}"],
                   kwargs=dict(out=n2i8[:, tsl, :], in_=inner[:], func=AF.Copy,
                               scale=cols[:, t, 10:11], bias=0.0))
            load_d()
            for t in tiles:
                tsl = t % 4
                for q4 in range(4):
                    pst, pst_name = (psT0, "psT0") if q4 % 2 == 0 else (psT1, "psT1")
                    OP("gpsimd", "tensor_copy", reads=[f"n2i8_{tsl}"], writes=["n2bf"],
                       kwargs=dict(out=n2bf[:], in_=n2i8[:, tsl, q4 * 1024:(q4 + 1) * 1024]))
                    for j in range(8):
                        OP("tensor", "transpose", reads=["n2bf", "ident"], writes=[pst_name],
                           kwargs=dict(out=pst[:, j * 128:(j + 1) * 128],
                                       in_=n2bf[:, j * 128:(j + 1) * 128], identity=ident[:]))
                    OP("vector", "tensor_copy", reads=[pst_name], writes=[f"n2cm{q4}"],
                       kwargs=dict(out=n2cm[:, q4 * 8:(q4 + 1) * 8, :],
                                   in_=pst[:].rearrange("p (a b) -> p a b", b=128)))
                for half in range(2):
                    pd, pd_n = (psE, "psE") if half == 0 else (psF, "psF")
                    for ic in range(I0 // 128):
                        OP("tensor", "matmul", reads=[f"n2cm{ic // 8}", f"w0p{ic}"],
                           writes=[pd_n] if ic in (0, I0 // 128 - 1) else [],
                           kwargs=dict(out=pd[:], lhsT=n2cm[:, ic, :],
                                       rhs=w0sb[:, ic, half * 512:(half + 1) * 512],
                                       start=(ic == 0), stop=(ic == I0 // 128 - 1)))
                    OP("vector", "scalar_tensor_tensor", reads=[pd_n, f"scd{V}{t}", f"res{t}"],
                       writes=[f"res{t}"],
                       kwargs=dict(out=res[:, t, half * 512:(half + 1) * 512], in0=pd[:],
                                   scalar=cols[:, t, 11:12], in1=res[:, t, half * 512:(half + 1) * 512],
                                   op0=ALU.mult, op1=ALU.add))
        k.barrier()
        fes.close()

    def l0attn():
        aes = ExitStack()
        wA = aes.enter_context(nc.sbuf_tensor("wA", [128, NC_, 2048], BF16))
        bias_b = aes.enter_context(nc.sbuf_tensor("bias_b", [128, C], F32))
        phe = aes.enter_context(nc.sbuf_tensor("phe", [128, 512], F32))
        OP("sync", "dma_start", writes=["bias_b"], dma=True,
           kwargs=dict(out=bias_b[:], in_=bproj_h[0:1, :].broadcast_to((128, C))))
        qfm = aes.enter_context(nc.sbuf_tensor("qfm", [128, NC_, ROWS], BF16))
        atk = aes.enter_context(nc.sbuf_tensor("atk", [128, NT, 1040], BF16))
        xbt = aes.enter_context(nc.sbuf_tensor("xbt", [128, C], BF16))
        nrm4 = aes.enter_context(nc.sbuf_tensor("nrm4", [128, 2, 4], F32))
        kvst = aes.enter_context(nc.sbuf_tensor("kvst", [128, 2, 455], F32))

        def phi_evac(ps_ap, ps_name, dst_ap, dst_name):
            OP("vector", "tensor_scalar", reads=[ps_name], writes=["phe"],
               kwargs=dict(out=phe[:], in0=ps_ap, scalar1=0.0, scalar2=None, op0=ALU.min))
            OP("scalar", "activation", reads=["phe"], writes=["phe"],
               kwargs=dict(out=phe[:], in_=phe[:], func=AF.Exp))
            OP("vector", "scalar_tensor_tensor", reads=[ps_name, "phe"], writes=[dst_name],
               kwargs=dict(out=dst_ap, in0=ps_ap, scalar=0.0, in1=phe[:], op0=ALU.max, op1=ALU.add))
        for t in range(NT):
            lnorm(t, False, out_bf_ap=xbt[:], out_bf_name="xbt", tag="a")
            transpose_tile_to_cm("xbt", xbt[:], t)
        for c in range(NC_):
            OP("sync", "dma_start", writes=[f"wA{c}"], dma=True,
               kwargs=dict(out=wA[:, c, 0:C], in_=s_qkv[c, :, 0:C]))
        for j in range(NC_):
            for tg in range(4):
                pq, pq_n = (psA, "psA") if tg % 2 == 0 else (psB, "psB")
                for c in range(NC_):
                    OP("tensor", "matmul",
                       reads=[f"wA{c}"] + [f"cm{c}t{tg * 4 + j2}" for j2 in range(4)],
                       writes=[pq_n] if c in (0, NC_ - 1) else [],
                       kwargs=dict(out=pq[:], lhsT=wA[:, c, j * 128:(j + 1) * 128],
                                   rhs=cm[:, c, tg * 512:(tg + 1) * 512],
                                   start=(c == 0), stop=(c == NC_ - 1)))
                phi_evac(pq[:], pq_n, qfm[:, j, tg * 512:(tg + 1) * 512], f"qfm{j}")
        for c in range(NC_):
            OP("sync", "dma_start", writes=[f"wA{c}"], dma=True,
               kwargs=dict(out=wA[:, c, :], in_=s_qkv[c, :, C:3 * C]))
        for hh in range(2):
            for t in range(NT):
                for kv in range(2):
                    pk, pk_n = (psC, "psC") if kv == 0 else (psD, "psD")
                    col0 = hh * 512 if kv == 0 else 1024 + hh * 512
                    for c in range(NC_):
                        OP("tensor", "matmul", reads=[f"cm{c}t{t}", f"wA{c}"],
                           writes=[pk_n] if c in (0, NC_ - 1) else [],
                           kwargs=dict(out=pk[:], lhsT=cm[:, c, t * 128:(t + 1) * 128],
                                       rhs=wA[:, c, col0:col0 + 512],
                                       start=(c == 0), stop=(c == NC_ - 1)))
                    if kv == 0:
                        phi_evac(pk[:], pk_n, atk[:, t, 0:512], f"atk{t}")
                    else:
                        OP("vector", "tensor_copy", reads=[pk_n], writes=[f"atk{t}"],
                           kwargs=dict(out=atk[:, t, 512:1032].rearrange("p (h j) -> p h j", j=65)[:, :, 0:64],
                                       in_=pk[:].rearrange("p (h j) -> p h j", j=64)))
                OP("vector", "memset", writes=[f"atk{t}"],
                   kwargs=dict(ap=atk[:, t, 512:1032].rearrange("p (h j) -> p h j", j=65)[:, :, 64:65], constant=1.0))
            for i in range(32):
                b, mloc = i // 4, i % 4
                m = hh * 4 + mloc
                g = b * 8 + m
                pk, pk_n = (psE, "psE") if (i // 7) % 2 == 0 else (psF, "psF")
                sl7 = i % 7
                for par in range(2):
                    hloc = 2 * mloc + par
                    for dt_ in range(2):
                        t = 2 * b + dt_
                        OP("tensor", "matmul", reads=[f"atk{t}"],
                           writes=[pk_n] if (par, dt_) in ((0, 0), (1, 1)) else [],
                           kwargs=dict(out=pk[par * 64:(par + 1) * 64, sl7 * 65:(sl7 + 1) * 65],
                                       lhsT=atk[:, t, hloc * 64:(hloc + 1) * 64],
                                       rhs=atk[:, t, 512 + hloc * 65:512 + (hloc + 1) * 65],
                                       start=(dt_ == 0), stop=(dt_ == 1),
                                       tile_position=(0, par * 64)))
                if sl7 == 6 or i == 31:
                    n = sl7 + 1
                    g0i = i - sl7
                    lin0 = hh * 32 + g0i
                    ksl = (i // 7) % 2
                    OP("vector", "tensor_copy", reads=[pk_n], writes=[f"kvst{ksl}"],
                       kwargs=dict(out=kvst[:, ksl, 0:n * 65], in_=pk[:, 0:n * 65]))
                    OP("sync", "dma_start", reads=[f"kvst{ksl}"], writes=["kv_in_d"], dma=True,
                       kwargs=dict(out=kv_in[:, lin0 * 65:(lin0 + n) * 65], in_=kvst[:, ksl, 0:n * 65]))
        OP("gpsimd", "collective_compute", reads=["kv_in_d"], writes=["kv_out_d"],
           kwargs=dict(kind="AllReduce", op=ALU.add, replica_groups=[list(range(NCORES))],
                       ins=[kv_in[:]], outs=[kv_out[:]]))
        kvb = aes.enter_context(nc.sbuf_tensor("kvb", [128, 64, 65], BF16))
        g0 = 0
        j = 0
        while g0 < 64:
            n = min(7, 64 - g0)
            ksl = j % 2; j += 1
            OP("sync", "dma_start", reads=["kv_out_d"], writes=[f"kvst{ksl}"], dma=True,
               kwargs=dict(out=kvst[:, ksl, 0:n * 65], in_=kv_out[:, g0 * 65:(g0 + n) * 65]))
            OP("vector", "tensor_copy", reads=[f"kvst{ksl}"], writes=["kvb"],
               kwargs=dict(out=kvb[:, g0:g0 + n, :], in_=kvst[:, ksl, 0:n * 65]))
            g0 += n
        for b in range(B):
            for half in range(2):
                t = 2 * b + half
                for par in range(2):
                    pm, pm_n = (psE, "psE") if par == 0 else (psF, "psF")
                    for m4 in range(2):
                        for mm in range(4):
                            m = m4 * 4 + mm
                            h = 2 * m + par
                            lin = (m // 4) * 32 + b * 4 + (m % 4)
                            OP("tensor", "matmul", reads=[f"qfm{h // 2}", "kvb"],
                               writes=[pm_n] if mm in (0, 3) else [],
                               kwargs=dict(out=pm[:, mm * 65:(mm + 1) * 65],
                                           lhsT=qfm[par * 64:(par + 1) * 64, h // 2, t * 128:(t + 1) * 128],
                                           rhs=kvb[par * 64:(par + 1) * 64, lin, :],
                                           start=True, stop=True,
                                           tile_position=(par * 64, 0)))
                        OP("vector", "tensor_scalar", reads=[pm_n], writes=[f"nrm{par}"],
                           kwargs=dict(out=nrm4[:, par, :],
                                       in0=pm[:, 0:260].rearrange("p (s j) -> p s j", j=65)[:, :, 64],
                                       scalar1=EPS_ATTN, scalar2=None, op0=ALU.add))
                        OP("vector", "reciprocal", reads=[f"nrm{par}"], writes=[f"nrm{par}"],
                           kwargs=dict(out=nrm4[:, par, :], in_=nrm4[:, par, :]))
                        OP("vector", "tensor_tensor", reads=[pm_n, f"nrm{par}"], writes=[f"atk{t}"],
                           kwargs=dict(out=atk[:, t, 0:1024].rearrange("p (m par2 j) -> p m par2 j", par2=2, j=64)[:, m4 * 4:(m4 + 1) * 4, par, :],
                                       in0=pm[:, 0:260].rearrange("p (s j) -> p s j", j=65)[:, :, 0:64],
                                       in1=nrm4[:, par, :, None].broadcast_to((128, 4, 64)),
                                       op=ALU.mult))
        for t in range(NT):
            transpose_tile_to_cm(f"atk{t}", atk[:, t, 0:1024], t)
        for c in range(NC_):
            OP("sync", "dma_start", writes=[f"wA{c}"], dma=True,
               kwargs=dict(out=wA[:, c, 0:C], in_=s_proj[c, :, :]))
        for t in range(NT):
            for half in range(2):
                pp, pp_n = (psA, "psA") if half == 0 else (psB, "psB")
                for c in range(NC_):
                    OP("tensor", "matmul", reads=[f"cm{c}t{t}", f"wA{c}"],
                       writes=[pp_n] if c in (0, NC_ - 1) else [],
                       kwargs=dict(out=pp[:], lhsT=cm[:, c, t * 128:(t + 1) * 128],
                                   rhs=wA[:, c, half * 512:(half + 1) * 512],
                                   start=(c == 0), stop=(c == NC_ - 1)))
                OP("vector", "tensor_tensor", reads=[pp_n, "bias_b"], writes=["phe"],
                   kwargs=dict(out=phe[:], in0=pp[:], in1=bias_b[:, half * 512:(half + 1) * 512], op=ALU.add))
                OP("vector", "tensor_tensor", reads=["phe", f"res{t}"], writes=[f"res{t}"],
                   kwargs=dict(out=res[:, t, half * 512:(half + 1) * 512],
                               in0=phe[:], in1=res[:, t, half * 512:(half + 1) * 512], op=ALU.add))
        k.barrier()
        aes.close()


    def l1ffn(inv, emit_out=False):
        V = f"g{inv}"
        J0 = inv * 4
        les0 = ExitStack()
        A = les0.enter_context(nc.sbuf_tensor(f"A{inv}", [128, 24, 2048], BF16))
        i16b = les0.enter_context(nc.sbuf_tensor(f"i16b{inv}", [128, 2, 2048], I16))

        lesa = ExitStack()
        xbt2 = lesa.enter_context(nc.sbuf_tensor(f"xbt2{inv}", [128, C], BF16))
        for t in range(NT):
            lnorm(t, True, out_bf_ap=xbt2[:], out_bf_name="xbt2", tag=V)
            OP("vector", "tensor_reduce", reads=["xlnf"], writes=[f"xm{V}{t}"],
               kwargs=dict(out=cols[:, t, 3:4], in_=xlnf[:], axis=AX.X, op=ALU.max,
                           apply_absolute_value=True))
            transpose_tile_to_cm("xbt2", xbt2[:], t)
        OP("vector", "tensor_reduce", reads=[f"xm{V}{t}" for t in range(NT)], writes=["tc8"],
           kwargs=dict(out=tiny[:, 8:9], in_=cols[:, :, 3], axis=AX.X, op=ALU.max))
        part_max("tc8", tiny[:, 8:9], NT, "tc9", tiny[0:1, 9:10], 7)
        col_allreduce([("tc9", tiny[0:1, 9:10])], J0 + 0, [("gcol6", gcols[:, 6:7])])
        OP("vector", "tensor_scalar", reads=["gcol6"], writes=["gcol7"],
           kwargs=dict(out=gcols[:, 7:8], in0=gcols[:, 6:7], scalar1=1.0 / 32767.0,
                       scalar2=1e-12, op0=ALU.mult, op1=ALU.max))
        OP("vector", "reciprocal", reads=["gcol7"], writes=["gcol8"],
           kwargs=dict(out=gcols[:, 8:9], in_=gcols[:, 7:8]))
        k.barrier()
        lesa.close()
        for c in range(NC_):
            OP("scalar", "activation", reads=[f"cm{c}t{tt_}" for tt_ in range(NT)] + ["gcol8"],
               writes=[f"i16b{c % 2}"],
               kwargs=dict(out=i16b[:, c % 2, :], in_=cm[:, c, :], func=AF.Copy, scale=gcols[:, 8:9], bias=0.0))
            OP("vector", "tensor_scalar", reads=[f"i16b{c % 2}", "gcol7"],
               writes=[f"cm{c}t{tt_}" for tt_ in range(NT)],
               kwargs=dict(out=cm[:, c, :], in0=i16b[:, c % 2, :], scalar1=gcols[:, 7:8], scalar2=None, op0=ALU.mult))

        def dump_fm(get, n=8):
            for j in range(n):
                ap, nm = get(j)
                OP("vector", "tensor_copy", reads=[nm], writes=[f"res{2 * j}"],
                   kwargs=dict(out=res[:, 2 * j, :], in_=ap[:, 0:1024]))
                OP("vector", "tensor_copy", reads=[nm], writes=[f"res{2 * j + 1}"],
                   kwargs=dict(out=res[:, 2 * j + 1, :], in_=ap[:, 1024:2048]))
        if dbg == "l1xq":
            dump_fm(lambda j: (cm[:, j, :], f"cm{j}"))
            k.barrier(); les0.close()
            return
        les1 = ExitStack()
        w1p = les1.enter_context(nc.sbuf_tensor(f"w1p{inv}", [128, 1, NC_, 128], BF16))
        sgf = les1.enter_context(nc.sbuf_tensor(f"sgf{inv}", [128, 2, 1024], F32))
        u32 = les1.enter_context(nc.sbuf_tensor(f"u32{inv}", [128, 1024], I32))
        upb = les1.enter_context(nc.sbuf_tensor(f"upb{inv}", [128, 2, 512], BF16))

        OP("vector", "memset", writes=["tc11"], kwargs=dict(ap=tiny[:, 11:12], constant=0.0))
        def ugmat(of, which, sl_ignore):
            src = s_w1g if which == "g" else s_w1u
            sl = 0
            OP("sync", "dma_start", writes=[f"w1p{sl}"], dma=True,
               kwargs=dict(out=w1p[:, sl, :, :], in_=src[of, :, :, :].rearrange("c p q -> p c q")))
            for tg in range(4):
                pq, pq_n = (psA, "psA") if tg % 2 == 0 else (psB, "psB")
                for c in range(NC_):
                    OP("tensor", "matmul",
                       reads=[f"w1p{sl}"] + [f"cm{c}t{tg * 4 + j2}" for j2 in range(4)],
                       writes=[pq_n] if c in (0, NC_ - 1) else [],
                       kwargs=dict(out=pq[:], lhsT=w1p[:, sl, c, :],
                                   rhs=cm[:, c, tg * 512:(tg + 1) * 512],
                                   start=(c == 0), stop=(c == NC_ - 1)))
                yield of, tg, pq, pq_n
        for of in range(24):
            for _, tg, pq, pq_n in ugmat(of, "g", of % 2):
                OP("vector", "tensor_copy", reads=[pq_n], writes=[f"A{of}"],
                   kwargs=dict(out=A[:, of, tg * 512:(tg + 1) * 512], in_=pq[:]))
            OP("vector", "tensor_reduce", reads=[f"A{of}"], writes=["tc10"],
               kwargs=dict(out=tiny[:, 10:11], in_=A[:, of, :], axis=AX.X, op=ALU.max,
                           apply_absolute_value=True))
            OP("vector", "tensor_tensor", reads=["tc10", "tc11"], writes=["tc11"],
               kwargs=dict(out=tiny[:, 11:12], in0=tiny[:, 10:11], in1=tiny[:, 11:12], op=ALU.max))
        if dbg == "l1gate":
            dump_fm(lambda j: (A[:, j, :], f"A{j}"))
            k.barrier(); les1.close(); les0.close()
            return
        part_max("tc11", tiny[:, 11:12], 1, "tc12", tiny[0:1, 12:13], 7)
        col_allreduce([("tc12", tiny[0:1, 12:13])], J0 + 1, [("gcol9", gcols[:, 9:10])])
        OP("vector", "tensor_scalar", reads=["gcol9"], writes=["gcol10"],
           kwargs=dict(out=gcols[:, 10:11], in0=gcols[:, 9:10], scalar1=1.0 / 32767.0,
                       scalar2=1e-12, op0=ALU.mult, op1=ALU.max))
        OP("vector", "reciprocal", reads=["gcol10"], writes=["gcol11"],
           kwargs=dict(out=gcols[:, 11:12], in_=gcols[:, 10:11]))
        OP("vector", "memset", writes=["tc13"], kwargs=dict(ap=tiny[:, 13:14], constant=0.0))
        OP("vector", "memset", writes=["tc15"], kwargs=dict(ap=tiny[:, 15:16], constant=0.0))
        for of in range(24):
            for _, tg, pq, pq_n in ugmat(of, "u", of % 2):
                OP("vector", "tensor_copy", reads=[pq_n], writes=[f"upb{tg % 2}"],
                   kwargs=dict(out=upb[:, tg % 2, :], in_=pq[:]))
                OP("vector", "tensor_reduce", reads=[pq_n], writes=["tc10"],
                   kwargs=dict(out=tiny[:, 10:11], in_=pq[:], axis=AX.X, op=ALU.max,
                               apply_absolute_value=True))
                OP("vector", "tensor_tensor", reads=["tc10", "tc13"], writes=["tc13"],
                   kwargs=dict(out=tiny[:, 13:14], in0=tiny[:, 10:11], in1=tiny[:, 13:14], op=ALU.max))
                OP("sync", "dma_start", reads=[f"upb{tg % 2}"], writes=["s_up_d"], dma=True,
                   kwargs=dict(out=s_up[of, :, tg * 512:(tg + 1) * 512], in_=upb[:, tg % 2, :]))
            ib = of % 2
            OP("scalar", "activation", reads=[f"A{of}", "gcol11"], writes=[f"i16b{ib}"],
               kwargs=dict(out=i16b[:, ib, :], in_=A[:, of, :], func=AF.Copy, scale=gcols[:, 11:12], bias=0.0))
            for hf in range(2):
                sl_ = slice(hf * 1024, (hf + 1) * 1024)
                OP("scalar", "activation", reads=[f"i16b{ib}", "gcol10"], writes=[f"sgf{hf}"],
                   kwargs=dict(out=sgf[:, hf, :], in_=i16b[:, ib, sl_], func=AF.Sigmoid, scale=gcols[:, 10:11]))
                OP("scalar", "activation", reads=[f"sgf{hf}"], writes=["u32"],
                   kwargs=dict(out=u32[:], in_=sgf[:, hf, :], func=AF.Copy, scale=65536.0, bias=0.0))
                OP("vector", "tensor_scalar", reads=["u32"], writes=[f"sgf{hf}"],
                   kwargs=dict(out=sgf[:, hf, :], in0=u32[:], scalar1=65535.0, scalar2=1.0 / 65536.0,
                               op0=ALU.min, op1=ALU.mult))
                OP("vector", "tensor_tensor", reads=[f"i16b{ib}", f"sgf{hf}"], writes=[f"A{of}"],
                   kwargs=dict(out=A[:, of, sl_], in0=i16b[:, ib, sl_], in1=sgf[:, hf, :], op=ALU.mult))
            OP("vector", "tensor_reduce", reads=[f"A{of}"], writes=["tc10"],
               kwargs=dict(out=tiny[:, 10:11], in_=A[:, of, :], axis=AX.X, op=ALU.max,
                           apply_absolute_value=True))
            OP("vector", "tensor_tensor", reads=["tc10", "tc15"], writes=["tc15"],
               kwargs=dict(out=tiny[:, 15:16], in0=tiny[:, 10:11], in1=tiny[:, 15:16], op=ALU.max))
        if dbg == "l1g2":
            dump_fm(lambda j: (A[:, j, :], f"A{j}"))
            k.barrier(); les1.close(); les0.close()
            return
        OP("vector", "tensor_tensor", reads=["tc15", "gcol10"], writes=["tc15"],
           kwargs=dict(out=tiny[:, 15:16], in0=tiny[:, 15:16], in1=gcols[:, 10:11], op=ALU.mult))
        part_max("tc13", tiny[:, 13:14], 1, "tc14", tiny[0:1, 14:15], 7)
        part_max("tc15", tiny[:, 15:16], 1, "tc15", tiny[0:1, 15:16], 7)
        col_allreduce([("tc14", tiny[0:1, 14:15]), ("tc15", tiny[0:1, 15:16])], J0 + 2,
                      [("gcol12", gcols[:, 12:13]), ("gcol13", gcols[:, 13:14])])
        OP("vector", "tensor_scalar", reads=["gcol12"], writes=["gcol12"],
           kwargs=dict(out=gcols[:, 12:13], in0=gcols[:, 12:13], scalar1=1.0 / 32767.0,
                       scalar2=1e-12, op0=ALU.mult, op1=ALU.max))
        OP("vector", "reciprocal", reads=["gcol12"], writes=["gcol14"],
           kwargs=dict(out=gcols[:, 14:15], in_=gcols[:, 12:13]))
        OP("vector", "tensor_scalar", reads=["gcol13"], writes=["gcol13"],
           kwargs=dict(out=gcols[:, 13:14], in0=gcols[:, 13:14], scalar1=1.0 / 32767.0,
                       scalar2=1e-12, op0=ALU.mult, op1=ALU.max))
        OP("vector", "reciprocal", reads=["gcol13"], writes=["gcol15"],
           kwargs=dict(out=gcols[:, 15:16], in_=gcols[:, 13:14]))
        OP("vector", "tensor_tensor", reads=["gcol15", "gcol10"], writes=["gcol15"],
           kwargs=dict(out=gcols[:, 15:16], in0=gcols[:, 15:16], in1=gcols[:, 10:11], op=ALU.mult))
        k.barrier()
        les1.close()
        les2 = ExitStack()
        upf = les2.enter_context(nc.sbuf_tensor(f"upf{inv}", [128, 2, 2048], BF16))
        i16c = les2.enter_context(nc.sbuf_tensor(f"i16c{inv}", [128, 2, 2048], I16))
        OP("vector", "memset", writes=["tc13"], kwargs=dict(ap=tiny[:, 13:14], constant=0.0))
        for of in range(24):
            sl = of % 2
            OP("sync", "dma_start", reads=["s_up_d"], writes=[f"upf{sl}"], dma=True,
               kwargs=dict(out=upf[:, sl, :], in_=s_up[of, :, :]))
            ib = of % 2
            OP("scalar", "activation", reads=[f"upf{sl}", "gcol14"], writes=[f"i16c{ib}"],
               kwargs=dict(out=i16c[:, ib, :], in_=upf[:, sl, :], func=AF.Copy, scale=gcols[:, 14:15], bias=0.0))
            OP("scalar", "activation", reads=[f"A{of}", "gcol15"], writes=[f"i16b{ib}"],
               kwargs=dict(out=i16b[:, ib, :], in_=A[:, of, :], func=AF.Copy, scale=gcols[:, 15:16], bias=0.0))
            OP("vector", "tensor_tensor", reads=[f"i16b{ib}", f"i16c{ib}"], writes=[f"A{of}"],
               kwargs=dict(out=A[:, of, :], in0=i16b[:, ib, :], in1=i16c[:, ib, :], op=ALU.mult))
            OP("vector", "tensor_reduce", reads=[f"A{of}"], writes=["tc10"],
               kwargs=dict(out=tiny[:, 10:11], in_=A[:, of, :], axis=AX.X, op=ALU.max,
                           apply_absolute_value=True))
            OP("vector", "tensor_tensor", reads=["tc10", "tc13"], writes=["tc13"],
               kwargs=dict(out=tiny[:, 13:14], in0=tiny[:, 10:11], in1=tiny[:, 13:14], op=ALU.max))
        if dbg == "l1p":
            dump_fm(lambda j: (A[:, j, :], f"A{j}"))
            k.barrier(); les2.close(); les1.close() if False else None; les0.close()
            return
        OP("vector", "tensor_tensor", reads=["tc13", "gcol13"], writes=["tc13"],
           kwargs=dict(out=tiny[:, 13:14], in0=tiny[:, 13:14], in1=gcols[:, 13:14], op=ALU.mult))
        OP("vector", "tensor_tensor", reads=["tc13", "gcol12"], writes=["tc13"],
           kwargs=dict(out=tiny[:, 13:14], in0=tiny[:, 13:14], in1=gcols[:, 12:13], op=ALU.mult))
        part_max("tc13", tiny[:, 13:14], 1, "tc14", tiny[0:1, 14:15], 7)
        col_allreduce([("tc14", tiny[0:1, 14:15])], J0 + 3, [("gcol9", gcols[:, 9:10])])
        OP("vector", "tensor_scalar", reads=["gcol9"], writes=["gcol9"],
           kwargs=dict(out=gcols[:, 9:10], in0=gcols[:, 9:10], scalar1=1.0 / 32767.0,
                       scalar2=1e-12, op0=ALU.mult, op1=ALU.max))
        OP("vector", "reciprocal", reads=["gcol9"], writes=["gcol10"],
           kwargs=dict(out=gcols[:, 10:11], in_=gcols[:, 9:10]))
        OP("vector", "tensor_tensor", reads=["gcol10", "gcol13"], writes=["gcol10"],
           kwargs=dict(out=gcols[:, 10:11], in0=gcols[:, 10:11], in1=gcols[:, 13:14], op=ALU.mult))
        OP("vector", "tensor_tensor", reads=["gcol10", "gcol12"], writes=["gcol10"],
           kwargs=dict(out=gcols[:, 10:11], in0=gcols[:, 10:11], in1=gcols[:, 12:13], op=ALU.mult))
        OP("vector", "tensor_scalar", reads=["gcol9"], writes=["gcol11"],
           kwargs=dict(out=gcols[:, 11:12], in0=gcols[:, 9:10], scalar1=0.5, scalar2=None, op0=ALU.mult))
        if dbg == "l1oq":
            dump_fm(lambda j: (A[:, j, :], f"A{j}"))
            k.barrier(); les2.close(); les0.close()
            return
        k.barrier()
        les2.close()
        les3 = ExitStack()
        w1dg = les3.enter_context(nc.sbuf_tensor(f"w1dg{inv}", [128, 8, C], BF16))

        def ld_ocg(g):
            for j in range(4):
                OP("sync" if j % 2 == 0 else "scalar", "dma_start",
                   writes=[f"w1dg{(g % 2) * 4 + j}"], dma=True,
                   kwargs=dict(out=w1dg[:, (g % 2) * 4 + j, :], in_=s_w1d[g * 4 + j, :, :]))

        ld_ocg(0)
        for ocg in range(6):
            for j in range(4):
                of = ocg * 4 + j
                OP("scalar", "activation", reads=[f"A{of}", "gcol10"], writes=[f"i16b{of % 2}"],
                   kwargs=dict(out=i16b[:, of % 2, :], in_=A[:, of, :], func=AF.Copy, scale=gcols[:, 10:11], bias=0.0))
                OP("gpsimd", "tensor_copy", reads=[f"i16b{of % 2}"], writes=[f"A{of}"],
                   kwargs=dict(out=A[:, of, :], in_=i16b[:, of % 2, :]))
            if ocg + 1 < 6:
                ld_ocg(ocg + 1)
            for t in range(NT):
                for half in range(2):
                    pd, pd_n = (psE, "psE") if half == 0 else (psF, "psF")
                    for j in range(4):
                        oc = ocg * 4 + j
                        sl = (ocg % 2) * 4 + j
                        OP("tensor", "matmul", reads=[f"A{oc}", f"w1dg{sl}"],
                           writes=[pd_n] if j in (0, 3) else [],
                           kwargs=dict(out=pd[:], lhsT=A[:, oc, t * 128:(t + 1) * 128],
                                       rhs=w1dg[:, sl, half * 512:(half + 1) * 512],
                                       start=(j == 0), stop=(j == 3)))
                    OP("vector", "scalar_tensor_tensor", reads=[pd_n, "gcol11", f"res{t}"],
                       writes=[f"res{t}"],
                       kwargs=dict(out=res[:, t, half * 512:(half + 1) * 512], in0=pd[:],
                                   scalar=gcols[:, 11:12], in1=res[:, t, half * 512:(half + 1) * 512],
                                   op0=ALU.mult, op1=ALU.add))
                if emit_out and ocg == 5:
                    b_, half_ = t // 2, t % 2
                    OP("sync", "dma_start", reads=[f"res{t}"], writes=["y_d"], dma=True,
                       kwargs=dict(out=y_h[b_, half_ * 128:(half_ + 1) * 128, :], in_=res[:, t, :]))
        k.barrier()
        les3.close()
        les0.close()

    def l1attn():
        ses = ExitStack()
        ut = ses.enter_context(nc.sbuf_tensor("ut", [128, NT, C], BF16))
        selmat = ses.enter_context(nc.sbuf_tensor("selmat", [8, 1024], F32))
        isel = ses.enter_context(nc.sbuf_tensor("isel", [8, 1024], I32))
        OP("gpsimd", "iota", writes=["isel"], kwargs=dict(out=isel[:], pattern=[[1, 1024]], base=0, channel_multiplier=-128))
        OP("gpsimd", "tensor_scalar", reads=["isel"], writes=["self0"],
           kwargs=dict(out=selmat[:], in0=isel[:], scalar1=0, scalar2=None, op0=ALU.is_ge))
        OP("gpsimd", "tensor_scalar", reads=["isel"], writes=["self1"],
           kwargs=dict(out=isel[:], in0=isel[:], scalar1=128, scalar2=None, op0=ALU.is_lt))
        OP("gpsimd", "tensor_tensor", reads=["self0", "self1"], writes=["selmat"],
           kwargs=dict(out=selmat[:], in0=selmat[:], in1=isel[:], op=ALU.mult))
        Kb = ses.enter_context(nc.sbuf_tensor("Kb", [128, 8, C], F32))
        Db = ses.enter_context(nc.sbuf_tensor("Db", [128, C], F32))
        accV = ses.enter_context(nc.sbuf_tensor("accV", [128, C], F32))
        tmpV = ses.enter_context(nc.sbuf_tensor("tmpV", [128, C], F32))
        accG = ses.enter_context(nc.sbuf_tensor("accG", [128, C], F32))
        tmpG = ses.enter_context(nc.sbuf_tensor("tmpG", [128, C], F32))
        Krow = ses.enter_context(nc.sbuf_tensor("Krow", [8, 1024], F32))
        sN = 16 * NDIM // 16
        ss = ses.enter_context(nc.sbuf_tensor("ssm", [128, 16, NC_, NDIM], F32))
        SS = lambda i: ss[:, i, :, :]

        def ld_param(dst_i, src, ncols, name):
            OP("sync", "dma_start", writes=[name], dma=True,
               kwargs=dict(out=ss[:, dst_i, :, 0:ncols],
                           in_=src.rearrange("(c p) n -> p c n", p=128),
                           allow_slow_non_contiguous=True))

        ld_param(0, ldt_h, 1, "ssm0")
        ld_param(1, lar_h, NDIM, "ssm1")
        ld_param(2, aim_h, NDIM, "ssm2")
        OP("sync", "dma_start", writes=["ssm3"], dma=True,
           kwargs=dict(out=ss[:, 3, :, :], in_=bc_h[0].rearrange("(c p) n -> p c n", p=128)))
        OP("sync", "dma_start", writes=["ssm4"], dma=True,
           kwargs=dict(out=ss[:, 4, :, :], in_=bc_h[1].rearrange("(c p) n -> p c n", p=128)))
        OP("sync", "dma_start", writes=["ssm5"], dma=True,
           kwargs=dict(out=ss[:, 5, :, :], in_=cc_h[0].rearrange("(c p) n -> p c n", p=128)))
        OP("sync", "dma_start", writes=["ssm6"], dma=True,
           kwargs=dict(out=ss[:, 6, :, :], in_=cc_h[1].rearrange("(c p) n -> p c n", p=128)))
        OP("sync", "dma_start", writes=["Db"], dma=True,
           kwargs=dict(out=Db[:], in_=dd_h[0:1, :].broadcast_to((128, C))))
        def tt(o, a, b, op, ro=None, rn=None):
            OP("vector", "tensor_tensor", reads=[f"ssm{a}", f"ssm{b}"] + (ro or []), writes=[f"ssm{o}"] + (rn or []),
               kwargs=dict(out=SS(o), in0=SS(a), in1=SS(b), op=op))
        def tsc(o, a, s1, s2, op0, op1=None):
            OP("vector", "tensor_scalar", reads=[f"ssm{a}"], writes=[f"ssm{o}"],
               kwargs=dict(out=SS(o), in0=SS(a), scalar1=s1, scalar2=s2, op0=op0,
                           **({"op1": op1} if op1 else {})))
        OP("scalar", "activation", reads=["ssm0"], writes=["ssm7"],
           kwargs=dict(out=ss[:, 7, :, 0:1], in_=ss[:, 0, :, 0:1], func=AF.Exp))
        OP("scalar", "activation", reads=["ssm1"], writes=["ssm8"],
           kwargs=dict(out=SS(8), in_=SS(1), func=AF.Exp))
        OP("vector", "tensor_tensor", reads=["ssm8", "ssm7"], writes=["ssm8"],
           kwargs=dict(out=SS(8), in0=SS(8), in1=ss[:, 7, :, 0:1].broadcast_to((128, NC_, NDIM)), op=ALU.mult))
        OP("vector", "tensor_tensor", reads=["ssm2", "ssm7"], writes=["ssm9"],
           kwargs=dict(out=SS(9), in0=SS(2), in1=ss[:, 7, :, 0:1].broadcast_to((128, NC_, NDIM)), op=ALU.mult))
        tsc(10, 8, 0.5, 1.0, ALU.mult, ALU.add)
        tsc(11, 9, -0.5, None, ALU.mult)
        OP("scalar", "activation", reads=["ssm10"], writes=["ssm12"],
           kwargs=dict(out=SS(12), in_=SS(10), func=AF.Square))
        OP("scalar", "activation", reads=["ssm11"], writes=["ssm13"],
           kwargs=dict(out=SS(13), in_=SS(11), func=AF.Square))
        tt(12, 12, 13, ALU.add)
        OP("vector", "reciprocal", reads=["ssm12"], writes=["ssm14"],
           kwargs=dict(out=SS(14), in_=SS(12)))
        tsc(12, 8, -0.5, 1.0, ALU.mult, ALU.add)
        tsc(13, 9, 0.5, None, ALU.mult)
        tt(15, 12, 10, ALU.mult)
        tt(8, 13, 11, ALU.mult, ro=["ssm8"])
        tt(8, 15, 8, ALU.add)
        tt(8, 8, 14, ALU.mult)
        tt(15, 13, 10, ALU.mult)
        tt(9, 12, 11, ALU.mult, ro=["ssm9"])
        tt(9, 15, 9, ALU.subtract)
        tt(9, 9, 14, ALU.mult)
        tt(15, 5, 3, ALU.mult)
        tt(12, 6, 4, ALU.mult)
        tt(12, 15, 12, ALU.subtract)
        tt(15, 5, 4, ALU.mult)
        tt(13, 6, 3, ALU.mult)
        tt(13, 15, 13, ALU.add)
        OP("vector", "tensor_tensor", reads=["ssm12", "ssm7"], writes=["ssm12"],
           kwargs=dict(out=SS(12), in0=SS(12), in1=ss[:, 7, :, 0:1].broadcast_to((128, NC_, NDIM)), op=ALU.mult))
        OP("vector", "tensor_tensor", reads=["ssm13", "ssm7"], writes=["ssm13"],
           kwargs=dict(out=SS(13), in0=SS(13), in1=ss[:, 7, :, 0:1].broadcast_to((128, NC_, NDIM)), op=ALU.mult))
        tt(15, 12, 10, ALU.mult)
        tt(5, 13, 11, ALU.mult, ro=["ssm5"])
        tt(5, 15, 5, ALU.add)
        tt(5, 5, 14, ALU.mult)
        tt(15, 13, 10, ALU.mult)
        tt(6, 12, 11, ALU.mult, ro=["ssm6"])
        tt(6, 15, 6, ALU.subtract)
        tt(6, 6, 14, ALU.mult)
        tsc(5, 5, 2.0 * SCALE_SSM, None, ALU.mult)
        tsc(6, 6, 2.0 * SCALE_SSM, None, ALU.mult)
        OP("vector", "memset", writes=["ssm10"], kwargs=dict(ap=SS(10), constant=1.0))
        OP("vector", "memset", writes=["ssm11"], kwargs=dict(ap=SS(11), constant=0.0))
        Kc = ses.enter_context(nc.sbuf_tensor("Kc", [128, NC_, 8], F32))
        for l in range(8):
            tt(15, 5, 10, ALU.mult)
            tt(12, 6, 11, ALU.mult)
            tt(12, 15, 12, ALU.subtract)
            OP("vector", "tensor_reduce", reads=["ssm12"], writes=[f"Kc{l}"],
               kwargs=dict(out=Kc[:, :, l], in_=SS(12), axis=AX.X, op=ALU.add))
            if l < 7:
                tt(15, 10, 8, ALU.mult)
                tt(12, 11, 9, ALU.mult)
                tt(13, 10, 9, ALU.mult)
                tt(10, 11, 8, ALU.mult, ro=["ssm10"])
                tt(11, 13, 10, ALU.add)
                tt(10, 15, 12, ALU.subtract)
        for c in range(NC_):
            OP("tensor", "transpose", reads=[f"Kc{l}" for l in range(8)] + ["identf"], writes=["psF"],
               kwargs=dict(out=psF[0:8, 0:128], in_=Kc[:, c, :], identity=ident_f[:]))
            OP("vector", "tensor_copy", reads=["psF"], writes=["Krow"],
               kwargs=dict(out=Krow[:, c * 128:(c + 1) * 128], in_=psF[0:8, 0:128]))
        for tau in range(8):
            for half in range(2):
                pq, pq_n = (psA, "psA") if half == 0 else (psB, "psB")
                OP("tensor", "matmul", reads=["selmat", "Krow"], writes=[pq_n],
                   kwargs=dict(out=pq[:], lhsT=selmat[:, tau * 128:(tau + 1) * 128],
                               rhs=Krow[:, half * 512:(half + 1) * 512], start=True, stop=True))
                OP("vector", "tensor_copy", reads=[pq_n], writes=[f"Kb{tau}"],
                   kwargs=dict(out=Kb[:, tau, half * 512:(half + 1) * 512], in_=pq[:]))
        OP("vector", "tensor_tensor", reads=["Kb0", "Db"], writes=["Kb0"],
           kwargs=dict(out=Kb[:, 0, :], in0=Kb[:, 0, :], in1=Db[:], op=ALU.add))
        for t in range(NT):
            lnorm(t, False, out_bf_ap=ut[:, t, :], out_bf_name=f"ut{t}", tag="s")
        for t in range(NT):
            b, half = t // 2, t % 2
            eng = "vector" if t % 2 == 0 else "gpsimd"
            acc, accn = (accV, "accV") if eng == "vector" else (accG, "accG")
            tmp, tmpn = (tmpV, "tmpV") if eng == "vector" else (tmpG, "tmpG")
            OP(eng, "tensor_tensor", reads=[f"ut{t}", "Kb0"], writes=[accn],
               kwargs=dict(out=acc[:], in0=ut[:, t, :], in1=Kb[:, 0, :], op=ALU.mult))
            for tau in range(1, b + 1):
                ts_ = 2 * (b - tau) + half
                OP(eng, "tensor_tensor", reads=[f"ut{ts_}", f"Kb{tau}"], writes=[tmpn],
                   kwargs=dict(out=tmp[:], in0=ut[:, ts_, :], in1=Kb[:, tau, :], op=ALU.mult))
                OP(eng, "tensor_tensor", reads=[tmpn, accn], writes=[accn],
                   kwargs=dict(out=acc[:], in0=tmp[:], in1=acc[:], op=ALU.add))
            OP(eng, "tensor_tensor", reads=[accn, f"res{t}"], writes=[f"res{t}"],
               kwargs=dict(out=res[:, t, :], in0=acc[:], in1=res[:, t, :], op=ALU.add))
        k.barrier()
        ses.close()

    if n_sub >= 1:
        l0ffn(0)
    if n_sub >= 2:
        l0attn()
    if n_sub >= 3:
        l0ffn(1)
    if n_sub >= 4:
        l1ffn(0)
    if n_sub >= 5:
        l1attn()
    if n_sub >= 6:
        l1ffn(1, emit_out=True)

    if n_sub < 6:
        for t in range(NT):
            b, half = t // 2, t % 2
            OP("sync", "dma_start", reads=[f"res{t}"], writes=["y_d"], dma=True,
               kwargs=dict(out=y_h[b, half * 128:(half + 1) * 128, :], in_=res[:, t, :]))
    for i in range(N_DMA_SEMS):
        k._wait("sync", f"d_sync{i}", k.cnt[f"d_sync{i}"])

    block = es.enter_context(nc.Block())
    k.finalize(block)
    es.close()
    return nc


def host_inputs(inputs, core):
    x = np.ascontiguousarray(inputs["L0_attn_x"][:, core * TPC:(core + 1) * TPC, :], dtype=np.float32)
    T32 = lambda a: np.ascontiguousarray(np.asarray(a, dtype=np.float32).T)
    C32 = lambda a: np.ascontiguousarray(np.asarray(a, dtype=np.float32))
    return {
        "x": x,
        "w0g": T32(inputs["L0_ffn_w_gate"]),
        "w0u": T32(inputs["L0_ffn_w_up"]),
        "w0d": T32(inputs["L0_ffn_w_down"]),
        "wqkv": T32(inputs["L0_attn_Wqkv"]),
        "wproj": T32(inputs["L0_attn_Wproj"]),
        "bproj": C32(inputs["L0_attn_bproj"]).reshape(1, C),
        "w1g": T32(inputs["L1_ffn_w_gate"]),
        "w1u": T32(inputs["L1_ffn_w_up"]),
        "w1d": T32(inputs["L1_ffn_w_down"]),
        "ldt": C32(inputs["L1_attn_log_dt"]).reshape(C, 1),
        "lar": C32(inputs["L1_attn_log_A_real"]),
        "aim": C32(inputs["L1_attn_A_imag"]),
        "bc": np.ascontiguousarray(np.asarray(inputs["L1_attn_B"], dtype=np.float32).transpose(2, 0, 1)),
        "ccp": np.ascontiguousarray(np.asarray(inputs["L1_attn_C"], dtype=np.float32).transpose(2, 0, 1)),
        "dd": C32(inputs["L1_attn_D"]).reshape(1, C),
    }


def run(inputs, n_sub=6):
    nc = build_program(n_sub)
    in_maps = [host_inputs(inputs, i) for i in range(NCORES)]
    res_ = run_bass_kernel_spmd(nc, in_maps, core_ids=list(range(NCORES)))
    return np.concatenate([r["y"] for r in res_.results], axis=1)


def kernel(**inputs):
    nc = build_program(6)
    in_maps = [host_inputs(inputs, i) for i in range(NCORES)]
    res_ = run_bass_kernel_spmd(nc, in_maps, core_ids=list(range(NCORES)))
    return np.concatenate([r["y"] for r in res_.results], axis=1)


def kernel(**inputs):
    nc = build_program(6)
    in_maps = [host_inputs(inputs, i) for i in range(NCORES)]
    res_ = run_bass_kernel_spmd(nc, in_maps, core_ids=list(range(NCORES)))
    return np.concatenate([r["y"] for r in res_.results], axis=1)


def kernel(**inputs):
    nc = build_program(6)
    in_maps = [host_inputs(inputs, i) for i in range(NCORES)]
    res_ = run_bass_kernel_spmd(nc, in_maps, core_ids=list(range(NCORES)))
    return np.concatenate([r["y"] for r in res_.results], axis=1)


def kernel(**inputs):
    nc = build_program(6)
    in_maps = [host_inputs(inputs, i) for i in range(NCORES)]
    res_ = run_bass_kernel_spmd(nc, in_maps, core_ids=list(range(NCORES)))
    return np.concatenate([r["y"] for r in res_.results], axis=1)


def kernel(**inputs):
    nc = build_program(6)
    in_maps = [host_inputs(inputs, i) for i in range(NCORES)]
    res_ = run_bass_kernel_spmd(nc, in_maps, core_ids=list(range(NCORES)))
    return np.concatenate([r["y"] for r in res_.results], axis=1)
